# version 17
# baseline (speedup 1.0000x reference)
import numpy as np
import concourse.bass as bass
import concourse.mybir as mybir
from concourse.bass_utils import run_bass_kernel_spmd

F32 = mybir.dt.float32
BF16 = mybir.dt.bfloat16
AF = mybir.ActivationFunctionType
ALU = mybir.AluOpType
AX = mybir.AxisListType

D = 2048
NDC = 16
NH = 16
HD = 128
S = 2048
TL = 1024
NE = 8
FFN = 1024
NFC = 8
NHC = 16
SCALE = float(HD) ** -0.5
EPS = 1e-5
BIG = 1e30

ENGS = ("tensor", "vector", "scalar", "gpsimd", "sync")

_ROPE_FREQ_BITS = [
    0x3f800000, 0x3f5dafd7, 0x3f3ff911, 0x3f263de0, 0x3f0ff59a, 0x3ef953cf, 0x3ed7e89b, 0x3ebaf81b,
    0x3ea1e89b, 0x3e8c3504, 0x3e72d423, 0x3e5247ed, 0x3e361887, 0x3e1db040, 0x3e088d77, 0x3dec7fd6,
    0x3dcccccd, 0x3db15978, 0x3d99940d, 0x3d84fe4d, 0x3d6655c2, 0x3d47763f, 0x3d2cba15, 0x3d159348,
    0x3d0186e3, 0x3ce054d2, 0x3cc2434f, 0x3ca8398b, 0x3c91ad39, 0x3c7c4d33, 0x3c5a7bf2, 0x3c3d3311,
    0x3c23d70a, 0x3c0de12d, 0x3bf5b9b0, 0x3bd4ca15, 0x3bb8449c, 0x3b9f91cc, 0x3b8a2e77, 0x3b6f520e,
    0x3b4f3e38, 0x3b33770f, 0x3b1b690d, 0x3b06946f, 0x3ae91528, 0x3ac9d75c, 0x3aaec98e, 0x3a975c0e,
    0x3a83126f, 0x3a6301e2, 0x3a44948c, 0x3a2a3b44, 0x3a136a16, 0x39ff4fac, 0x39dd1725, 0x39bf74d7,
    0x39a5cb60, 0x398f9272, 0x3978a815, 0x395753e4, 0x393a7753, 0x39217916, 0x390bd472, 0x38f22ce2,
]


class Sched:

    def __init__(self):
        self.items = []
        self.stream = {e: [] for e in ENGS}
        self.lastw = {}
        self.readers = {}
        self.batch_sems = set()

    def op(self, eng, fn, r=(), w=(), dma=None):
        o = {"eng": eng, "fn": fn, "dma": dma, "idx": len(self.items),
             "marked": False, "cnt": None,
             "sem": ("dma", dma) if dma else ("eng", eng)}
        deps = {}

        def add_dep(d):
            if d is None or d is o:
                return
            if (eng == "tensor" and d["eng"] == "tensor"
                    and d["dma"] is None and dma is None):
                return
            if dma is not None and d["dma"] == dma and dma in self.batch_sems:
                return
            k = d["sem"]
            prev = deps.get(k)
            if prev is None or d["idx"] > prev["idx"]:
                deps[k] = d

        for res in r:
            add_dep(self.lastw.get(res))
        for res in w:
            add_dep(self.lastw.get(res))
            for dd in self.readers.get(res, {}).values():
                add_dep(dd)
        o["deps"] = list(deps.values())
        for res in r:
            self.readers.setdefault(res, {})[o["sem"]] = o
        for res in w:
            self.lastw[res] = o
            self.readers[res] = {}
        self.items.append(o)
        self.stream[eng].append(o)
        return o

    def materialize(self, nc):
        for o in self.items:
            if o["dma"] in ("st", "dbg", "dbg2", "dbg3", "dbg40", "dbg41", "dbg5", "dbg6"):
                o["marked"] = True
            for d in o["deps"]:
                d["marked"] = True
        totals = {}
        for o in self.items:
            if o["marked"]:
                amt = 16 if o["dma"] else 1
                totals[o["sem"]] = totals.get(o["sem"], 0) + amt
                o["cnt"] = totals[o["sem"]]
        self.totals = totals
        self.handles = {}
        for skey in totals:
            name = skey[1] if skey[0] == "dma" else "e_" + skey[1]
            self.handles[skey] = nc.alloc_semaphore(name)
        return self.handles

    def replay(self, engname, engobj):
        wm = {}
        for o in self.stream[engname]:
            waits = {}
            for d in o["deps"]:
                if d["cnt"] is None:
                    continue
                val = (self.totals[d["sem"]]
                       if (d["dma"] and d["dma"] in self.batch_sems)
                       else d["cnt"])
                if waits.get(d["sem"], 0) < val:
                    waits[d["sem"]] = val
            wl = [(k, v) for k, v in waits.items() if wm.get(k, 0) < v]
            for k, v in wl:
                engobj.wait_ge(self.handles[k], v)
                wm[k] = v
            inst = o["fn"](engobj)
            if o["marked"]:
                inst.then_inc(self.handles[o["sem"]], 16 if o["dma"] else 1)


def build_program(debug=False):
    nc = bass.Bass("TRN2", target_bir_lowering=False)

    def din(name, shape):
        return nc.dram_tensor(name, list(shape), F32, kind="ExternalInput").ap()

    d_x = din("x_t", (NDC, 128, S))
    d_wq = din("wq_t", (NH, 128, NDC, 128))
    d_wk = din("wk_t", (NH, 128, NDC, 128))
    d_wv = din("wv_t", (NH // 2, 128, NDC, 256))
    d_wo = din("wo_t", (NHC, 128, NDC, 128))
    d_w1 = din("w1_t", (NE, NFC, 128, NDC, 128))
    d_w3 = din("w3_t", (NE, NFC, 128, NDC, 128))
    d_w2 = din("w2_t", (NE, NHC, 128, NFC, 128))
    d_wg = din("wg_t", (128, NDC, NE))
    d_bq = din("bq", (128, NH))
    d_bk = din("bk", (128, NH))
    d_bv = din("bv", (128, NH))
    d_bo = din("bo", (128, NHC))
    d_fb = din("fbias", (1, NE))
    d_ck = din("rope_ck", (128, S))
    d_sk = din("rope_sk", (128, S))
    d_ps = din("pswap", (128, 128))
    d_id = din("ident", (128, 128))
    d_y = nc.dram_tensor("y", [NDC, 128, TL], F32, kind="ExternalOutput").ap()
    if debug:
        d_dbg_g = nc.dram_tensor("dbg_g", [NE, 128, NH * TL], BF16,
                                 kind="ExternalOutput").ap()
        d_dbg_acc = nc.dram_tensor("dbg_acc", [NE, 128, NHC * TL], F32,
                                   kind="ExternalOutput").ap()
        d_dbg_ewb = nc.dram_tensor("dbg_ewb", [NE, 128, TL], F32,
                                   kind="ExternalOutput").ap()
        d_dbg_ao = nc.dram_tensor("dbg_ao", [128, NH * TL], BF16,
                                  kind="ExternalOutput").ap()
        d_dbg_qkv = nc.dram_tensor("dbg_qkv", [3, 128, 2 * S], BF16,
                                   kind="ExternalOutput").ap()

    R1_bf = nc.alloc_sbuf_tensor("R1_bf", [128, NDC * S], BF16)
    r1_addr = nc.lookup_mloc(R1_bf).addr
    R1_f32 = nc.alloc_sbuf_tensor_at("R1_f32", [128, NDC * TL], F32,
                                     offset=r1_addr)
    aog = nc.alloc_sbuf_tensor("aog", [128, NH * TL], BF16)
    xy = nc.alloc_sbuf_tensor("xy", [128, NDC * TL], BF16)
    qbuf = nc.alloc_sbuf_tensor("qbuf", [128, 2 * TL], BF16)
    kbuf = nc.alloc_sbuf_tensor("kbuf", [128, 2 * S], BF16)
    vbuf = nc.alloc_sbuf_tensor("vbuf", [128, NDC * 256], BF16)
    wslab = nc.alloc_sbuf_tensor("wslab", [128, 12288], BF16)
    scr_bf = nc.alloc_sbuf_tensor("scr_bf", [128, 2048], BF16)
    scr_f = nc.alloc_sbuf_tensor("scr_f", [128, 2048], F32)
    ck_sb = nc.alloc_sbuf_tensor("ck_sb", [128, S], F32)
    sk_sb = nc.alloc_sbuf_tensor("sk_sb", [128, S], F32)
    id_sb = nc.alloc_sbuf_tensor("id_sb", [128, 128], F32)
    ps_sb = nc.alloc_sbuf_tensor("ps_sb", [128, 128], BF16)
    ones_c_bf = nc.alloc_sbuf_tensor("ones_c_bf", [128, 1], BF16)
    ones_r_bf = nc.alloc_sbuf_tensor("ones_r_bf", [1, 512], BF16)
    ones_c_f = nc.alloc_sbuf_tensor("ones_c_f", [128, 1], F32)
    ones_r_f = nc.alloc_sbuf_tensor("ones_r_f", [1, 128], F32)
    ones_sq = nc.alloc_sbuf_tensor("ones_sq", [128, 128], BF16)
    wg_sb = nc.alloc_sbuf_tensor("wg_sb", [128, NDC * NE], BF16)
    bqc = nc.alloc_sbuf_tensor("bqc", [128, NH], F32)
    bkc = nc.alloc_sbuf_tensor("bkc", [128, NH], F32)
    bvc = nc.alloc_sbuf_tensor("bvc", [128, NH], F32)
    boc = nc.alloc_sbuf_tensor("boc", [128, NHC], F32)
    msc = nc.alloc_sbuf_tensor("msc", [1, 8], F32)
    fbrow = nc.alloc_sbuf_tensor("fbrow", [1, NE], F32)
    fbb = nc.alloc_sbuf_tensor("fbb", [128, NE], F32)
    mq_sb = nc.alloc_sbuf_tensor("mq_sb", [128, 8], F32)
    rinv = nc.alloc_sbuf_tensor("rinv", [1, 512], BF16)
    rows_f = nc.alloc_sbuf_tensor("rows_f", [1, 512], F32)
    rt_sb = nc.alloc_sbuf_tensor("rt_sb", [128, 64], F32)
    wr_sb = nc.alloc_sbuf_tensor("wr_sb", [128, 64], F32)
    sc_sb = nc.alloc_sbuf_tensor("sc_sb", [128, 64], F32)
    s2_sb = nc.alloc_sbuf_tensor("s2_sb", [128, 64], F32)
    ew_sb = nc.alloc_sbuf_tensor("ew_sb", [128, 64], F32)
    m1_sb = nc.alloc_sbuf_tensor("m1_sb", [128, 8], F32)

    pb = [nc.alloc_psum_tensor(f"pb{i}", [128, 1024], F32) for i in range(4)]

    class _PA:
        def __init__(self, t, off):
            self.t, self.off = t, off

        def ap(self):
            return self.t.ap()[:, self.off:self.off + 512]

    pa = [_PA(pb[i // 2], (i % 2) * 512) for i in range(8)]

    S_ = Sched()
    S_.batch_sems.update({"cst", "gld", "st", "dbg", "dbg5"})

    def T(fn, r=(), w=()):
        S_.op("tensor", fn, r, w)

    def V(fn, r=(), w=()):
        S_.op("vector", fn, r, w)

    def A(fn, r=(), w=()):
        S_.op("scalar", fn, r, w)

    def G(fn, r=(), w=(), dma=None):
        S_.op("gpsimd", fn, r, w, dma=dma)

    def Y(fn, r=(), w=(), dma=None):
        S_.op("sync", fn, r, w, dma=dma)

    R1k = [f"R1_{i}" for i in range(NDC)]
    XYk = [f"xy{i}" for i in range(NDC)]

    G(lambda g: g.memset(ones_c_bf.ap(), 1.0), w=["ones"])
    G(lambda g: g.memset(ones_r_bf.ap(), 1.0), w=["ones"])
    G(lambda g: g.memset(ones_c_f.ap(), 1.0), w=["ones"])
    G(lambda g: g.memset(ones_r_f.ap(), 1.0), w=["ones"])
    G(lambda g: g.memset(ones_sq.ap(), 1.0), w=["ones"])
    Y(lambda s: s.dma_start(ck_sb.ap(), d_ck), w=["ck"], dma="cst")
    Y(lambda s: s.dma_start(sk_sb.ap(), d_sk), w=["sk"], dma="cst")
    Y(lambda s: s.dma_start(id_sb.ap(), d_id), w=["id"], dma="cst")
    Y(lambda s: s.dma_start(fbrow.ap(), d_fb), w=["fbrow"], dma="cst")
    Y(lambda s: s.dma_start(bqc.ap(), d_bq), w=["bqc"], dma="cst")
    Y(lambda s: s.dma_start(bkc.ap(), d_bk), w=["bkc"], dma="cst")
    Y(lambda s: s.dma_start(bvc.ap(), d_bv), w=["bvc"], dma="cst")
    Y(lambda s: s.dma_start(boc.ap(), d_bo), w=["boc"], dma="cst")
    G(lambda g: g.dma_start(
        wslab.ap()[:, 8192:12288].rearrange("p (dc e) -> p dc e", dc=NDC),
        d_wv[0]), w=["ws4", "ws5"], dma="swv")
    for dc in range(NDC):
        G(lambda g, dc=dc: g.dma_start(
            R1_bf.ap()[:, dc * S:(dc + 1) * S], d_x[dc]),
            r=([f"R1_{dc - 2}"] if dc >= 2 else []),
            w=[f"R1_{dc}"], dma=f"gx{dc % 2}")
    G(lambda g: g.dma_start(
        wslab.ap()[:, 0:4096].rearrange("p (h dc e) -> p h dc e", h=2, dc=NDC),
        d_wq[0:2].rearrange("h p dc e -> p h dc e")),
        w=["ws0", "ws1"], dma="swq")
    G(lambda g: g.dma_start(ps_sb.ap(), d_ps), w=["pswap"], dma="gld")
    G(lambda g: g.dma_start(
        wslab.ap()[:, 4096:8192].rearrange("p (h dc e) -> p h dc e", h=2, dc=NDC),
        d_wk[0:2].rearrange("h p dc e -> p h dc e")),
        w=["ws2", "ws3"], dma="swk")
    G(lambda g: g.dma_start(
        wg_sb.ap().rearrange("p (dc e) -> p dc e", dc=NDC), d_wg),
        w=["wg"], dma="gld")
    T(lambda t: t.matmul(pa[0].ap()[:, 0:NE], ones_r_f.ap(), fbrow.ap(),
                         start=True, stop=True),
      r=["ones", "fbrow"], w=["pa0"])
    V(lambda v: v.tensor_copy(fbb.ap(), pa[0].ap()[:, 0:NE]),
      r=["pa0"], w=["fbb"])

    def emit_group(gi):
        if gi > 0:
            G(lambda g, gi=gi: g.dma_start(
                wslab.ap()[:, 8192:12288].rearrange("p (dc e) -> p dc e", dc=NDC),
                d_wv[gi]), w=["ws4", "ws5"], dma="swv")
            G(lambda g, gi=gi: g.dma_start(
                wslab.ap()[:, 0:4096].rearrange(
                    "p (h dc e) -> p h dc e", h=2, dc=NDC),
                d_wq[2 * gi:2 * gi + 2].rearrange("h p dc e -> p h dc e")),
                w=["ws0", "ws1"], dma="swq")
            G(lambda g, gi=gi: g.dma_start(
                wslab.ap()[:, 4096:8192].rearrange(
                    "p (h dc e) -> p h dc e", h=2, dc=NDC),
                d_wk[2 * gi:2 * gi + 2].rearrange("h p dc e -> p h dc e")),
                w=["ws2", "ws3"], dma="swk")

        if gi == 0:
            for dc in range(NDC):
                def vmm0(t, dc=dc):
                    last = None
                    for tc in range(NDC):
                        last = t.matmul(
                            pa[tc // 2].ap()[:, (tc % 2) * 256:
                                             (tc % 2) * 256 + 256],
                            R1_bf.ap()[:, dc * S + tc * 128:
                                       dc * S + tc * 128 + 128],
                            wslab.ap()[:, 8192 + dc * 256: 8192 + dc * 256 + 256],
                            start=(dc == 0 and tc % 2 == 0),
                            stop=(dc == NDC - 1),
                            skip_group_check=True)
                    return last
                T(vmm0, r=[f"R1_{dc}", "ws4" if dc < 8 else "ws5"],
                  w=[f"pa{b}" for b in range(8)])
            for tc in range(NDC):
                A(lambda s, tc=tc: s.copy(
                    vbuf.ap()[:, tc * 256: tc * 256 + 256],
                    pa[tc // 2].ap()[:, (tc % 2) * 256:(tc % 2) * 256 + 256]),
                    r=[f"pa{tc // 2}"], w=[f"vb{tc}"])
        else:
            for tc in range(NDC):
                bank = 2 + (tc % 2)
                pk = f"pa{bank}"

                def vmm(t, tc=tc, bank=bank):
                    last = None
                    for dc in range(NDC):
                        last = t.matmul(
                            pa[bank].ap()[:, 0:256],
                            R1_bf.ap()[:, dc * S + tc * 128: dc * S + tc * 128 + 128],
                            wslab.ap()[:, 8192 + dc * 256: 8192 + dc * 256 + 256],
                            start=(dc == 0), stop=(dc == NDC - 1))
                    return last
                T(vmm, r=R1k + ["ws4", "ws5"], w=[pk])
                A(lambda s, tc=tc, bank=bank: s.copy(
                    vbuf.ap()[:, tc * 256: tc * 256 + 256], pa[bank].ap()[:, 0:256]),
                    r=[pk], w=[f"vb{tc}"])

        for hl in range(2):
            def proj_rope(nt, wbase, wkeys, bcol, bckey, cos, sin, obuf, ooff,
                          okeys, hl=hl, hh=2 * gi + hl, finale=None):
                def emit_tail(nq):
                    bank = nq % 2
                    pk = f"pa{bank}"
                    swb = 2 + (nq % 2)
                    spk = f"pa{swb}"
                    qs_slot = 2 + (nq % 2)
                    t1 = nq % 2
                    t2 = 2 + (nq % 2)
                    T(lambda t, swb=swb, qs_slot=qs_slot: t.matmul(
                        pa[swb].ap(), ps_sb.ap(),
                        scr_bf.ap()[:, qs_slot * 512:(qs_slot + 1) * 512],
                        start=True, stop=True),
                        r=[f"sb{qs_slot}", "pswap"], w=[spk])
                    V(lambda v, nq=nq, swb=swb, t1=t1, sin=sin: v.tensor_tensor(
                        scr_f.ap()[:, t1 * 512:(t1 + 1) * 512], pa[swb].ap(),
                        sin.ap()[:, nq * 512:(nq + 1) * 512], ALU.mult),
                        r=[spk, "sk"], w=[f"sf{t1}"])
                    V(lambda v, nq=nq, t2=t2, qs_slot=qs_slot, cos=cos:
                      v.tensor_tensor(
                        scr_f.ap()[:, t2 * 512:(t2 + 1) * 512],
                        scr_bf.ap()[:, qs_slot * 512:(qs_slot + 1) * 512],
                        cos.ap()[:, nq * 512:(nq + 1) * 512], ALU.mult),
                        r=[f"sb{qs_slot}", "ck"], w=[f"sf{t2}"])
                    V(lambda v, nq=nq, t1=t1, t2=t2: v.tensor_tensor(
                        obuf.ap()[:, ooff + nq * 512: ooff + nq * 512 + 512],
                        scr_f.ap()[:, t2 * 512:(t2 + 1) * 512],
                        scr_f.ap()[:, t1 * 512:(t1 + 1) * 512], ALU.add),
                        r=[f"sf{t1}", f"sf{t2}"], w=[okeys[nq]])

                def emit_evac(nq, pb):
                    bank = pb[nq % 2]
                    qs_slot = 2 + (nq % 2)
                    A(lambda s, bank=bank, qs_slot=qs_slot: s.activation(
                        scr_bf.ap()[:, qs_slot * 512:(qs_slot + 1) * 512],
                        pa[bank].ap(), AF.Identity, bias=bcol.ap()[:, hh:hh + 1]),
                        r=[f"pa{bank}", bckey], w=[f"sb{qs_slot}"])
                    sq = nq % 2
                    V(lambda v, sq=sq, qs_slot=qs_slot: v.tensor_tensor(
                        scr_bf.ap()[:, sq * 512:(sq + 1) * 512],
                        scr_bf.ap()[:, qs_slot * 512:(qs_slot + 1) * 512],
                        scr_bf.ap()[:, qs_slot * 512:(qs_slot + 1) * 512],
                        ALU.mult), r=[f"sb{qs_slot}"], w=[f"sb{sq}"])
                    T(lambda t, sq=sq: t.matmul(
                        pa[4].ap(), ones_sq.ap(),
                        scr_bf.ap()[:, sq * 512:(sq + 1) * 512],
                        start=True, stop=True), r=[f"sb{sq}", "ones"], w=["pa4"])
                    V(lambda v, nq=nq: v.tensor_reduce(
                        msc.ap()[0:1, nq:nq + 1], pa[4].ap()[0:1, :], AX.X,
                        ALU.max), r=["pa4"], w=["msc"])
                    if nt == 4:
                        if nq == 0:
                            V(lambda v: v.tensor_copy(
                                msc.ap()[0:1, 5:6], msc.ap()[0:1, 0:1]),
                                r=["msc"], w=["msc"])
                        else:
                            V(lambda v, nq=nq: v.tensor_tensor(
                                msc.ap()[0:1, 5:6], msc.ap()[0:1, 5:6],
                                msc.ap()[0:1, nq:nq + 1], ALU.max),
                                r=["msc"], w=["msc"])

                prbanks = [(0, 1)] if nt == 2 else [(6, 7), (0, 1)]
                for pr in range(nt // 2):
                    a = 2 * pr
                    pb = prbanks[pr]
                    for dc in range(NDC):
                        def pmm(t, a=a, dc=dc, pb=pb):
                            t.matmul(
                                pa[pb[0]].ap(),
                                wslab.ap()[:, wbase + dc * 128: wbase + dc * 128 + 128],
                                R1_bf.ap()[:, dc * S + a * 512: dc * S + a * 512 + 512],
                                start=(dc == 0), stop=(dc == NDC - 1),
                                skip_group_check=True)
                            return t.matmul(
                                pa[pb[1]].ap(),
                                wslab.ap()[:, wbase + dc * 128: wbase + dc * 128 + 128],
                                R1_bf.ap()[:, dc * S + a * 512 + 512:
                                            dc * S + a * 512 + 1024],
                                start=(dc == 0), stop=(dc == NDC - 1),
                                skip_group_check=True)
                        T(pmm, r=[f"R1_{dc}"] + wkeys, w=[f"pa{pb[0]}", f"pa{pb[1]}"])
                    if pr >= 1:
                        emit_tail(a - 2)
                        emit_tail(a - 1)
                    emit_evac(a, pb)
                    emit_evac(a + 1, pb)
                if finale is not None:
                    finale()
                emit_tail(nt - 2)
                emit_tail(nt - 1)

            qoff = hl * TL
            koff = hl * S
            hh = 2 * gi + hl
            proj_rope(2, hl * 2048, ["ws0" if hl == 0 else "ws1"], bqc, "bqc",
                      ck_sb, sk_sb, qbuf, qoff, [f"qb{hl}_{i}" for i in range(2)])
            V(lambda v: v.tensor_tensor(
                msc.ap()[0:1, 4:5], msc.ap()[0:1, 0:1], msc.ap()[0:1, 1:2],
                ALU.max), r=["msc"], w=["msc"])
            def bound_finale():
                V(lambda v: v.tensor_tensor(
                    msc.ap()[0:1, 6:7], msc.ap()[0:1, 4:5], msc.ap()[0:1, 5:6],
                    ALU.mult), r=["msc"], w=["msc"])
                A(lambda s: s.activation(msc.ap()[0:1, 7:8], msc.ap()[0:1, 6:7],
                                         AF.Sqrt, scale=SCALE * SCALE),
                  r=["msc"], w=["msc"])
                T(lambda t: t.matmul(
                    pa[3].ap()[:, 0:1], ones_r_f.ap(), msc.ap()[0:1, 7:8],
                    start=True, stop=True), r=["msc", "ones"], w=["pa3"])
                V(lambda v: v.tensor_scalar(
                    mq_sb.ap()[:, 6:7], pa[3].ap()[:, 0:1], -1.0, None,
                    ALU.mult), r=["pa3"], w=["mq"])

            proj_rope(4, 4096 + hl * 2048, ["ws2" if hl == 0 else "ws3"], bkc,
                      "bkc", ck_sb, sk_sb, kbuf, koff,
                      [f"kb{hl}_{i}" for i in range(4)], finale=bound_finale)

            def st_merged():
                def sums_pvs(kc):
                    j = kc % 2
                    for qb in range(2):
                        T(lambda t, kc=kc, j=j, qb=qb: t.matmul(
                            pa[6 + qb].ap(), ones_sq.ap(),
                            scr_bf.ap()[:, j * 1024 + qb * 512:
                                        j * 1024 + qb * 512 + 512],
                            start=(kc == 0), stop=(kc == NDC - 1),
                            skip_group_check=True),
                            r=[f"sb{2 * j + qb}", "ones"], w=[f"pa{6 + qb}"])
                        T(lambda t, kc=kc, j=j, qb=qb, hl=hl: t.matmul(
                            pa[4 + qb].ap(),
                            vbuf.ap()[:, kc * 256 + hl * 128:
                                      kc * 256 + hl * 128 + 128],
                            scr_bf.ap()[:, j * 1024 + qb * 512:
                                        j * 1024 + qb * 512 + 512],
                            start=(kc == 0), stop=(kc == NDC - 1),
                            skip_group_check=True),
                            r=[f"sb{2 * j + qb}", f"vb{kc}"], w=[f"pa{4 + qb}"])

                for kc in range(NDC):
                    j = kc % 2
                    for qb in range(2):
                        T(lambda t, kc=kc, j=j, qb=qb, qoff=qoff, koff=koff:
                          t.matmul(
                            pa[2 * j + qb].ap(),
                            kbuf.ap()[:, koff + kc * 128: koff + kc * 128 + 128],
                            qbuf.ap()[:, qoff + qb * 512: qoff + qb * 512 + 512],
                            start=True, stop=True),
                            r=[f"kb{hl}_{kc // 4}", f"qb{hl}_{qb}"],
                            w=[f"pa{2 * j + qb}"])
                    A(lambda s, j=j: s.activation(
                        scr_bf.ap()[:, j * 1024:(j + 1) * 1024],
                        pb[j].ap(), AF.Exp, bias=mq_sb.ap()[:, 6:7],
                        scale=SCALE),
                        r=[f"pa{2 * j}", f"pa{2 * j + 1}", "mq"],
                        w=[f"sb{2 * j}", f"sb{2 * j + 1}"])
                    if kc >= 1:
                        sums_pvs(kc - 1)
                sums_pvs(NDC - 1)

            def epilogue_m():
                for qb in range(2):
                    A(lambda s, qb=qb: s.activation(
                        scr_f.ap()[:, qb * 512:(qb + 1) * 512],
                        pa[6 + qb].ap(), AF.Ln),
                        r=[f"pa{6 + qb}"], w=[f"sf{qb}"])
                    A(lambda s, qb=qb: s.activation(
                        scr_f.ap()[:, qb * 512:(qb + 1) * 512],
                        scr_f.ap()[:, qb * 512:(qb + 1) * 512], AF.Exp,
                        scale=-1.0), r=[f"sf{qb}"], w=[f"sf{qb}"])
                    V(lambda v, qb=qb, hh=hh: v.tensor_tensor(
                        aog.ap()[:, hh * TL + qb * 512: hh * TL + qb * 512 + 512],
                        pa[4 + qb].ap(), scr_f.ap()[:, qb * 512:(qb + 1) * 512],
                        ALU.mult), r=[f"pa{4 + qb}", f"sf{qb}"], w=[f"ao{hh}"])
                    A(lambda s, hh=hh, qb=qb: s.activation(
                        aog.ap()[:, hh * TL + qb * 512: hh * TL + qb * 512 + 512],
                        aog.ap()[:, hh * TL + qb * 512: hh * TL + qb * 512 + 512],
                        AF.Identity, bias=bvc.ap()[:, hh:hh + 1]),
                        r=[f"ao{hh}", "bvc"], w=[f"ao{hh}"])

            st_merged()
            epilogue_m()

    for gi in range(NH // 2):
        emit_group(gi)
        if debug and gi == 0:
            Y(lambda s: s.dma_start(d_dbg_qkv[0][:, 0:2 * TL], qbuf.ap()),
              r=["qb0_0", "qb0_1", "qb1_0", "qb1_1"], dma="dbg5")
            Y(lambda s: s.dma_start(d_dbg_qkv[1], kbuf.ap()),
              r=[f"kb{h}_{i}" for h in range(2) for i in range(4)], dma="dbg5")
            Y(lambda s: s.dma_start(d_dbg_qkv[2], vbuf.ap()),
              r=[f"vb{t}" for t in range(NDC)], dma="dbg5")

    for i in range(NDC):
        V(lambda v, i=i: v.tensor_copy(
            xy.ap()[:, i * TL:(i + 1) * TL], R1_bf.ap()[:, i * S: i * S + TL]),
            r=[f"R1_{i}"], w=[f"xy{i}"])

    if debug:
        Y(lambda s: s.dma_start(d_dbg_ao, aog.ap()),
          r=[f"ao{h}" for h in range(NH)], dma="dbg6")
    pending_stats = [None]
    for ec in range(NHC):
        sl = ec % 2
        wok = f"ws{sl}"
        G(lambda g, ec=ec, sl=sl: g.dma_start(
            wslab.ap()[:, sl * 2048:(sl + 1) * 2048].rearrange(
                "p (dc e) -> p dc e", dc=NDC), d_wo[ec]),
            w=[wok], dma=f"swo{sl}")
        ob = 0 if ec % 2 == 0 else 2
        for dc in range(NDC):
            def omm(t, ec=ec, dc=dc, sl=sl, ob=ob):
                for nq in range(2):
                    last = t.matmul(
                        pa[ob + nq].ap(),
                        wslab.ap()[:, sl * 2048 + dc * 128:
                                   sl * 2048 + dc * 128 + 128],
                        aog.ap()[:, dc * TL + nq * 512: dc * TL + nq * 512 + 512],
                        start=(dc == 0), stop=(dc == NDC - 1),
                        skip_group_check=True)
                return last
            T(omm, r=[f"ao{dc}", wok], w=[f"pa{ob}", f"pa{ob + 1}"])
        if pending_stats[0] is not None:
            pending_stats[0]()
        for nq in range(2):
            V(lambda v, ec=ec, nq=nq, ob=ob: v.scalar_tensor_tensor(
                R1_f32.ap()[:, ec * TL + nq * 512: ec * TL + nq * 512 + 512],
                pa[ob + nq].ap(), boc.ap()[:, ec:ec + 1],
                xy.ap()[:, ec * TL + nq * 512: ec * TL + nq * 512 + 512],
                ALU.add, ALU.add),
                r=[f"pa{ob + nq}", f"xy{ec}", "boc"], w=[f"R1_{ec}"])

        def emit_stats(ec=ec):
            sqs = ec % 2
            A(lambda s, ec=ec, sqs=sqs: s.activation(
                vbuf.ap()[:, sqs * 1024:(sqs + 1) * 1024],
                R1_f32.ap()[:, ec * TL:(ec + 1) * TL], AF.Square),
                r=[f"R1_{ec}"], w=[f"vb{t}" for t in range(4 * sqs, 4 * sqs + 4)])
            A(lambda s, ec=ec, sqs=sqs: s.activation(
                scr_bf.ap()[:, sqs * 1024:(sqs + 1) * 1024],
                R1_f32.ap()[:, ec * TL:(ec + 1) * TL], AF.Identity),
                r=[f"R1_{ec}"], w=[f"sb{2 * sqs}", f"sb{2 * sqs + 1}"])
            for nq in range(2):
                T(lambda t, ec=ec, nq=nq, sqs=sqs: t.matmul(
                    pa[4 + nq].ap(), ones_sq.ap(),
                    scr_bf.ap()[:, sqs * 1024 + nq * 512:
                                sqs * 1024 + nq * 512 + 512],
                    start=(ec == 0), stop=(ec == NHC - 1),
                    skip_group_check=True),
                    r=[f"sb{2 * sqs + nq}", "ones"], w=[f"pa{4 + nq}"])
                T(lambda t, ec=ec, nq=nq, sqs=sqs: t.matmul(
                    pa[6 + nq].ap(), ones_sq.ap(),
                    vbuf.ap()[:, sqs * 1024 + nq * 512:
                              sqs * 1024 + nq * 512 + 512],
                    start=(ec == 0), stop=(ec == NHC - 1),
                    skip_group_check=True),
                    r=[f"vb{4 * sqs + nq * 2}", f"vb{4 * sqs + nq * 2 + 1}",
                       "ones"], w=[f"pa{6 + nq}"])
        pending_stats[0] = emit_stats
    pending_stats[0]()

    def emit_ln(src_f32, srckeys, dst, dstkeys, inplace, mpb, vpb, mkeys, vkeys):
        V(lambda v: v.tensor_scalar(
            kbuf.ap()[0:1, 0:1024], mpb.ap()[0:1, :], 1.0 / D, None,
            ALU.mult), r=mkeys, w=["kb0_0", "kb0_1"])
        V(lambda v: v.tensor_tensor(
            kbuf.ap()[0:1, 1024:2048], kbuf.ap()[0:1, 0:1024],
            kbuf.ap()[0:1, 0:1024], ALU.mult),
            r=["kb0_0", "kb0_1"], w=["kb0_2", "kb0_3"])
        V(lambda v: v.tensor_scalar(
            scr_f.ap()[0:1, 1024:2048], vpb.ap()[0:1, :], 1.0 / D, EPS,
            ALU.mult, ALU.add), r=vkeys, w=["sf2", "sf3"])
        V(lambda v: v.tensor_tensor(
            scr_f.ap()[0:1, 1024:2048], scr_f.ap()[0:1, 1024:2048],
            kbuf.ap()[0:1, 1024:2048], ALU.subtract),
            r=["kb0_2", "kb0_3", "sf2", "sf3"], w=["sf2", "sf3"])
        A(lambda s: s.activation(
            scr_f.ap()[0:1, 1024:2048], scr_f.ap()[0:1, 1024:2048], AF.Ln),
            r=["sf2", "sf3"], w=["sf2", "sf3"])
        A(lambda s: s.activation(
            scr_f.ap()[0:1, 1024:2048], scr_f.ap()[0:1, 1024:2048], AF.Exp,
            scale=-0.5), r=["sf2", "sf3"], w=["sf2", "sf3"])
        for nq in range(2):
            T(lambda t, nq=nq: t.matmul(
                pa[2].ap(), ones_r_bf.ap()[0:1, 0:128],
                kbuf.ap()[0:1, nq * 512: nq * 512 + 512],
                start=True, stop=True), r=[f"kb0_{nq}", "ones"], w=["pa2"])
            V(lambda v, nq=nq: v.tensor_copy(
                qbuf.ap()[:, nq * 512:(nq + 1) * 512], pa[2].ap()),
                r=["pa2"], w=[f"qb0_{nq}"])
            T(lambda t, nq=nq: t.matmul(
                pa[3].ap(), ones_r_f.ap()[0:1, 0:128],
                scr_f.ap()[0:1, 1024 + nq * 512: 1024 + nq * 512 + 512],
                start=True, stop=True), r=[f"sf{2 + nq}", "ones"], w=["pa3"])
            V(lambda v, nq=nq: v.tensor_copy(
                qbuf.ap()[:, 1024 + nq * 512: 1024 + nq * 512 + 512], pa[3].ap()),
                r=["pa3"], w=[f"qb1_{nq}"])
        for ec in range(NHC):
            vsl = ec % 2
            vk = [f"vb{t}" for t in range(4 * vsl, 4 * vsl + 4)]
            V(lambda v, ec=ec, vsl=vsl: v.tensor_tensor(
                vbuf.ap()[:, vsl * 1024:(vsl + 1) * 1024],
                src_f32.ap()[:, ec * TL:(ec + 1) * TL],
                qbuf.ap()[:, 0:1024], ALU.subtract),
                r=[srckeys[ec], "qb0_0", "qb0_1"], w=vk)
            V(lambda v, ec=ec, vsl=vsl: v.tensor_tensor(
                dst.ap()[:, ec * TL:(ec + 1) * TL],
                vbuf.ap()[:, vsl * 1024:(vsl + 1) * 1024],
                qbuf.ap()[:, 1024:2048], ALU.mult),
                r=vk + ["qb1_0", "qb1_1"], w=[dstkeys[ec]])
            if inplace:
                Y(lambda s, ec=ec: s.dma_start(
                    d_y[ec], R1_f32.ap()[:, ec * TL:(ec + 1) * TL]),
                    r=[f"R1_{ec}"], dma="st")

    emit_ln(R1_f32, R1k, xy, XYk, False, pb[2], pb[3],
            ["pa4", "pa5"], ["pa6", "pa7"])

    for dc in range(NDC):
        def lmm(t, dc=dc):
            for nq in range(2):
                last = t.matmul(
                    pa[nq].ap()[0:8, :], wg_sb.ap()[:, dc * NE:(dc + 1) * NE],
                    xy.ap()[:, dc * TL + nq * 512: dc * TL + nq * 512 + 512],
                    start=(dc == 0), stop=(dc == NDC - 1),
                    skip_group_check=True)
            return last
        T(lmm, r=[f"xy{dc}", "wg"], w=["pa0", "pa1"])
    for nq in range(2):
        V(lambda v, nq=nq: v.tensor_copy(
            scr_f.ap()[0:8, nq * 512:(nq + 1) * 512], pa[nq].ap()[0:8, :]),
            r=[f"pa{nq}"], w=["sf0", "sf1"])
    for tt in range(8):
        T(lambda t, tt=tt: t.transpose(
            pa[2].ap()[:, 0:8], scr_f.ap()[0:8, tt * 128:(tt + 1) * 128],
            id_sb.ap()[0:8, 0:8]), r=["sf0", "sf1", "id"], w=["pa2"])
        V(lambda v, tt=tt: v.tensor_copy(
            rt_sb.ap()[:, tt * 8:(tt + 1) * 8], pa[2].ap()[:, 0:8]),
            r=["pa2"], w=["rt"])
    A(lambda s: s.activation(wr_sb.ap(), rt_sb.ap(), AF.Sigmoid),
      r=["rt"], w=["wr"])
    for tt in range(8):
        V(lambda v, tt=tt: v.tensor_tensor(
            sc_sb.ap()[:, tt * 8:(tt + 1) * 8], wr_sb.ap()[:, tt * 8:(tt + 1) * 8],
            fbb.ap(), ALU.add), r=["wr", "fbb"], w=["sc"])
    for tt in range(8):
        sl8 = slice(tt * 8, (tt + 1) * 8)
        V(lambda v, tt=tt: v.tensor_reduce(
            m1_sb.ap()[:, tt:tt + 1], sc_sb.ap()[:, tt * 8:(tt + 1) * 8], AX.X,
            ALU.max), r=["sc"], w=["m1"])
        V(lambda v, tt=tt: v.tensor_scalar(
            s2_sb.ap()[:, tt * 8:(tt + 1) * 8], sc_sb.ap()[:, tt * 8:(tt + 1) * 8],
            m1_sb.ap()[:, tt:tt + 1], None, ALU.is_ge), r=["sc", "m1"], w=["s2"])
        V(lambda v, tt=tt: v.tensor_scalar(
            s2_sb.ap()[:, tt * 8:(tt + 1) * 8], s2_sb.ap()[:, tt * 8:(tt + 1) * 8],
            -BIG, None, ALU.mult), r=["s2"], w=["s2"])
        V(lambda v, tt=tt: v.tensor_tensor(
            s2_sb.ap()[:, tt * 8:(tt + 1) * 8], sc_sb.ap()[:, tt * 8:(tt + 1) * 8],
            s2_sb.ap()[:, tt * 8:(tt + 1) * 8], ALU.add), r=["sc", "s2"], w=["s2"])
        V(lambda v, tt=tt: v.tensor_reduce(
            m1_sb.ap()[:, tt:tt + 1], s2_sb.ap()[:, tt * 8:(tt + 1) * 8], AX.X,
            ALU.max), r=["s2"], w=["m1"])
        V(lambda v, tt=tt: v.tensor_scalar(
            s2_sb.ap()[:, tt * 8:(tt + 1) * 8], sc_sb.ap()[:, tt * 8:(tt + 1) * 8],
            m1_sb.ap()[:, tt:tt + 1], None, ALU.is_ge), r=["sc", "m1"], w=["s2"])
        V(lambda v, tt=tt: v.tensor_tensor(
            ew_sb.ap()[:, tt * 8:(tt + 1) * 8], wr_sb.ap()[:, tt * 8:(tt + 1) * 8],
            s2_sb.ap()[:, tt * 8:(tt + 1) * 8], ALU.mult), r=["wr", "s2"], w=["ew"])
        V(lambda v, tt=tt: v.tensor_reduce(
            m1_sb.ap()[:, tt:tt + 1], ew_sb.ap()[:, tt * 8:(tt + 1) * 8], AX.X,
            ALU.add), r=["ew"], w=["m1"])
    V(lambda v: v.reciprocal(m1_sb.ap()[:, 0:8], m1_sb.ap()[:, 0:8]),
      r=["m1"], w=["m1"])
    for tt in range(8):
        V(lambda v, tt=tt: v.tensor_scalar(
            ew_sb.ap()[:, tt * 8:(tt + 1) * 8], ew_sb.ap()[:, tt * 8:(tt + 1) * 8],
            m1_sb.ap()[:, tt:tt + 1], None, ALU.mult), r=["ew", "m1"], w=["ew"])

    for hc in range(NHC):
        V(lambda v, hc=hc: v.tensor_copy(
            R1_f32.ap()[:, hc * TL:(hc + 1) * TL],
            xy.ap()[:, hc * TL:(hc + 1) * TL]), r=[f"xy{hc}"], w=[f"R1_{hc}"])

    W1OFF = [0, 2048]
    W3OFF = [4096, 6144]
    W2OFF = [8192, 9216]
    W1KEY = ["ws0", "ws1"]
    W3KEY = ["ws2", "ws3"]
    W2KEY = ["ws4a", "ws4b"]

    w13_q = [(e, fc) for e in range(NE) for fc in range(NFC)]
    w2_q = [(e, hc) for e in range(NE) for hc in range(NHC)]
    w13_pos = [0]
    w2_pos = [0]

    def pump_w13(n=1):
        for _ in range(n):
            if w13_pos[0] >= len(w13_q):
                return
            e, fc = w13_q[w13_pos[0]]
            w13_pos[0] += 1
            s1 = fc % 2
            G(lambda g, e=e, fc=fc, s1=s1: g.dma_start(
                wslab.ap()[:, W1OFF[s1]:W1OFF[s1] + 2048].rearrange(
                    "p (dc f) -> p dc f", dc=NDC), d_w1[e, fc]),
                w=[W1KEY[s1]], dma=f"sw1_{s1}")
            G(lambda g, e=e, fc=fc, s1=s1: g.dma_start(
                wslab.ap()[:, W3OFF[s1]:W3OFF[s1] + 2048].rearrange(
                    "p (dc f) -> p dc f", dc=NDC), d_w3[e, fc]),
                w=[W3KEY[s1]], dma=f"sw3_{s1}")

    def pump_w2(n=1):
        for _ in range(n):
            if w2_pos[0] >= len(w2_q):
                return
            e, hc = w2_q[w2_pos[0]]
            w2_pos[0] += 1
            s2 = hc % 2
            G(lambda g, e=e, hc=hc, s2=s2: g.dma_start(
                wslab.ap()[:, W2OFF[s2]:W2OFF[s2] + 1024].rearrange(
                    "p (fc h) -> p fc h", fc=NFC), d_w2[e, hc]),
                w=[W2KEY[s2]], dma=f"sw2_{s2}")

    def emit_ew_rows(e):
        sl = e % 2
        for tt in range(8):
            bank = 2 + tt % 2
            T(lambda t, e=e, tt=tt, bank=bank: t.transpose(
                pa[bank].ap()[0:1, 0:128],
                ew_sb.ap()[:, tt * 8 + e: tt * 8 + e + 1], id_sb.ap()),
                r=["ew", "id"], w=[f"pa{bank}"])
            V(lambda v, tt=tt, bank=bank, sl=sl: v.tensor_copy(
                kbuf.ap()[0:1, sl * 1024 + tt * 128: sl * 1024 + tt * 128 + 128],
                pa[bank].ap()[0:1, 0:128]),
                r=[f"pa{bank}"], w=[f"kb0_{2 * sl + tt // 4}"])
        for nq in range(2):
            bank = 2 + nq
            T(lambda t, nq=nq, sl=sl, bank=bank: t.matmul(
                pa[bank].ap(), ones_r_bf.ap()[0:1, 0:128],
                kbuf.ap()[0:1, sl * 1024 + nq * 512: sl * 1024 + nq * 512 + 512],
                start=True, stop=True),
                r=[f"kb0_{2 * sl + nq}", "ones"], w=[f"pa{bank}"])
            V(lambda v, nq=nq, sl=sl, bank=bank: v.tensor_copy(
                scr_f.ap()[:, sl * 1024 + nq * 512: sl * 1024 + nq * 512 + 512],
                pa[bank].ap()), r=[f"pa{bank}"], w=[f"sf{2 * sl + nq}"])

    def emit_h13(e, hook=None):
        sl = e % 2
        for fc in range(NFC):
            s1 = fc % 2
            for dc in range(NDC):
                def h1mm(t, dc=dc, s1=s1):
                    for nc2 in range(2):
                        last = t.matmul(
                            pa[nc2].ap(),
                            wslab.ap()[:, W1OFF[s1] + dc * 128:
                                       W1OFF[s1] + dc * 128 + 128],
                            xy.ap()[:, dc * TL + nc2 * 512:
                                    dc * TL + nc2 * 512 + 512],
                            start=(dc == 0), stop=(dc == NDC - 1),
                            skip_group_check=True)
                    return last
                T(h1mm, r=[f"xy{dc}", W1KEY[s1]], w=["pa0", "pa1"])
            for nc2 in range(2):
                A(lambda s, nc2=nc2: s.activation(
                    scr_bf.ap()[:, nc2 * 512:(nc2 + 1) * 512], pa[nc2].ap(),
                    AF.Silu), r=[f"pa{nc2}"], w=[f"sb{nc2}"])
            for dc in range(NDC):
                def h3mm(t, dc=dc, s1=s1):
                    for nc2 in range(2):
                        last = t.matmul(
                            pa[4 + nc2].ap(),
                            wslab.ap()[:, W3OFF[s1] + dc * 128:
                                       W3OFF[s1] + dc * 128 + 128],
                            xy.ap()[:, dc * TL + nc2 * 512:
                                    dc * TL + nc2 * 512 + 512],
                            start=(dc == 0), stop=(dc == NDC - 1),
                            skip_group_check=True)
                    return last
                T(h3mm, r=[f"xy{dc}", W3KEY[s1]], w=["pa4", "pa5"])
                if hook is not None and fc == 0 and dc == NDC - 1:
                    hook()
                    hook = None
            for nc2 in range(2):
                bsl = 2 + nc2
                V(lambda v, bsl=bsl, sl=sl, nc2=nc2: v.tensor_tensor(
                    scr_bf.ap()[:, bsl * 512:(bsl + 1) * 512], pa[4 + nc2].ap(),
                    scr_f.ap()[:, sl * 1024 + nc2 * 512: sl * 1024 + nc2 * 512 + 512],
                    ALU.mult), r=[f"pa{4 + nc2}", f"sf{2 * sl + nc2}"], w=[f"sb{bsl}"])
                V(lambda v, fc=fc, nc2=nc2, bsl=bsl: v.tensor_tensor(
                    aog.ap()[:, fc * TL + nc2 * 512: fc * TL + nc2 * 512 + 512],
                    scr_bf.ap()[:, nc2 * 512:(nc2 + 1) * 512],
                    scr_bf.ap()[:, bsl * 512:(bsl + 1) * 512], ALU.mult),
                    r=[f"sb{nc2}", f"sb{bsl}"], w=[f"ao{fc}"])
            pump_w13(1)

    def emit_ln2_stats(ec):
        sqs = ec % 2
        A(lambda s, ec=ec, sqs=sqs: s.activation(
            vbuf.ap()[:, sqs * 1024:(sqs + 1) * 1024],
            R1_f32.ap()[:, ec * TL:(ec + 1) * TL], AF.Square),
            r=[f"R1_{ec}"], w=[f"vb{t}" for t in range(4 * sqs, 4 * sqs + 4)])
        A(lambda s, ec=ec, sqs=sqs: s.activation(
            scr_bf.ap()[:, sqs * 1024:(sqs + 1) * 1024],
            R1_f32.ap()[:, ec * TL:(ec + 1) * TL], AF.Identity),
            r=[f"R1_{ec}"], w=[f"sb{2 * sqs}", f"sb{2 * sqs + 1}"])
        for nq in range(2):
            T(lambda t, ec=ec, nq=nq, sqs=sqs: t.matmul(
                pa[0 + nq].ap(), ones_sq.ap(),
                scr_bf.ap()[:, sqs * 1024 + nq * 512: sqs * 1024 + nq * 512 + 512],
                start=(ec == 0), stop=(ec == NHC - 1), skip_group_check=True),
                r=[f"sb{2 * sqs + nq}", "ones"], w=[f"pa{0 + nq}"])
            T(lambda t, ec=ec, nq=nq, sqs=sqs: t.matmul(
                pa[4 + nq].ap(), ones_sq.ap(),
                vbuf.ap()[:, sqs * 1024 + nq * 512: sqs * 1024 + nq * 512 + 512],
                start=(ec == 0), stop=(ec == NHC - 1), skip_group_check=True),
                r=[f"vb{4 * sqs + nq * 2}", f"vb{4 * sqs + nq * 2 + 1}", "ones"],
                w=[f"pa{4 + nq}"])

    def emit_w2(e, stats=False):
        pend = [None]
        for hc in range(NHC):
            s2 = hc % 2
            wb = 6 if hc % 2 == 0 else 2
            for fc in range(NFC):
                def w2mm(t, hc=hc, fc=fc, s2=s2, wb=wb):
                    for nc2 in range(2):
                        last = t.matmul(
                            pa[wb + nc2].ap(),
                            wslab.ap()[:, W2OFF[s2] + fc * 128:
                                       W2OFF[s2] + fc * 128 + 128],
                            aog.ap()[:, fc * TL + nc2 * 512: fc * TL + nc2 * 512 + 512],
                            start=(fc == 0), stop=(fc == NFC - 1),
                            skip_group_check=True)
                    return last
                T(w2mm, r=[f"ao{fc}", W2KEY[s2]], w=[f"pa{wb}", f"pa{wb + 1}"])
            if pend[0] is not None:
                pend[0]()
                pend[0] = None
            for nc2 in range(2):
                V(lambda v, hc=hc, nc2=nc2, wb=wb: v.tensor_tensor(
                    R1_f32.ap()[:, hc * TL + nc2 * 512: hc * TL + nc2 * 512 + 512],
                    pa[wb + nc2].ap(),
                    R1_f32.ap()[:, hc * TL + nc2 * 512: hc * TL + nc2 * 512 + 512],
                    ALU.add), r=[f"pa{wb + nc2}", f"R1_{hc}"], w=[f"R1_{hc}"])
            if stats:
                pend[0] = lambda hc=hc: emit_ln2_stats(hc)
            pump_w2(1)
        if pend[0] is not None:
            pend[0]()

    pump_w13(2)
    pump_w2(2)
    for e in range(NE):
        emit_h13(e, hook=lambda e=e: emit_ew_rows(e))
        emit_w2(e, stats=(e == NE - 1))
        if debug:
            Y(lambda s, e=e: s.dma_start(d_dbg_acc[e], R1_f32.ap()),
              r=[f"R1_{i}" for i in range(NHC)], dma="dbg2")
            Y(lambda s, e=e: s.dma_start(d_dbg_g[e], aog.ap()),
              r=[f"ao{j}" for j in range(NH)], dma="dbg3")
            Y(lambda s, e=e, sl=e % 2: s.dma_start(
                d_dbg_ewb[e], scr_f.ap()[:, sl * TL:(sl + 1) * TL]),
                r=[f"sf{2 * (e % 2)}", f"sf{2 * (e % 2) + 1}"], dma=f"dbg4{e % 2}")

    emit_ln(R1_f32, R1k, R1_f32, R1k, True, pb[0], pb[2],
            ["pa0", "pa1"], ["pa4", "pa5"])

    build_program.last_sched = S_
    handles = S_.materialize(nc)
    st_total = S_.totals.get(("dma", "st"), 0)
    import contextlib
    with nc.allow_low_precision(
            reason="bf16 intermediates by design; fp32 accumulation in PSUM, "
                   "graded against the fp64 sim"), nc.Block() as block:
        @block.tensor
        def _(t):
            S_.replay("tensor", t)

        @block.vector
        def _(v):
            S_.replay("vector", v)

        @block.scalar
        def _(s):
            S_.replay("scalar", s)

        @block.gpsimd
        def _(g):
            S_.replay("gpsimd", g)

        @block.sync
        def _(s):
            S_.replay("sync", s)
            s.wait_ge(handles[("dma", "st")], st_total)
    return nc


_NC_CACHE = {}


def _get_nc():
    if "nc" not in _NC_CACHE:
        _NC_CACHE["nc"] = build_program()
    return _NC_CACHE["nc"]


def _host_inputs(inputs):
    f = np.float32
    x = np.asarray(inputs["L0_attn_x"], f)
    Wq = np.asarray(inputs["L0_attn_Wq"], f)
    Wk = np.asarray(inputs["L0_attn_Wk"], f)
    Wv = np.asarray(inputs["L0_attn_Wv"], f)
    Wo = np.asarray(inputs["L0_attn_Wo"], f)
    Wg = np.asarray(inputs["L0_ffn_Wg"], f)
    W1 = np.asarray(inputs["L0_ffn_W1"], f)
    W2 = np.asarray(inputs["L0_ffn_W2"], f)
    W3 = np.asarray(inputs["L0_ffn_W3"], f)

    def att_t(W):
        return np.ascontiguousarray(
            W.reshape(NH, 128, NDC, 128).transpose(0, 3, 2, 1))

    shared = {
        "wq_t": att_t(Wq), "wk_t": att_t(Wk), "wo_t": att_t(Wo),
        "wv_t": np.ascontiguousarray(
            Wv.reshape(NH // 2, 256, NDC, 128).transpose(0, 3, 2, 1)),
        "w1_t": np.ascontiguousarray(
            W1.reshape(NE, NFC, 128, NDC, 128).transpose(0, 1, 4, 3, 2)),
        "w3_t": np.ascontiguousarray(
            W3.reshape(NE, NFC, 128, NDC, 128).transpose(0, 1, 4, 3, 2)),
        "w2_t": np.ascontiguousarray(
            W2.reshape(NE, NHC, 128, NFC, 128).transpose(0, 1, 4, 3, 2)),
        "wg_t": np.ascontiguousarray(Wg.T.reshape(NDC, 128, NE).transpose(1, 0, 2)),
        "bq": np.ascontiguousarray(np.asarray(inputs["L0_attn_bq"], f).reshape(NH, 128).T),
        "bk": np.ascontiguousarray(np.asarray(inputs["L0_attn_bk"], f).reshape(NH, 128).T),
        "bv": np.ascontiguousarray(np.asarray(inputs["L0_attn_bv"], f).reshape(NH, 128).T),
        "bo": np.ascontiguousarray(np.asarray(inputs["L0_attn_bo"], f).reshape(NHC, 128).T),
        "fbias": np.asarray(inputs["L0_ffn_bias"], f).reshape(1, NE),
    }
    pswap = np.zeros((128, 128), f)
    pswap[np.arange(128), np.arange(128) ^ 1] = 1.0
    shared["pswap"] = pswap
    shared["ident"] = np.eye(128, dtype=f)

    f32 = np.float32
    freqs = np.array(_ROPE_FREQ_BITS, dtype=np.uint32).view(f32)
    pos = np.arange(S, dtype=f32)
    ang = pos[:, None] * freqs[None, :]
    cosT = np.cos(ang)
    sinT = np.sin(ang)
    cos = np.repeat(cosT.T, 2, axis=0)
    sin = sinT.T
    ssgn = np.empty((128, S), f32)
    ssgn[0::2] = -sin
    ssgn[1::2] = sin

    in_maps = []
    for c in range(8):
        b = c // 2
        half = c % 2
        own = np.arange(half * TL, (half + 1) * TL)
        other = np.arange((1 - half) * TL, (2 - half) * TL)
        perm = np.concatenate([own, other])
        m = dict(shared)
        m["x_t"] = np.ascontiguousarray(x[b].T[:, perm]).reshape(NDC, 128, S)
        m["rope_ck"] = np.ascontiguousarray(cos[:, perm]).astype(f)
        m["rope_sk"] = np.ascontiguousarray(ssgn[:, perm]).astype(f)
        in_maps.append(m)
    return in_maps


def kernel(**inputs):
    in_maps = _host_inputs(inputs)
    nc = _get_nc()
    res = run_bass_kernel_spmd(nc, in_maps, core_ids=list(range(8)))
    y = np.empty((4, S, D), np.float32)
    for c in range(8):
        o = np.asarray(res.results[c]["y"])
        y[c // 2, (c % 2) * TL:(c % 2 + 1) * TL, :] = \
            o.transpose(2, 0, 1).reshape(TL, D)
    return y


# version 20
# speedup vs baseline: 1.0280x; 1.0280x over previous
import numpy as np
import concourse.bass as bass
import concourse.mybir as mybir
from concourse.bass_utils import run_bass_kernel_spmd

F32 = mybir.dt.float32
BF16 = mybir.dt.bfloat16
AF = mybir.ActivationFunctionType
ALU = mybir.AluOpType
AX = mybir.AxisListType

D = 2048
NDC = 16
NH = 16
HD = 128
S = 2048
TL = 1024
NE = 8
FFN = 1024
NFC = 8
NHC = 16
SCALE = float(HD) ** -0.5
EPS = 1e-5
BIG = 1e30

ENGS = ("tensor", "vector", "scalar", "gpsimd", "sync")

_ROPE_FREQ_BITS = [
    0x3f800000, 0x3f5dafd7, 0x3f3ff911, 0x3f263de0, 0x3f0ff59a, 0x3ef953cf, 0x3ed7e89b, 0x3ebaf81b,
    0x3ea1e89b, 0x3e8c3504, 0x3e72d423, 0x3e5247ed, 0x3e361887, 0x3e1db040, 0x3e088d77, 0x3dec7fd6,
    0x3dcccccd, 0x3db15978, 0x3d99940d, 0x3d84fe4d, 0x3d6655c2, 0x3d47763f, 0x3d2cba15, 0x3d159348,
    0x3d0186e3, 0x3ce054d2, 0x3cc2434f, 0x3ca8398b, 0x3c91ad39, 0x3c7c4d33, 0x3c5a7bf2, 0x3c3d3311,
    0x3c23d70a, 0x3c0de12d, 0x3bf5b9b0, 0x3bd4ca15, 0x3bb8449c, 0x3b9f91cc, 0x3b8a2e77, 0x3b6f520e,
    0x3b4f3e38, 0x3b33770f, 0x3b1b690d, 0x3b06946f, 0x3ae91528, 0x3ac9d75c, 0x3aaec98e, 0x3a975c0e,
    0x3a83126f, 0x3a6301e2, 0x3a44948c, 0x3a2a3b44, 0x3a136a16, 0x39ff4fac, 0x39dd1725, 0x39bf74d7,
    0x39a5cb60, 0x398f9272, 0x3978a815, 0x395753e4, 0x393a7753, 0x39217916, 0x390bd472, 0x38f22ce2,
]


class Sched:

    def __init__(self):
        self.items = []
        self.stream = {e: [] for e in ENGS}
        self.lastw = {}
        self.readers = {}
        self.batch_sems = set()

    def op(self, eng, fn, r=(), w=(), dma=None):
        o = {"eng": eng, "fn": fn, "dma": dma, "idx": len(self.items),
             "marked": False, "cnt": None,
             "sem": ("dma", dma) if dma else ("eng", eng)}
        deps = {}

        def add_dep(d):
            if d is None or d is o:
                return
            if (eng == "tensor" and d["eng"] == "tensor"
                    and d["dma"] is None and dma is None):
                return
            if dma is not None and d["dma"] == dma and dma in self.batch_sems:
                return
            k = d["sem"]
            prev = deps.get(k)
            if prev is None or d["idx"] > prev["idx"]:
                deps[k] = d

        for res in r:
            add_dep(self.lastw.get(res))
        for res in w:
            add_dep(self.lastw.get(res))
            for dd in self.readers.get(res, {}).values():
                add_dep(dd)
        o["deps"] = list(deps.values())
        for res in r:
            self.readers.setdefault(res, {})[o["sem"]] = o
        for res in w:
            self.lastw[res] = o
            self.readers[res] = {}
        self.items.append(o)
        self.stream[eng].append(o)
        return o

    def materialize(self, nc):
        for o in self.items:
            if o["dma"] in ("st", "dbg", "dbg2", "dbg3", "dbg40", "dbg41", "dbg5", "dbg6"):
                o["marked"] = True
            for d in o["deps"]:
                d["marked"] = True
        totals = {}
        for o in self.items:
            if o["marked"]:
                amt = 16 if o["dma"] else 1
                totals[o["sem"]] = totals.get(o["sem"], 0) + amt
                o["cnt"] = totals[o["sem"]]
        self.totals = totals
        self.handles = {}
        for skey in totals:
            name = skey[1] if skey[0] == "dma" else "e_" + skey[1]
            self.handles[skey] = nc.alloc_semaphore(name)
        return self.handles

    def replay(self, engname, engobj):
        wm = {}
        for o in self.stream[engname]:
            waits = {}
            for d in o["deps"]:
                if d["cnt"] is None:
                    continue
                val = (self.totals[d["sem"]]
                       if (d["dma"] and d["dma"] in self.batch_sems)
                       else d["cnt"])
                if waits.get(d["sem"], 0) < val:
                    waits[d["sem"]] = val
            wl = [(k, v) for k, v in waits.items() if wm.get(k, 0) < v]
            for k, v in wl:
                engobj.wait_ge(self.handles[k], v)
                wm[k] = v
            inst = o["fn"](engobj)
            if o["marked"]:
                inst.then_inc(self.handles[o["sem"]], 16 if o["dma"] else 1)


def build_program(debug=False):
    nc = bass.Bass("TRN2", target_bir_lowering=False)

    def din(name, shape):
        return nc.dram_tensor(name, list(shape), F32, kind="ExternalInput").ap()

    d_x = din("x_t", (NDC, 128, S))
    d_wq = din("wq_t", (NH, 128, NDC, 128))
    d_wk = din("wk_t", (NH, 128, NDC, 128))
    d_wv = din("wv_t", (NH // 2, 128, NDC, 256))
    d_wo = din("wo_t", (NHC, 128, NDC, 128))
    d_w1 = din("w1_t", (NE, NFC, 128, NDC, 128))
    d_w3 = din("w3_t", (NE, NFC, 128, NDC, 128))
    d_w2 = din("w2_t", (NE, NHC, 128, NFC, 128))
    d_wg = din("wg_t", (128, NDC, NE))
    d_bq = din("bq", (128, NH))
    d_bk = din("bk", (128, NH))
    d_bv = din("bv", (128, NH))
    d_bo = din("bo", (128, NHC))
    d_fb = din("fbias", (1, NE))
    d_ck = din("rope_ck", (128, S))
    d_sk = din("rope_sk", (128, S))
    d_ps = din("pswap", (128, 128))
    d_id = din("ident", (128, 128))
    d_y = nc.dram_tensor("y", [NDC, 128, TL], F32, kind="ExternalOutput").ap()
    if debug:
        d_dbg_g = nc.dram_tensor("dbg_g", [NE, 128, NH * TL], BF16,
                                 kind="ExternalOutput").ap()
        d_dbg_acc = nc.dram_tensor("dbg_acc", [NE, 128, NHC * TL], F32,
                                   kind="ExternalOutput").ap()
        d_dbg_ewb = nc.dram_tensor("dbg_ewb", [NE, 128, TL], F32,
                                   kind="ExternalOutput").ap()
        d_dbg_ao = nc.dram_tensor("dbg_ao", [128, NH * TL], BF16,
                                  kind="ExternalOutput").ap()
        d_dbg_qkv = nc.dram_tensor("dbg_qkv", [3, 128, 2 * S], BF16,
                                   kind="ExternalOutput").ap()

    R1_bf = nc.alloc_sbuf_tensor("R1_bf", [128, NDC * S], BF16)
    r1_addr = nc.lookup_mloc(R1_bf).addr
    R1_f32 = nc.alloc_sbuf_tensor_at("R1_f32", [128, NDC * TL], F32,
                                     offset=r1_addr)
    aog = nc.alloc_sbuf_tensor("aog", [128, NH * TL], BF16)
    xy = nc.alloc_sbuf_tensor("xy", [128, NDC * TL], BF16)
    qbuf = nc.alloc_sbuf_tensor("qbuf", [128, 2 * TL], BF16)
    kbuf = nc.alloc_sbuf_tensor("kbuf", [128, 2 * S], BF16)
    vbuf = nc.alloc_sbuf_tensor("vbuf", [128, NDC * 256], BF16)
    wslab = nc.alloc_sbuf_tensor("wslab", [128, 12288], BF16)
    scr_bf = nc.alloc_sbuf_tensor("scr_bf", [128, 2048], BF16)
    scr_f = nc.alloc_sbuf_tensor("scr_f", [128, 2048], F32)
    ck_sb = nc.alloc_sbuf_tensor("ck_sb", [128, S], F32)
    sk_sb = nc.alloc_sbuf_tensor("sk_sb", [128, S], F32)
    id_sb = nc.alloc_sbuf_tensor("id_sb", [128, 128], F32)
    ps_sb = nc.alloc_sbuf_tensor("ps_sb", [128, 128], BF16)
    ones_c_bf = nc.alloc_sbuf_tensor("ones_c_bf", [128, 1], BF16)
    ones_r_bf = nc.alloc_sbuf_tensor("ones_r_bf", [1, 512], BF16)
    ones_c_f = nc.alloc_sbuf_tensor("ones_c_f", [128, 1], F32)
    ones_r_f = nc.alloc_sbuf_tensor("ones_r_f", [1, 128], F32)
    ones_sq = nc.alloc_sbuf_tensor("ones_sq", [128, 128], BF16)
    wg_sb = nc.alloc_sbuf_tensor("wg_sb", [128, NDC * NE], BF16)
    bqc = nc.alloc_sbuf_tensor("bqc", [128, NH], F32)
    bkc = nc.alloc_sbuf_tensor("bkc", [128, NH], F32)
    bvc = nc.alloc_sbuf_tensor("bvc", [128, NH], F32)
    boc = nc.alloc_sbuf_tensor("boc", [128, NHC], F32)
    msc = nc.alloc_sbuf_tensor("msc", [1, 8], F32)
    fbrow = nc.alloc_sbuf_tensor("fbrow", [1, NE], F32)
    fbb = nc.alloc_sbuf_tensor("fbb", [128, NE], F32)
    mq_sb = nc.alloc_sbuf_tensor("mq_sb", [128, 8], F32)
    rinv = nc.alloc_sbuf_tensor("rinv", [1, 512], BF16)
    rows_f = nc.alloc_sbuf_tensor("rows_f", [1, 512], F32)
    rt_sb = nc.alloc_sbuf_tensor("rt_sb", [128, 64], F32)
    wr_sb = nc.alloc_sbuf_tensor("wr_sb", [128, 64], F32)
    sc_sb = nc.alloc_sbuf_tensor("sc_sb", [128, 64], F32)
    s2_sb = nc.alloc_sbuf_tensor("s2_sb", [128, 64], F32)
    ew_sb = nc.alloc_sbuf_tensor("ew_sb", [128, 64], F32)
    m1_sb = nc.alloc_sbuf_tensor("m1_sb", [128, 8], F32)

    pb = [nc.alloc_psum_tensor(f"pb{i}", [128, 1024], F32) for i in range(4)]

    class _PA:
        def __init__(self, t, off):
            self.t, self.off = t, off

        def ap(self):
            return self.t.ap()[:, self.off:self.off + 512]

    pa = [_PA(pb[i // 2], (i % 2) * 512) for i in range(8)]

    S_ = Sched()
    S_.batch_sems.update({"cst", "gld", "st", "dbg", "dbg5"})

    def T(fn, r=(), w=()):
        S_.op("tensor", fn, r, w)

    def V(fn, r=(), w=()):
        S_.op("vector", fn, r, w)

    def A(fn, r=(), w=()):
        S_.op("scalar", fn, r, w)

    def G(fn, r=(), w=(), dma=None):
        S_.op("gpsimd", fn, r, w, dma=dma)

    def Y(fn, r=(), w=(), dma=None):
        S_.op("sync", fn, r, w, dma=dma)

    R1k = [f"R1_{i}" for i in range(NDC)]
    XYk = [f"xy{i}" for i in range(NDC)]

    G(lambda g: g.memset(ones_c_bf.ap(), 1.0), w=["ones"])
    G(lambda g: g.memset(ones_r_bf.ap(), 1.0), w=["ones"])
    G(lambda g: g.memset(ones_c_f.ap(), 1.0), w=["ones"])
    G(lambda g: g.memset(ones_r_f.ap(), 1.0), w=["ones"])
    G(lambda g: g.memset(ones_sq.ap(), 1.0), w=["ones"])
    Y(lambda s: s.dma_start(ck_sb.ap(), d_ck), w=["ck"], dma="cst")
    Y(lambda s: s.dma_start(sk_sb.ap(), d_sk), w=["sk"], dma="cst")
    Y(lambda s: s.dma_start(id_sb.ap(), d_id), w=["id"], dma="cst")
    Y(lambda s: s.dma_start(fbrow.ap(), d_fb), w=["fbrow"], dma="cst")
    Y(lambda s: s.dma_start(bqc.ap(), d_bq), w=["bqc"], dma="cst")
    Y(lambda s: s.dma_start(bkc.ap(), d_bk), w=["bkc"], dma="cst")
    Y(lambda s: s.dma_start(bvc.ap(), d_bv), w=["bvc"], dma="cst")
    Y(lambda s: s.dma_start(boc.ap(), d_bo), w=["boc"], dma="cst")
    G(lambda g: g.dma_start(
        wslab.ap()[:, 8192:12288].rearrange("p (dc e) -> p dc e", dc=NDC),
        d_wv[0]), w=["ws4", "ws5"], dma="swv")
    for dc in range(NDC):
        G(lambda g, dc=dc: g.dma_start(
            R1_bf.ap()[:, dc * S:(dc + 1) * S], d_x[dc]),
            r=([f"R1_{dc - 2}"] if dc >= 2 else []),
            w=[f"R1_{dc}"], dma=f"gx{dc % 2}")
    G(lambda g: g.dma_start(
        wslab.ap()[:, 0:4096].rearrange("p (h dc e) -> p h dc e", h=2, dc=NDC),
        d_wq[0:2].rearrange("h p dc e -> p h dc e")),
        w=["ws0", "ws1"], dma="swq")
    G(lambda g: g.dma_start(ps_sb.ap(), d_ps), w=["pswap"], dma="gld")
    G(lambda g: g.dma_start(
        wslab.ap()[:, 4096:8192].rearrange("p (h dc e) -> p h dc e", h=2, dc=NDC),
        d_wk[0:2].rearrange("h p dc e -> p h dc e")),
        w=["ws2", "ws3"], dma="swk")
    G(lambda g: g.dma_start(
        wg_sb.ap().rearrange("p (dc e) -> p dc e", dc=NDC), d_wg),
        w=["wg"], dma="gld")
    T(lambda t: t.matmul(pa[0].ap()[:, 0:NE], ones_r_f.ap(), fbrow.ap(),
                         start=True, stop=True),
      r=["ones", "fbrow"], w=["pa0"])
    V(lambda v: v.tensor_copy(fbb.ap(), pa[0].ap()[:, 0:NE]),
      r=["pa0"], w=["fbb"])

    def emit_group(gi):
        if gi > 0:
            G(lambda g, gi=gi: g.dma_start(
                wslab.ap()[:, 8192:12288].rearrange("p (dc e) -> p dc e", dc=NDC),
                d_wv[gi]), w=["ws4", "ws5"], dma="swv")
            G(lambda g, gi=gi: g.dma_start(
                wslab.ap()[:, 0:4096].rearrange(
                    "p (h dc e) -> p h dc e", h=2, dc=NDC),
                d_wq[2 * gi:2 * gi + 2].rearrange("h p dc e -> p h dc e")),
                w=["ws0", "ws1"], dma="swq")
            G(lambda g, gi=gi: g.dma_start(
                wslab.ap()[:, 4096:8192].rearrange(
                    "p (h dc e) -> p h dc e", h=2, dc=NDC),
                d_wk[2 * gi:2 * gi + 2].rearrange("h p dc e -> p h dc e")),
                w=["ws2", "ws3"], dma="swk")

        if gi == 0:
            for dc in range(NDC):
                def vmm0(t, dc=dc):
                    last = None
                    for tc in range(NDC):
                        last = t.matmul(
                            pa[tc // 2].ap()[:, (tc % 2) * 256:
                                             (tc % 2) * 256 + 256],
                            R1_bf.ap()[:, dc * S + tc * 128:
                                       dc * S + tc * 128 + 128],
                            wslab.ap()[:, 8192 + dc * 256: 8192 + dc * 256 + 256],
                            start=(dc == 0 and tc % 2 == 0),
                            stop=(dc == NDC - 1),
                            skip_group_check=True)
                    return last
                T(vmm0, r=[f"R1_{dc}", "ws4" if dc < 8 else "ws5"],
                  w=[f"pa{b}" for b in range(8)])
            for tc in range(NDC):
                A(lambda s, tc=tc: s.copy(
                    vbuf.ap()[:, tc * 256: tc * 256 + 256],
                    pa[tc // 2].ap()[:, (tc % 2) * 256:(tc % 2) * 256 + 256]),
                    r=[f"pa{tc // 2}"], w=[f"vb{tc}"])
        else:
            for tc in range(NDC):
                bank = 2 + (tc % 2)
                pk = f"pa{bank}"

                def vmm(t, tc=tc, bank=bank):
                    last = None
                    for dc in range(NDC):
                        last = t.matmul(
                            pa[bank].ap()[:, 0:256],
                            R1_bf.ap()[:, dc * S + tc * 128: dc * S + tc * 128 + 128],
                            wslab.ap()[:, 8192 + dc * 256: 8192 + dc * 256 + 256],
                            start=(dc == 0), stop=(dc == NDC - 1))
                    return last
                T(vmm, r=R1k + ["ws4", "ws5"], w=[pk])
                A(lambda s, tc=tc, bank=bank: s.copy(
                    vbuf.ap()[:, tc * 256: tc * 256 + 256], pa[bank].ap()[:, 0:256]),
                    r=[pk], w=[f"vb{tc}"])

        for hl in range(2):
            def proj_rope(nt, wbase, wkeys, bcol, bckey, cos, sin, obuf, ooff,
                          okeys, hl=hl, hh=2 * gi + hl, finale=None):
                def emit_tail(nq):
                    bank = nq % 2
                    pk = f"pa{bank}"
                    swb = 2 + (nq % 2)
                    spk = f"pa{swb}"
                    qs_slot = 2 + (nq % 2)
                    t1 = nq % 2
                    t2 = 2 + (nq % 2)
                    T(lambda t, swb=swb, qs_slot=qs_slot: t.matmul(
                        pa[swb].ap(), ps_sb.ap(),
                        scr_bf.ap()[:, qs_slot * 512:(qs_slot + 1) * 512],
                        start=True, stop=True),
                        r=[f"sb{qs_slot}", "pswap"], w=[spk])
                    V(lambda v, nq=nq, swb=swb, t1=t1, sin=sin: v.tensor_tensor(
                        scr_f.ap()[:, t1 * 512:(t1 + 1) * 512], pa[swb].ap(),
                        sin.ap()[:, nq * 512:(nq + 1) * 512], ALU.mult),
                        r=[spk, "sk"], w=[f"sf{t1}"])
                    V(lambda v, nq=nq, t2=t2, qs_slot=qs_slot, cos=cos:
                      v.tensor_tensor(
                        scr_f.ap()[:, t2 * 512:(t2 + 1) * 512],
                        scr_bf.ap()[:, qs_slot * 512:(qs_slot + 1) * 512],
                        cos.ap()[:, nq * 512:(nq + 1) * 512], ALU.mult),
                        r=[f"sb{qs_slot}", "ck"], w=[f"sf{t2}"])
                    V(lambda v, nq=nq, t1=t1, t2=t2: v.tensor_tensor(
                        obuf.ap()[:, ooff + nq * 512: ooff + nq * 512 + 512],
                        scr_f.ap()[:, t2 * 512:(t2 + 1) * 512],
                        scr_f.ap()[:, t1 * 512:(t1 + 1) * 512], ALU.add),
                        r=[f"sf{t1}", f"sf{t2}"], w=[okeys[nq]])

                def emit_evac(nq, pb):
                    bank = pb[nq % 2]
                    qs_slot = 2 + (nq % 2)
                    A(lambda s, bank=bank, qs_slot=qs_slot: s.activation(
                        scr_bf.ap()[:, qs_slot * 512:(qs_slot + 1) * 512],
                        pa[bank].ap(), AF.Identity, bias=bcol.ap()[:, hh:hh + 1]),
                        r=[f"pa{bank}", bckey], w=[f"sb{qs_slot}"])
                    sq = nq % 2
                    V(lambda v, sq=sq, qs_slot=qs_slot: v.tensor_tensor(
                        scr_bf.ap()[:, sq * 512:(sq + 1) * 512],
                        scr_bf.ap()[:, qs_slot * 512:(qs_slot + 1) * 512],
                        scr_bf.ap()[:, qs_slot * 512:(qs_slot + 1) * 512],
                        ALU.mult), r=[f"sb{qs_slot}"], w=[f"sb{sq}"])
                    T(lambda t, sq=sq: t.matmul(
                        pa[4].ap(), ones_sq.ap(),
                        scr_bf.ap()[:, sq * 512:(sq + 1) * 512],
                        start=True, stop=True), r=[f"sb{sq}", "ones"], w=["pa4"])
                    V(lambda v, nq=nq: v.tensor_reduce(
                        msc.ap()[0:1, nq:nq + 1], pa[4].ap()[0:1, :], AX.X,
                        ALU.max), r=["pa4"], w=["msc"])
                    if nt == 4:
                        if nq == 0:
                            V(lambda v: v.tensor_copy(
                                msc.ap()[0:1, 5:6], msc.ap()[0:1, 0:1]),
                                r=["msc"], w=["msc"])
                        else:
                            V(lambda v, nq=nq: v.tensor_tensor(
                                msc.ap()[0:1, 5:6], msc.ap()[0:1, 5:6],
                                msc.ap()[0:1, nq:nq + 1], ALU.max),
                                r=["msc"], w=["msc"])

                prbanks = [(0, 1)] if nt == 2 else [(6, 7), (0, 1)]
                for pr in range(nt // 2):
                    a = 2 * pr
                    pb = prbanks[pr]
                    for dc in range(NDC):
                        def pmm(t, a=a, dc=dc, pb=pb):
                            t.matmul(
                                pa[pb[0]].ap(),
                                wslab.ap()[:, wbase + dc * 128: wbase + dc * 128 + 128],
                                R1_bf.ap()[:, dc * S + a * 512: dc * S + a * 512 + 512],
                                start=(dc == 0), stop=(dc == NDC - 1),
                                skip_group_check=True)
                            return t.matmul(
                                pa[pb[1]].ap(),
                                wslab.ap()[:, wbase + dc * 128: wbase + dc * 128 + 128],
                                R1_bf.ap()[:, dc * S + a * 512 + 512:
                                            dc * S + a * 512 + 1024],
                                start=(dc == 0), stop=(dc == NDC - 1),
                                skip_group_check=True)
                        T(pmm, r=[f"R1_{dc}"] + wkeys, w=[f"pa{pb[0]}", f"pa{pb[1]}"])
                    if pr >= 1:
                        emit_tail(a - 2)
                        emit_tail(a - 1)
                    emit_evac(a, pb)
                    emit_evac(a + 1, pb)
                if finale is not None:
                    finale()
                emit_tail(nt - 2)
                emit_tail(nt - 1)

            qoff = hl * TL
            koff = hl * S
            hh = 2 * gi + hl
            proj_rope(2, hl * 2048, ["ws0" if hl == 0 else "ws1"], bqc, "bqc",
                      ck_sb, sk_sb, qbuf, qoff, [f"qb{hl}_{i}" for i in range(2)])
            V(lambda v: v.tensor_tensor(
                msc.ap()[0:1, 4:5], msc.ap()[0:1, 0:1], msc.ap()[0:1, 1:2],
                ALU.max), r=["msc"], w=["msc"])
            def bound_finale():
                V(lambda v: v.tensor_tensor(
                    msc.ap()[0:1, 6:7], msc.ap()[0:1, 4:5], msc.ap()[0:1, 5:6],
                    ALU.mult), r=["msc"], w=["msc"])
                A(lambda s: s.activation(msc.ap()[0:1, 7:8], msc.ap()[0:1, 6:7],
                                         AF.Ln, scale=SCALE * SCALE),
                  r=["msc"], w=["msc"])
                A(lambda s: s.activation(msc.ap()[0:1, 7:8], msc.ap()[0:1, 7:8],
                                         AF.Exp, scale=0.5),
                  r=["msc"], w=["msc"])
                T(lambda t: t.matmul(
                    pa[3].ap()[:, 0:1], ones_r_f.ap(), msc.ap()[0:1, 7:8],
                    start=True, stop=True), r=["msc", "ones"], w=["pa3"])
                V(lambda v: v.tensor_scalar(
                    mq_sb.ap()[:, 6:7], pa[3].ap()[:, 0:1], -1.0, None,
                    ALU.mult), r=["pa3"], w=["mq"])

            proj_rope(4, 4096 + hl * 2048, ["ws2" if hl == 0 else "ws3"], bkc,
                      "bkc", ck_sb, sk_sb, kbuf, koff,
                      [f"kb{hl}_{i}" for i in range(4)], finale=bound_finale)

            def st_merged():
                def sums_pvs(kc):
                    j = kc % 2
                    for qb in range(2):
                        T(lambda t, kc=kc, j=j, qb=qb: t.matmul(
                            pa[6 + qb].ap(), ones_sq.ap(),
                            scr_bf.ap()[:, j * 1024 + qb * 512:
                                        j * 1024 + qb * 512 + 512],
                            start=(kc == 0), stop=(kc == NDC - 1),
                            skip_group_check=True),
                            r=[f"sb{2 * j + qb}", "ones"], w=[f"pa{6 + qb}"])
                        T(lambda t, kc=kc, j=j, qb=qb, hl=hl: t.matmul(
                            pa[4 + qb].ap(),
                            vbuf.ap()[:, kc * 256 + hl * 128:
                                      kc * 256 + hl * 128 + 128],
                            scr_bf.ap()[:, j * 1024 + qb * 512:
                                        j * 1024 + qb * 512 + 512],
                            start=(kc == 0), stop=(kc == NDC - 1),
                            skip_group_check=True),
                            r=[f"sb{2 * j + qb}", f"vb{kc}"], w=[f"pa{4 + qb}"])

                for kc in range(NDC):
                    j = kc % 2
                    for qb in range(2):
                        T(lambda t, kc=kc, j=j, qb=qb, qoff=qoff, koff=koff:
                          t.matmul(
                            pa[2 * j + qb].ap(),
                            kbuf.ap()[:, koff + kc * 128: koff + kc * 128 + 128],
                            qbuf.ap()[:, qoff + qb * 512: qoff + qb * 512 + 512],
                            start=True, stop=True),
                            r=[f"kb{hl}_{kc // 4}", f"qb{hl}_{qb}"],
                            w=[f"pa{2 * j + qb}"])
                    A(lambda s, j=j: s.activation(
                        scr_bf.ap()[:, j * 1024:(j + 1) * 1024],
                        pb[j].ap(), AF.Exp, bias=mq_sb.ap()[:, 6:7],
                        scale=SCALE),
                        r=[f"pa{2 * j}", f"pa{2 * j + 1}", "mq"],
                        w=[f"sb{2 * j}", f"sb{2 * j + 1}"])
                    if kc >= 1:
                        sums_pvs(kc - 1)
                sums_pvs(NDC - 1)

            def epilogue_m():
                for qb in range(2):
                    A(lambda s, qb=qb: s.activation(
                        scr_f.ap()[:, qb * 512:(qb + 1) * 512],
                        pa[6 + qb].ap(), AF.Ln),
                        r=[f"pa{6 + qb}"], w=[f"sf{qb}"])
                    A(lambda s, qb=qb: s.activation(
                        scr_f.ap()[:, qb * 512:(qb + 1) * 512],
                        scr_f.ap()[:, qb * 512:(qb + 1) * 512], AF.Exp,
                        scale=-1.0), r=[f"sf{qb}"], w=[f"sf{qb}"])
                    V(lambda v, qb=qb, hh=hh: v.tensor_tensor(
                        aog.ap()[:, hh * TL + qb * 512: hh * TL + qb * 512 + 512],
                        pa[4 + qb].ap(), scr_f.ap()[:, qb * 512:(qb + 1) * 512],
                        ALU.mult), r=[f"pa{4 + qb}", f"sf{qb}"], w=[f"ao{hh}"])
                    A(lambda s, hh=hh, qb=qb: s.activation(
                        aog.ap()[:, hh * TL + qb * 512: hh * TL + qb * 512 + 512],
                        aog.ap()[:, hh * TL + qb * 512: hh * TL + qb * 512 + 512],
                        AF.Identity, bias=bvc.ap()[:, hh:hh + 1]),
                        r=[f"ao{hh}", "bvc"], w=[f"ao{hh}"])

            st_merged()
            epilogue_m()

    for gi in range(NH // 2):
        emit_group(gi)
        if debug and gi == 0:
            Y(lambda s: s.dma_start(d_dbg_qkv[0][:, 0:2 * TL], qbuf.ap()),
              r=["qb0_0", "qb0_1", "qb1_0", "qb1_1"], dma="dbg5")
            Y(lambda s: s.dma_start(d_dbg_qkv[1], kbuf.ap()),
              r=[f"kb{h}_{i}" for h in range(2) for i in range(4)], dma="dbg5")
            Y(lambda s: s.dma_start(d_dbg_qkv[2], vbuf.ap()),
              r=[f"vb{t}" for t in range(NDC)], dma="dbg5")

    for i in range(NDC):
        V(lambda v, i=i: v.tensor_copy(
            xy.ap()[:, i * TL:(i + 1) * TL], R1_bf.ap()[:, i * S: i * S + TL]),
            r=[f"R1_{i}"], w=[f"xy{i}"])

    if debug:
        Y(lambda s: s.dma_start(d_dbg_ao, aog.ap()),
          r=[f"ao{h}" for h in range(NH)], dma="dbg6")
    pending_stats = [None]
    for ec in range(NHC):
        sl = ec % 2
        wok = f"ws{sl}"
        G(lambda g, ec=ec, sl=sl: g.dma_start(
            wslab.ap()[:, sl * 2048:(sl + 1) * 2048].rearrange(
                "p (dc e) -> p dc e", dc=NDC), d_wo[ec]),
            w=[wok], dma=f"swo{sl}")
        ob = 0 if ec % 2 == 0 else 2
        for dc in range(NDC):
            def omm(t, ec=ec, dc=dc, sl=sl, ob=ob):
                for nq in range(2):
                    last = t.matmul(
                        pa[ob + nq].ap(),
                        wslab.ap()[:, sl * 2048 + dc * 128:
                                   sl * 2048 + dc * 128 + 128],
                        aog.ap()[:, dc * TL + nq * 512: dc * TL + nq * 512 + 512],
                        start=(dc == 0), stop=(dc == NDC - 1),
                        skip_group_check=True)
                return last
            T(omm, r=[f"ao{dc}", wok], w=[f"pa{ob}", f"pa{ob + 1}"])
        if pending_stats[0] is not None:
            pending_stats[0]()
        for nq in range(2):
            V(lambda v, ec=ec, nq=nq, ob=ob: v.scalar_tensor_tensor(
                R1_f32.ap()[:, ec * TL + nq * 512: ec * TL + nq * 512 + 512],
                pa[ob + nq].ap(), boc.ap()[:, ec:ec + 1],
                xy.ap()[:, ec * TL + nq * 512: ec * TL + nq * 512 + 512],
                ALU.add, ALU.add),
                r=[f"pa{ob + nq}", f"xy{ec}", "boc"], w=[f"R1_{ec}"])

        def emit_stats(ec=ec):
            sqs = ec % 2
            A(lambda s, ec=ec, sqs=sqs: s.activation(
                vbuf.ap()[:, sqs * 1024:(sqs + 1) * 1024],
                R1_f32.ap()[:, ec * TL:(ec + 1) * TL], AF.Square),
                r=[f"R1_{ec}"], w=[f"vb{t}" for t in range(4 * sqs, 4 * sqs + 4)])
            A(lambda s, ec=ec, sqs=sqs: s.activation(
                scr_bf.ap()[:, sqs * 1024:(sqs + 1) * 1024],
                R1_f32.ap()[:, ec * TL:(ec + 1) * TL], AF.Identity),
                r=[f"R1_{ec}"], w=[f"sb{2 * sqs}", f"sb{2 * sqs + 1}"])
            for nq in range(2):
                T(lambda t, ec=ec, nq=nq, sqs=sqs: t.matmul(
                    pa[4 + nq].ap(), ones_sq.ap(),
                    scr_bf.ap()[:, sqs * 1024 + nq * 512:
                                sqs * 1024 + nq * 512 + 512],
                    start=(ec == 0), stop=(ec == NHC - 1),
                    skip_group_check=True),
                    r=[f"sb{2 * sqs + nq}", "ones"], w=[f"pa{4 + nq}"])
                T(lambda t, ec=ec, nq=nq, sqs=sqs: t.matmul(
                    pa[6 + nq].ap(), ones_sq.ap(),
                    vbuf.ap()[:, sqs * 1024 + nq * 512:
                              sqs * 1024 + nq * 512 + 512],
                    start=(ec == 0), stop=(ec == NHC - 1),
                    skip_group_check=True),
                    r=[f"vb{4 * sqs + nq * 2}", f"vb{4 * sqs + nq * 2 + 1}",
                       "ones"], w=[f"pa{6 + nq}"])
        pending_stats[0] = emit_stats
    pending_stats[0]()

    def emit_ln(src_f32, srckeys, dst, dstkeys, inplace, mpb, vpb, mkeys, vkeys):
        V(lambda v: v.tensor_scalar(
            kbuf.ap()[0:1, 0:1024], mpb.ap()[0:1, :], 1.0 / D, None,
            ALU.mult), r=mkeys, w=["kb0_0", "kb0_1"])
        V(lambda v: v.tensor_tensor(
            kbuf.ap()[0:1, 1024:2048], kbuf.ap()[0:1, 0:1024],
            kbuf.ap()[0:1, 0:1024], ALU.mult),
            r=["kb0_0", "kb0_1"], w=["kb0_2", "kb0_3"])
        V(lambda v: v.tensor_scalar(
            scr_f.ap()[0:1, 1024:2048], vpb.ap()[0:1, :], 1.0 / D, EPS,
            ALU.mult, ALU.add), r=vkeys, w=["sf2", "sf3"])
        V(lambda v: v.tensor_tensor(
            scr_f.ap()[0:1, 1024:2048], scr_f.ap()[0:1, 1024:2048],
            kbuf.ap()[0:1, 1024:2048], ALU.subtract),
            r=["kb0_2", "kb0_3", "sf2", "sf3"], w=["sf2", "sf3"])
        A(lambda s: s.activation(
            scr_f.ap()[0:1, 1024:2048], scr_f.ap()[0:1, 1024:2048], AF.Ln),
            r=["sf2", "sf3"], w=["sf2", "sf3"])
        A(lambda s: s.activation(
            scr_f.ap()[0:1, 1024:2048], scr_f.ap()[0:1, 1024:2048], AF.Exp,
            scale=-0.5), r=["sf2", "sf3"], w=["sf2", "sf3"])
        for nq in range(2):
            T(lambda t, nq=nq: t.matmul(
                pa[2].ap(), ones_r_bf.ap()[0:1, 0:128],
                kbuf.ap()[0:1, nq * 512: nq * 512 + 512],
                start=True, stop=True), r=[f"kb0_{nq}", "ones"], w=["pa2"])
            V(lambda v, nq=nq: v.tensor_copy(
                qbuf.ap()[:, nq * 512:(nq + 1) * 512], pa[2].ap()),
                r=["pa2"], w=[f"qb0_{nq}"])
            T(lambda t, nq=nq: t.matmul(
                pa[3].ap(), ones_r_f.ap()[0:1, 0:128],
                scr_f.ap()[0:1, 1024 + nq * 512: 1024 + nq * 512 + 512],
                start=True, stop=True), r=[f"sf{2 + nq}", "ones"], w=["pa3"])
            V(lambda v, nq=nq: v.tensor_copy(
                qbuf.ap()[:, 1024 + nq * 512: 1024 + nq * 512 + 512], pa[3].ap()),
                r=["pa3"], w=[f"qb1_{nq}"])
        for ec in range(NHC):
            vsl = ec % 2
            vk = [f"vb{t}" for t in range(4 * vsl, 4 * vsl + 4)]
            V(lambda v, ec=ec, vsl=vsl: v.tensor_tensor(
                vbuf.ap()[:, vsl * 1024:(vsl + 1) * 1024],
                src_f32.ap()[:, ec * TL:(ec + 1) * TL],
                qbuf.ap()[:, 0:1024], ALU.subtract),
                r=[srckeys[ec], "qb0_0", "qb0_1"], w=vk)
            V(lambda v, ec=ec, vsl=vsl: v.tensor_tensor(
                dst.ap()[:, ec * TL:(ec + 1) * TL],
                vbuf.ap()[:, vsl * 1024:(vsl + 1) * 1024],
                qbuf.ap()[:, 1024:2048], ALU.mult),
                r=vk + ["qb1_0", "qb1_1"], w=[dstkeys[ec]])
            if inplace:
                Y(lambda s, ec=ec: s.dma_start(
                    d_y[ec], R1_f32.ap()[:, ec * TL:(ec + 1) * TL]),
                    r=[f"R1_{ec}"], dma="st")

    emit_ln(R1_f32, R1k, xy, XYk, False, pb[2], pb[3],
            ["pa4", "pa5"], ["pa6", "pa7"])

    for dc in range(NDC):
        def lmm(t, dc=dc):
            for nq in range(2):
                last = t.matmul(
                    pa[nq].ap()[0:8, :], wg_sb.ap()[:, dc * NE:(dc + 1) * NE],
                    xy.ap()[:, dc * TL + nq * 512: dc * TL + nq * 512 + 512],
                    start=(dc == 0), stop=(dc == NDC - 1),
                    skip_group_check=True)
            return last
        T(lmm, r=[f"xy{dc}", "wg"], w=["pa0", "pa1"])
    for nq in range(2):
        V(lambda v, nq=nq: v.tensor_copy(
            scr_f.ap()[0:8, nq * 512:(nq + 1) * 512], pa[nq].ap()[0:8, :]),
            r=[f"pa{nq}"], w=["sf0", "sf1"])
    for tt in range(8):
        T(lambda t, tt=tt: t.transpose(
            pa[2].ap()[:, 0:8], scr_f.ap()[0:8, tt * 128:(tt + 1) * 128],
            id_sb.ap()[0:8, 0:8]), r=["sf0", "sf1", "id"], w=["pa2"])
        V(lambda v, tt=tt: v.tensor_copy(
            rt_sb.ap()[:, tt * 8:(tt + 1) * 8], pa[2].ap()[:, 0:8]),
            r=["pa2"], w=["rt"])
    A(lambda s: s.activation(wr_sb.ap(), rt_sb.ap(), AF.Sigmoid),
      r=["rt"], w=["wr"])
    for tt in range(8):
        V(lambda v, tt=tt: v.tensor_tensor(
            sc_sb.ap()[:, tt * 8:(tt + 1) * 8], wr_sb.ap()[:, tt * 8:(tt + 1) * 8],
            fbb.ap(), ALU.add), r=["wr", "fbb"], w=["sc"])
    for tt in range(8):
        sl8 = slice(tt * 8, (tt + 1) * 8)
        V(lambda v, tt=tt: v.tensor_reduce(
            m1_sb.ap()[:, tt:tt + 1], sc_sb.ap()[:, tt * 8:(tt + 1) * 8], AX.X,
            ALU.max), r=["sc"], w=["m1"])
        V(lambda v, tt=tt: v.tensor_scalar(
            s2_sb.ap()[:, tt * 8:(tt + 1) * 8], sc_sb.ap()[:, tt * 8:(tt + 1) * 8],
            m1_sb.ap()[:, tt:tt + 1], None, ALU.is_ge), r=["sc", "m1"], w=["s2"])
        V(lambda v, tt=tt: v.tensor_scalar(
            s2_sb.ap()[:, tt * 8:(tt + 1) * 8], s2_sb.ap()[:, tt * 8:(tt + 1) * 8],
            -BIG, None, ALU.mult), r=["s2"], w=["s2"])
        V(lambda v, tt=tt: v.tensor_tensor(
            s2_sb.ap()[:, tt * 8:(tt + 1) * 8], sc_sb.ap()[:, tt * 8:(tt + 1) * 8],
            s2_sb.ap()[:, tt * 8:(tt + 1) * 8], ALU.add), r=["sc", "s2"], w=["s2"])
        V(lambda v, tt=tt: v.tensor_reduce(
            m1_sb.ap()[:, tt:tt + 1], s2_sb.ap()[:, tt * 8:(tt + 1) * 8], AX.X,
            ALU.max), r=["s2"], w=["m1"])
        V(lambda v, tt=tt: v.tensor_scalar(
            s2_sb.ap()[:, tt * 8:(tt + 1) * 8], sc_sb.ap()[:, tt * 8:(tt + 1) * 8],
            m1_sb.ap()[:, tt:tt + 1], None, ALU.is_ge), r=["sc", "m1"], w=["s2"])
        V(lambda v, tt=tt: v.tensor_tensor(
            ew_sb.ap()[:, tt * 8:(tt + 1) * 8], wr_sb.ap()[:, tt * 8:(tt + 1) * 8],
            s2_sb.ap()[:, tt * 8:(tt + 1) * 8], ALU.mult), r=["wr", "s2"], w=["ew"])
        V(lambda v, tt=tt: v.tensor_reduce(
            m1_sb.ap()[:, tt:tt + 1], ew_sb.ap()[:, tt * 8:(tt + 1) * 8], AX.X,
            ALU.add), r=["ew"], w=["m1"])
    V(lambda v: v.reciprocal(m1_sb.ap()[:, 0:8], m1_sb.ap()[:, 0:8]),
      r=["m1"], w=["m1"])
    for tt in range(8):
        V(lambda v, tt=tt: v.tensor_scalar(
            ew_sb.ap()[:, tt * 8:(tt + 1) * 8], ew_sb.ap()[:, tt * 8:(tt + 1) * 8],
            m1_sb.ap()[:, tt:tt + 1], None, ALU.mult), r=["ew", "m1"], w=["ew"])

    for hc in range(NHC):
        V(lambda v, hc=hc: v.tensor_copy(
            R1_f32.ap()[:, hc * TL:(hc + 1) * TL],
            xy.ap()[:, hc * TL:(hc + 1) * TL]), r=[f"xy{hc}"], w=[f"R1_{hc}"])

    W1OFF = [0, 2048]
    W3OFF = [4096, 6144]
    W2OFF = [8192, 9216]
    W1KEY = ["ws0", "ws1"]
    W3KEY = ["ws2", "ws3"]
    W2KEY = ["ws4a", "ws4b"]

    w13_q = [(e, fc) for e in range(NE) for fc in range(NFC)]
    w2_q = [(e, hc) for e in range(NE) for hc in range(NHC)]
    w13_pos = [0]
    w2_pos = [0]

    def pump_w13(n=1):
        for _ in range(n):
            if w13_pos[0] >= len(w13_q):
                return
            e, fc = w13_q[w13_pos[0]]
            w13_pos[0] += 1
            s1 = fc % 2
            G(lambda g, e=e, fc=fc, s1=s1: g.dma_start(
                wslab.ap()[:, W1OFF[s1]:W1OFF[s1] + 2048].rearrange(
                    "p (dc f) -> p dc f", dc=NDC), d_w1[e, fc]),
                w=[W1KEY[s1]], dma=f"sw1_{s1}")
            G(lambda g, e=e, fc=fc, s1=s1: g.dma_start(
                wslab.ap()[:, W3OFF[s1]:W3OFF[s1] + 2048].rearrange(
                    "p (dc f) -> p dc f", dc=NDC), d_w3[e, fc]),
                w=[W3KEY[s1]], dma=f"sw3_{s1}")

    def pump_w2(n=1):
        for _ in range(n):
            if w2_pos[0] >= len(w2_q):
                return
            e, hc = w2_q[w2_pos[0]]
            w2_pos[0] += 1
            s2 = hc % 2
            G(lambda g, e=e, hc=hc, s2=s2: g.dma_start(
                wslab.ap()[:, W2OFF[s2]:W2OFF[s2] + 1024].rearrange(
                    "p (fc h) -> p fc h", fc=NFC), d_w2[e, hc]),
                w=[W2KEY[s2]], dma=f"sw2_{s2}")

    def emit_ew_rows(e):
        sl = e % 2
        for tt in range(8):
            bank = 2 + tt % 2
            T(lambda t, e=e, tt=tt, bank=bank: t.transpose(
                pa[bank].ap()[0:1, 0:128],
                ew_sb.ap()[:, tt * 8 + e: tt * 8 + e + 1], id_sb.ap()),
                r=["ew", "id"], w=[f"pa{bank}"])
            V(lambda v, tt=tt, bank=bank, sl=sl: v.tensor_copy(
                kbuf.ap()[0:1, sl * 1024 + tt * 128: sl * 1024 + tt * 128 + 128],
                pa[bank].ap()[0:1, 0:128]),
                r=[f"pa{bank}"], w=[f"kb0_{2 * sl + tt // 4}"])
        for nq in range(2):
            bank = 2 + nq
            T(lambda t, nq=nq, sl=sl, bank=bank: t.matmul(
                pa[bank].ap(), ones_r_bf.ap()[0:1, 0:128],
                kbuf.ap()[0:1, sl * 1024 + nq * 512: sl * 1024 + nq * 512 + 512],
                start=True, stop=True),
                r=[f"kb0_{2 * sl + nq}", "ones"], w=[f"pa{bank}"])
            V(lambda v, nq=nq, sl=sl, bank=bank: v.tensor_copy(
                scr_f.ap()[:, sl * 1024 + nq * 512: sl * 1024 + nq * 512 + 512],
                pa[bank].ap()), r=[f"pa{bank}"], w=[f"sf{2 * sl + nq}"])

    def emit_h13(e, hook=None):
        sl = e % 2
        for fc in range(NFC):
            s1 = fc % 2
            for dc in range(NDC):
                def h1mm(t, dc=dc, s1=s1):
                    for nc2 in range(2):
                        last = t.matmul(
                            pa[nc2].ap(),
                            wslab.ap()[:, W1OFF[s1] + dc * 128:
                                       W1OFF[s1] + dc * 128 + 128],
                            xy.ap()[:, dc * TL + nc2 * 512:
                                    dc * TL + nc2 * 512 + 512],
                            start=(dc == 0), stop=(dc == NDC - 1),
                            skip_group_check=True)
                    return last
                T(h1mm, r=[f"xy{dc}", W1KEY[s1]], w=["pa0", "pa1"])
            for nc2 in range(2):
                A(lambda s, nc2=nc2: s.activation(
                    scr_bf.ap()[:, nc2 * 512:(nc2 + 1) * 512], pa[nc2].ap(),
                    AF.Silu), r=[f"pa{nc2}"], w=[f"sb{nc2}"])
            for dc in range(NDC):
                def h3mm(t, dc=dc, s1=s1):
                    for nc2 in range(2):
                        last = t.matmul(
                            pa[4 + nc2].ap(),
                            wslab.ap()[:, W3OFF[s1] + dc * 128:
                                       W3OFF[s1] + dc * 128 + 128],
                            xy.ap()[:, dc * TL + nc2 * 512:
                                    dc * TL + nc2 * 512 + 512],
                            start=(dc == 0), stop=(dc == NDC - 1),
                            skip_group_check=True)
                    return last
                T(h3mm, r=[f"xy{dc}", W3KEY[s1]], w=["pa4", "pa5"])
                if hook is not None and fc == 0 and dc == NDC - 1:
                    hook()
                    hook = None
            for nc2 in range(2):
                bsl = 2 + nc2
                V(lambda v, bsl=bsl, sl=sl, nc2=nc2: v.tensor_tensor(
                    scr_bf.ap()[:, bsl * 512:(bsl + 1) * 512], pa[4 + nc2].ap(),
                    scr_f.ap()[:, sl * 1024 + nc2 * 512: sl * 1024 + nc2 * 512 + 512],
                    ALU.mult), r=[f"pa{4 + nc2}", f"sf{2 * sl + nc2}"], w=[f"sb{bsl}"])
                V(lambda v, fc=fc, nc2=nc2, bsl=bsl: v.tensor_tensor(
                    aog.ap()[:, fc * TL + nc2 * 512: fc * TL + nc2 * 512 + 512],
                    scr_bf.ap()[:, nc2 * 512:(nc2 + 1) * 512],
                    scr_bf.ap()[:, bsl * 512:(bsl + 1) * 512], ALU.mult),
                    r=[f"sb{nc2}", f"sb{bsl}"], w=[f"ao{fc}"])
            pump_w13(1)

    def emit_ln2_stats(ec):
        sqs = ec % 2
        A(lambda s, ec=ec, sqs=sqs: s.activation(
            vbuf.ap()[:, sqs * 1024:(sqs + 1) * 1024],
            R1_f32.ap()[:, ec * TL:(ec + 1) * TL], AF.Square),
            r=[f"R1_{ec}"], w=[f"vb{t}" for t in range(4 * sqs, 4 * sqs + 4)])
        A(lambda s, ec=ec, sqs=sqs: s.activation(
            scr_bf.ap()[:, sqs * 1024:(sqs + 1) * 1024],
            R1_f32.ap()[:, ec * TL:(ec + 1) * TL], AF.Identity),
            r=[f"R1_{ec}"], w=[f"sb{2 * sqs}", f"sb{2 * sqs + 1}"])
        for nq in range(2):
            T(lambda t, ec=ec, nq=nq, sqs=sqs: t.matmul(
                pa[0 + nq].ap(), ones_sq.ap(),
                scr_bf.ap()[:, sqs * 1024 + nq * 512: sqs * 1024 + nq * 512 + 512],
                start=(ec == 0), stop=(ec == NHC - 1), skip_group_check=True),
                r=[f"sb{2 * sqs + nq}", "ones"], w=[f"pa{0 + nq}"])
            T(lambda t, ec=ec, nq=nq, sqs=sqs: t.matmul(
                pa[4 + nq].ap(), ones_sq.ap(),
                vbuf.ap()[:, sqs * 1024 + nq * 512: sqs * 1024 + nq * 512 + 512],
                start=(ec == 0), stop=(ec == NHC - 1), skip_group_check=True),
                r=[f"vb{4 * sqs + nq * 2}", f"vb{4 * sqs + nq * 2 + 1}", "ones"],
                w=[f"pa{4 + nq}"])

    def emit_w2(e, stats=False):
        pend = [None]
        for hc in range(NHC):
            s2 = hc % 2
            wb = 6 if hc % 2 == 0 else 2
            for fc in range(NFC):
                def w2mm(t, hc=hc, fc=fc, s2=s2, wb=wb):
                    for nc2 in range(2):
                        last = t.matmul(
                            pa[wb + nc2].ap(),
                            wslab.ap()[:, W2OFF[s2] + fc * 128:
                                       W2OFF[s2] + fc * 128 + 128],
                            aog.ap()[:, fc * TL + nc2 * 512: fc * TL + nc2 * 512 + 512],
                            start=(fc == 0), stop=(fc == NFC - 1),
                            skip_group_check=True)
                    return last
                T(w2mm, r=[f"ao{fc}", W2KEY[s2]], w=[f"pa{wb}", f"pa{wb + 1}"])
            if pend[0] is not None:
                pend[0]()
                pend[0] = None
            for nc2 in range(2):
                V(lambda v, hc=hc, nc2=nc2, wb=wb: v.tensor_tensor(
                    R1_f32.ap()[:, hc * TL + nc2 * 512: hc * TL + nc2 * 512 + 512],
                    pa[wb + nc2].ap(),
                    R1_f32.ap()[:, hc * TL + nc2 * 512: hc * TL + nc2 * 512 + 512],
                    ALU.add), r=[f"pa{wb + nc2}", f"R1_{hc}"], w=[f"R1_{hc}"])
            if stats:
                pend[0] = lambda hc=hc: emit_ln2_stats(hc)
            pump_w2(1)
        if pend[0] is not None:
            pend[0]()

    pump_w13(2)
    pump_w2(2)
    for e in range(NE):
        emit_h13(e, hook=lambda e=e: emit_ew_rows(e))
        emit_w2(e, stats=(e == NE - 1))
        if debug:
            Y(lambda s, e=e: s.dma_start(d_dbg_acc[e], R1_f32.ap()),
              r=[f"R1_{i}" for i in range(NHC)], dma="dbg2")
            Y(lambda s, e=e: s.dma_start(d_dbg_g[e], aog.ap()),
              r=[f"ao{j}" for j in range(NH)], dma="dbg3")
            Y(lambda s, e=e, sl=e % 2: s.dma_start(
                d_dbg_ewb[e], scr_f.ap()[:, sl * TL:(sl + 1) * TL]),
                r=[f"sf{2 * (e % 2)}", f"sf{2 * (e % 2) + 1}"], dma=f"dbg4{e % 2}")

    emit_ln(R1_f32, R1k, R1_f32, R1k, True, pb[0], pb[2],
            ["pa0", "pa1"], ["pa4", "pa5"])

    build_program.last_sched = S_
    handles = S_.materialize(nc)
    st_total = S_.totals.get(("dma", "st"), 0)
    import contextlib
    with nc.allow_low_precision(
            reason="bf16 intermediates by design; fp32 accumulation in PSUM, "
                   "graded against the fp64 sim"), nc.Block() as block:
        @block.tensor
        def _(t):
            S_.replay("tensor", t)

        @block.vector
        def _(v):
            S_.replay("vector", v)

        @block.scalar
        def _(s):
            S_.replay("scalar", s)

        @block.gpsimd
        def _(g):
            S_.replay("gpsimd", g)

        @block.sync
        def _(s):
            S_.replay("sync", s)
            s.wait_ge(handles[("dma", "st")], st_total)
    return nc


_NC_CACHE = {}


def _get_nc():
    if "nc" not in _NC_CACHE:
        _NC_CACHE["nc"] = build_program()
    return _NC_CACHE["nc"]


def _host_inputs(inputs):
    f = np.float32
    x = np.asarray(inputs["L0_attn_x"], f)
    Wq = np.asarray(inputs["L0_attn_Wq"], f)
    Wk = np.asarray(inputs["L0_attn_Wk"], f)
    Wv = np.asarray(inputs["L0_attn_Wv"], f)
    Wo = np.asarray(inputs["L0_attn_Wo"], f)
    Wg = np.asarray(inputs["L0_ffn_Wg"], f)
    W1 = np.asarray(inputs["L0_ffn_W1"], f)
    W2 = np.asarray(inputs["L0_ffn_W2"], f)
    W3 = np.asarray(inputs["L0_ffn_W3"], f)

    def att_t(W):
        return np.ascontiguousarray(
            W.reshape(NH, 128, NDC, 128).transpose(0, 3, 2, 1))

    shared = {
        "wq_t": att_t(Wq), "wk_t": att_t(Wk), "wo_t": att_t(Wo),
        "wv_t": np.ascontiguousarray(
            Wv.reshape(NH // 2, 256, NDC, 128).transpose(0, 3, 2, 1)),
        "w1_t": np.ascontiguousarray(
            W1.reshape(NE, NFC, 128, NDC, 128).transpose(0, 1, 4, 3, 2)),
        "w3_t": np.ascontiguousarray(
            W3.reshape(NE, NFC, 128, NDC, 128).transpose(0, 1, 4, 3, 2)),
        "w2_t": np.ascontiguousarray(
            W2.reshape(NE, NHC, 128, NFC, 128).transpose(0, 1, 4, 3, 2)),
        "wg_t": np.ascontiguousarray(Wg.T.reshape(NDC, 128, NE).transpose(1, 0, 2)),
        "bq": np.ascontiguousarray(np.asarray(inputs["L0_attn_bq"], f).reshape(NH, 128).T),
        "bk": np.ascontiguousarray(np.asarray(inputs["L0_attn_bk"], f).reshape(NH, 128).T),
        "bv": np.ascontiguousarray(np.asarray(inputs["L0_attn_bv"], f).reshape(NH, 128).T),
        "bo": np.ascontiguousarray(np.asarray(inputs["L0_attn_bo"], f).reshape(NHC, 128).T),
        "fbias": np.asarray(inputs["L0_ffn_bias"], f).reshape(1, NE),
    }
    pswap = np.zeros((128, 128), f)
    pswap[np.arange(128), np.arange(128) ^ 1] = 1.0
    shared["pswap"] = pswap
    shared["ident"] = np.eye(128, dtype=f)

    f32 = np.float32
    freqs = np.array(_ROPE_FREQ_BITS, dtype=np.uint32).view(f32)
    pos = np.arange(S, dtype=f32)
    ang = pos[:, None] * freqs[None, :]
    cosT = np.cos(ang)
    sinT = np.sin(ang)
    cos = np.repeat(cosT.T, 2, axis=0)
    sin = sinT.T
    ssgn = np.empty((128, S), f32)
    ssgn[0::2] = -sin
    ssgn[1::2] = sin

    in_maps = []
    for c in range(8):
        b = c // 2
        half = c % 2
        own = np.arange(half * TL, (half + 1) * TL)
        other = np.arange((1 - half) * TL, (2 - half) * TL)
        perm = np.concatenate([own, other])
        m = dict(shared)
        m["x_t"] = np.ascontiguousarray(x[b].T[:, perm]).reshape(NDC, 128, S)
        m["rope_ck"] = np.ascontiguousarray(cos[:, perm]).astype(f)
        m["rope_sk"] = np.ascontiguousarray(ssgn[:, perm]).astype(f)
        in_maps.append(m)
    return in_maps


def kernel(**inputs):
    in_maps = _host_inputs(inputs)
    nc = _get_nc()
    res = run_bass_kernel_spmd(nc, in_maps, core_ids=list(range(8)))
    y = np.empty((4, S, D), np.float32)
    for c in range(8):
        o = np.asarray(res.results[c]["y"])
        y[c // 2, (c % 2) * TL:(c % 2 + 1) * TL, :] = \
            o.transpose(2, 0, 1).reshape(TL, D)
    return y


# version 21
# speedup vs baseline: 1.0330x; 1.0049x over previous
import numpy as np
import concourse.bass as bass
import concourse.mybir as mybir
from concourse.bass_utils import run_bass_kernel_spmd

F32 = mybir.dt.float32
BF16 = mybir.dt.bfloat16
AF = mybir.ActivationFunctionType
ALU = mybir.AluOpType
AX = mybir.AxisListType

D = 2048
NDC = 16
NH = 16
HD = 128
S = 2048
TL = 1024
NE = 8
FFN = 1024
NFC = 8
NHC = 16
SCALE = float(HD) ** -0.5
EPS = 1e-5
BIG = 1e30

ENGS = ("tensor", "vector", "scalar", "gpsimd", "sync")

_ROPE_FREQ_BITS = [
    0x3f800000, 0x3f5dafd7, 0x3f3ff911, 0x3f263de0, 0x3f0ff59a, 0x3ef953cf, 0x3ed7e89b, 0x3ebaf81b,
    0x3ea1e89b, 0x3e8c3504, 0x3e72d423, 0x3e5247ed, 0x3e361887, 0x3e1db040, 0x3e088d77, 0x3dec7fd6,
    0x3dcccccd, 0x3db15978, 0x3d99940d, 0x3d84fe4d, 0x3d6655c2, 0x3d47763f, 0x3d2cba15, 0x3d159348,
    0x3d0186e3, 0x3ce054d2, 0x3cc2434f, 0x3ca8398b, 0x3c91ad39, 0x3c7c4d33, 0x3c5a7bf2, 0x3c3d3311,
    0x3c23d70a, 0x3c0de12d, 0x3bf5b9b0, 0x3bd4ca15, 0x3bb8449c, 0x3b9f91cc, 0x3b8a2e77, 0x3b6f520e,
    0x3b4f3e38, 0x3b33770f, 0x3b1b690d, 0x3b06946f, 0x3ae91528, 0x3ac9d75c, 0x3aaec98e, 0x3a975c0e,
    0x3a83126f, 0x3a6301e2, 0x3a44948c, 0x3a2a3b44, 0x3a136a16, 0x39ff4fac, 0x39dd1725, 0x39bf74d7,
    0x39a5cb60, 0x398f9272, 0x3978a815, 0x395753e4, 0x393a7753, 0x39217916, 0x390bd472, 0x38f22ce2,
]


class Sched:

    def __init__(self):
        self.items = []
        self.stream = {e: [] for e in ENGS}
        self.lastw = {}
        self.readers = {}
        self.batch_sems = set()

    def op(self, eng, fn, r=(), w=(), dma=None):
        o = {"eng": eng, "fn": fn, "dma": dma, "idx": len(self.items),
             "marked": False, "cnt": None,
             "sem": ("dma", dma) if dma else ("eng", eng)}
        deps = {}

        def add_dep(d):
            if d is None or d is o:
                return
            if (eng == "tensor" and d["eng"] == "tensor"
                    and d["dma"] is None and dma is None):
                return
            if dma is not None and d["dma"] == dma and dma in self.batch_sems:
                return
            k = d["sem"]
            prev = deps.get(k)
            if prev is None or d["idx"] > prev["idx"]:
                deps[k] = d

        for res in r:
            add_dep(self.lastw.get(res))
        for res in w:
            add_dep(self.lastw.get(res))
            for dd in self.readers.get(res, {}).values():
                add_dep(dd)
        o["deps"] = list(deps.values())
        for res in r:
            self.readers.setdefault(res, {})[o["sem"]] = o
        for res in w:
            self.lastw[res] = o
            self.readers[res] = {}
        self.items.append(o)
        self.stream[eng].append(o)
        return o

    def materialize(self, nc):
        for o in self.items:
            if o["dma"] in ("st", "dbg", "dbg2", "dbg3", "dbg40", "dbg41", "dbg5", "dbg6"):
                o["marked"] = True
            for d in o["deps"]:
                d["marked"] = True
        totals = {}
        for o in self.items:
            if o["marked"]:
                amt = 16 if o["dma"] else 1
                totals[o["sem"]] = totals.get(o["sem"], 0) + amt
                o["cnt"] = totals[o["sem"]]
        self.totals = totals
        self.handles = {}
        for skey in totals:
            name = skey[1] if skey[0] == "dma" else "e_" + skey[1]
            self.handles[skey] = nc.alloc_semaphore(name)
        return self.handles

    def replay(self, engname, engobj):
        wm = {}
        for o in self.stream[engname]:
            waits = {}
            for d in o["deps"]:
                if d["cnt"] is None:
                    continue
                val = (self.totals[d["sem"]]
                       if (d["dma"] and d["dma"] in self.batch_sems)
                       else d["cnt"])
                if waits.get(d["sem"], 0) < val:
                    waits[d["sem"]] = val
            wl = [(k, v) for k, v in waits.items() if wm.get(k, 0) < v]
            for k, v in wl:
                engobj.wait_ge(self.handles[k], v)
                wm[k] = v
            inst = o["fn"](engobj)
            if o["marked"]:
                inst.then_inc(self.handles[o["sem"]], 16 if o["dma"] else 1)


def build_program(debug=False):
    nc = bass.Bass("TRN2", target_bir_lowering=False)

    def din(name, shape):
        return nc.dram_tensor(name, list(shape), F32, kind="ExternalInput").ap()

    d_x = din("x_t", (NDC, 128, S))
    d_wq = din("wq_t", (NH, 128, NDC, 128))
    d_wk = din("wk_t", (NH, 128, NDC, 128))
    d_wv = din("wv_t", (NH // 2, 128, NDC, 256))
    d_wo = din("wo_t", (NHC, 128, NDC, 128))
    d_w1 = din("w1_t", (NE, NFC, 128, NDC, 128))
    d_w3 = din("w3_t", (NE, NFC, 128, NDC, 128))
    d_w2 = din("w2_t", (NE, NHC, 128, NFC, 128))
    d_wg = din("wg_t", (128, NDC, NE))
    d_bq = din("bq", (128, NH))
    d_bk = din("bk", (128, NH))
    d_bv = din("bv", (128, NH))
    d_bo = din("bo", (128, NHC))
    d_fb = din("fbias", (1, NE))
    d_ck = din("rope_ck", (128, S))
    d_sk = din("rope_sk", (128, S))
    d_ps = din("pswap", (128, 128))
    d_id = din("ident", (128, 128))
    d_y = nc.dram_tensor("y", [NDC, 128, TL], F32, kind="ExternalOutput").ap()
    if debug:
        d_dbg_g = nc.dram_tensor("dbg_g", [NE, 128, NH * TL], BF16,
                                 kind="ExternalOutput").ap()
        d_dbg_acc = nc.dram_tensor("dbg_acc", [NE, 128, NHC * TL], F32,
                                   kind="ExternalOutput").ap()
        d_dbg_ewb = nc.dram_tensor("dbg_ewb", [NE, 128, TL], F32,
                                   kind="ExternalOutput").ap()
        d_dbg_ao = nc.dram_tensor("dbg_ao", [128, NH * TL], BF16,
                                  kind="ExternalOutput").ap()
        d_dbg_qkv = nc.dram_tensor("dbg_qkv", [3, 128, 2 * S], BF16,
                                   kind="ExternalOutput").ap()

    R1_bf = nc.alloc_sbuf_tensor("R1_bf", [128, NDC * S], BF16)
    r1_addr = nc.lookup_mloc(R1_bf).addr
    R1_f32 = nc.alloc_sbuf_tensor_at("R1_f32", [128, NDC * TL], F32,
                                     offset=r1_addr)
    aog = nc.alloc_sbuf_tensor("aog", [128, NH * TL], BF16)
    xy = nc.alloc_sbuf_tensor("xy", [128, NDC * TL], BF16)
    qbuf = nc.alloc_sbuf_tensor("qbuf", [128, 2 * TL], BF16)
    kbuf = nc.alloc_sbuf_tensor("kbuf", [128, 2 * S], BF16)
    vbuf = nc.alloc_sbuf_tensor("vbuf", [128, NDC * 256], BF16)
    wslab = nc.alloc_sbuf_tensor("wslab", [128, 12288], BF16)
    scr_bf = nc.alloc_sbuf_tensor("scr_bf", [128, 2048], BF16)
    scr_f = nc.alloc_sbuf_tensor("scr_f", [128, 2048], F32)
    ck_sb = nc.alloc_sbuf_tensor("ck_sb", [128, S], F32)
    sk_sb = nc.alloc_sbuf_tensor("sk_sb", [128, S], F32)
    id_sb = nc.alloc_sbuf_tensor("id_sb", [128, 128], F32)
    ps_sb = nc.alloc_sbuf_tensor("ps_sb", [128, 128], BF16)
    ones_c_bf = nc.alloc_sbuf_tensor("ones_c_bf", [128, 1], BF16)
    ones_r_bf = nc.alloc_sbuf_tensor("ones_r_bf", [1, 512], BF16)
    ones_c_f = nc.alloc_sbuf_tensor("ones_c_f", [128, 1], F32)
    ones_r_f = nc.alloc_sbuf_tensor("ones_r_f", [1, 128], F32)
    ones_sq = nc.alloc_sbuf_tensor("ones_sq", [128, 128], BF16)
    wg_sb = nc.alloc_sbuf_tensor("wg_sb", [128, NDC * NE], BF16)
    bqc = nc.alloc_sbuf_tensor("bqc", [128, NH], F32)
    bkc = nc.alloc_sbuf_tensor("bkc", [128, NH], F32)
    bvc = nc.alloc_sbuf_tensor("bvc", [128, NH], F32)
    boc = nc.alloc_sbuf_tensor("boc", [128, NHC], F32)
    msc = nc.alloc_sbuf_tensor("msc", [1, 8], F32)
    fbrow = nc.alloc_sbuf_tensor("fbrow", [1, NE], F32)
    fbb = nc.alloc_sbuf_tensor("fbb", [128, NE], F32)
    mq_sb = nc.alloc_sbuf_tensor("mq_sb", [128, 8], F32)
    rinv = nc.alloc_sbuf_tensor("rinv", [1, 512], BF16)
    rows_f = nc.alloc_sbuf_tensor("rows_f", [1, 512], F32)
    rt_sb = nc.alloc_sbuf_tensor("rt_sb", [128, 64], F32)
    wr_sb = nc.alloc_sbuf_tensor("wr_sb", [128, 64], F32)
    sc_sb = nc.alloc_sbuf_tensor("sc_sb", [128, 64], F32)
    s2_sb = nc.alloc_sbuf_tensor("s2_sb", [128, 64], F32)
    ew_sb = nc.alloc_sbuf_tensor("ew_sb", [128, 64], F32)
    m1_sb = nc.alloc_sbuf_tensor("m1_sb", [128, 8], F32)

    pb = [nc.alloc_psum_tensor(f"pb{i}", [128, 1024], F32) for i in range(4)]

    class _PA:
        def __init__(self, t, off):
            self.t, self.off = t, off

        def ap(self):
            return self.t.ap()[:, self.off:self.off + 512]

    pa = [_PA(pb[i // 2], (i % 2) * 512) for i in range(8)]
    pb_ = pb

    S_ = Sched()
    S_.batch_sems.update({"cst", "gld", "st", "dbg", "dbg5"})

    def T(fn, r=(), w=()):
        S_.op("tensor", fn, r, w)

    def V(fn, r=(), w=()):
        S_.op("vector", fn, r, w)

    def A(fn, r=(), w=()):
        S_.op("scalar", fn, r, w)

    def G(fn, r=(), w=(), dma=None):
        S_.op("gpsimd", fn, r, w, dma=dma)

    def Y(fn, r=(), w=(), dma=None):
        S_.op("sync", fn, r, w, dma=dma)

    R1k = [f"R1_{i}" for i in range(NDC)]
    XYk = [f"xy{i}" for i in range(NDC)]

    G(lambda g: g.memset(ones_c_bf.ap(), 1.0), w=["ones"])
    G(lambda g: g.memset(ones_r_bf.ap(), 1.0), w=["ones"])
    G(lambda g: g.memset(ones_c_f.ap(), 1.0), w=["ones"])
    G(lambda g: g.memset(ones_r_f.ap(), 1.0), w=["ones"])
    G(lambda g: g.memset(ones_sq.ap(), 1.0), w=["ones"])
    Y(lambda s: s.dma_start(ck_sb.ap(), d_ck), w=["ck"], dma="cst")
    Y(lambda s: s.dma_start(sk_sb.ap(), d_sk), w=["sk"], dma="cst")
    Y(lambda s: s.dma_start(id_sb.ap(), d_id), w=["id"], dma="cst")
    Y(lambda s: s.dma_start(fbrow.ap(), d_fb), w=["fbrow"], dma="cst")
    Y(lambda s: s.dma_start(bqc.ap(), d_bq), w=["bqc"], dma="cst")
    Y(lambda s: s.dma_start(bkc.ap(), d_bk), w=["bkc"], dma="cst")
    Y(lambda s: s.dma_start(bvc.ap(), d_bv), w=["bvc"], dma="cst")
    Y(lambda s: s.dma_start(boc.ap(), d_bo), w=["boc"], dma="cst")
    G(lambda g: g.dma_start(
        wslab.ap()[:, 8192:12288].rearrange("p (dc e) -> p dc e", dc=NDC),
        d_wv[0]), w=["ws4", "ws5"], dma="swv")
    for dc in range(NDC):
        G(lambda g, dc=dc: g.dma_start(
            R1_bf.ap()[:, dc * S:(dc + 1) * S], d_x[dc]),
            r=([f"R1_{dc - 2}"] if dc >= 2 else []),
            w=[f"R1_{dc}"], dma=f"gx{dc % 2}")
    G(lambda g: g.dma_start(
        wslab.ap()[:, 0:4096].rearrange("p (h dc e) -> p h dc e", h=2, dc=NDC),
        d_wq[0:2].rearrange("h p dc e -> p h dc e")),
        w=["ws0", "ws1"], dma="swq")
    G(lambda g: g.dma_start(ps_sb.ap(), d_ps), w=["pswap"], dma="gld")
    G(lambda g: g.dma_start(
        wslab.ap()[:, 4096:8192].rearrange("p (h dc e) -> p h dc e", h=2, dc=NDC),
        d_wk[0:2].rearrange("h p dc e -> p h dc e")),
        w=["ws2", "ws3"], dma="swk")
    G(lambda g: g.dma_start(
        wg_sb.ap().rearrange("p (dc e) -> p dc e", dc=NDC), d_wg),
        w=["wg"], dma="gld")
    T(lambda t: t.matmul(pa[0].ap()[:, 0:NE], ones_r_f.ap(), fbrow.ap(),
                         start=True, stop=True),
      r=["ones", "fbrow"], w=["pa0"])
    V(lambda v: v.tensor_copy(fbb.ap(), pa[0].ap()[:, 0:NE]),
      r=["pa0"], w=["fbb"])

    def emit_group(gi):
        if gi > 0:
            G(lambda g, gi=gi: g.dma_start(
                wslab.ap()[:, 8192:12288].rearrange("p (dc e) -> p dc e", dc=NDC),
                d_wv[gi]), w=["ws4", "ws5"], dma="swv")
            G(lambda g, gi=gi: g.dma_start(
                wslab.ap()[:, 0:4096].rearrange(
                    "p (h dc e) -> p h dc e", h=2, dc=NDC),
                d_wq[2 * gi:2 * gi + 2].rearrange("h p dc e -> p h dc e")),
                w=["ws0", "ws1"], dma="swq")
            G(lambda g, gi=gi: g.dma_start(
                wslab.ap()[:, 4096:8192].rearrange(
                    "p (h dc e) -> p h dc e", h=2, dc=NDC),
                d_wk[2 * gi:2 * gi + 2].rearrange("h p dc e -> p h dc e")),
                w=["ws2", "ws3"], dma="swk")

        if gi == 0:
            for dc in range(NDC):
                def vmm0(t, dc=dc):
                    last = None
                    for tc in range(NDC):
                        last = t.matmul(
                            pa[tc // 2].ap()[:, (tc % 2) * 256:
                                             (tc % 2) * 256 + 256],
                            R1_bf.ap()[:, dc * S + tc * 128:
                                       dc * S + tc * 128 + 128],
                            wslab.ap()[:, 8192 + dc * 256: 8192 + dc * 256 + 256],
                            start=(dc == 0 and tc % 2 == 0),
                            stop=(dc == NDC - 1),
                            skip_group_check=True)
                    return last
                T(vmm0, r=[f"R1_{dc}", "ws4" if dc < 8 else "ws5"],
                  w=[f"pa{b}" for b in range(8)])
            for tc in range(NDC):
                A(lambda s, tc=tc: s.copy(
                    vbuf.ap()[:, tc * 256: tc * 256 + 256],
                    pa[tc // 2].ap()[:, (tc % 2) * 256:(tc % 2) * 256 + 256]),
                    r=[f"pa{tc // 2}"], w=[f"vb{tc}"])
        else:
            for tc in range(NDC):
                bank = 2 + (tc % 2)
                pk = f"pa{bank}"

                def vmm(t, tc=tc, bank=bank):
                    last = None
                    for dc in range(NDC):
                        last = t.matmul(
                            pa[bank].ap()[:, 0:256],
                            R1_bf.ap()[:, dc * S + tc * 128: dc * S + tc * 128 + 128],
                            wslab.ap()[:, 8192 + dc * 256: 8192 + dc * 256 + 256],
                            start=(dc == 0), stop=(dc == NDC - 1))
                    return last
                T(vmm, r=R1k + ["ws4", "ws5"], w=[pk])
                A(lambda s, tc=tc, bank=bank: s.copy(
                    vbuf.ap()[:, tc * 256: tc * 256 + 256], pa[bank].ap()[:, 0:256]),
                    r=[pk], w=[f"vb{tc}"])

        for hl in range(2):
            def proj_rope(nt, wbase, wkeys, bcol, bckey, cos, sin, obuf, ooff,
                          okeys, hl=hl, hh=2 * gi + hl, finale=None):
                def emit_tail_pair(a):
                    for q2 in range(2):
                        T(lambda t, a=a, q2=q2: t.matmul(
                            pa[2 + q2].ap(), ps_sb.ap(),
                            scr_bf.ap()[:, 1024 + q2 * 512: 1024 + q2 * 512 + 512],
                            start=True, stop=True),
                            r=[f"sb{2 + q2}", "pswap"], w=[f"pa{2 + q2}"])
                    V(lambda v, a=a, sin=sin: v.tensor_tensor(
                        scr_f.ap()[:, 0:1024], pb_[1].ap(),
                        sin.ap()[:, a * 512: a * 512 + 1024], ALU.mult),
                        r=["pa2", "pa3", "sk"], w=["sf0", "sf1"])
                    V(lambda v, a=a, cos=cos: v.tensor_tensor(
                        scr_f.ap()[:, 1024:2048], scr_bf.ap()[:, 1024:2048],
                        cos.ap()[:, a * 512: a * 512 + 1024], ALU.mult),
                        r=["sb2", "sb3", "ck"], w=["sf2", "sf3"])
                    V(lambda v, a=a: v.tensor_tensor(
                        obuf.ap()[:, ooff + a * 512: ooff + a * 512 + 1024],
                        scr_f.ap()[:, 1024:2048], scr_f.ap()[:, 0:1024],
                        ALU.add),
                        r=["sf0", "sf1", "sf2", "sf3"],
                        w=[okeys[a], okeys[a + 1]])

                def emit_evac_pair(a, pbt, pbkeys):
                    A(lambda s, pbt=pbt: s.activation(
                        scr_bf.ap()[:, 1024:2048], pbt.ap(), AF.Identity,
                        bias=bcol.ap()[:, hh:hh + 1]),
                        r=pbkeys + [bckey], w=["sb2", "sb3"])
                    V(lambda v: v.tensor_tensor(
                        scr_bf.ap()[:, 0:1024], scr_bf.ap()[:, 1024:2048],
                        scr_bf.ap()[:, 1024:2048], ALU.mult),
                        r=["sb2", "sb3"], w=["sb0", "sb1"])
                    for q2 in range(2):
                        T(lambda t, q2=q2: t.matmul(
                            pa[4 + q2].ap(), ones_sq.ap(),
                            scr_bf.ap()[:, q2 * 512: q2 * 512 + 512],
                            start=True, stop=True),
                            r=[f"sb{q2}", "ones"], w=[f"pa{4 + q2}"])
                    if nt == 2:
                        mcol = 4
                    elif a == 0:
                        mcol = 5
                    else:
                        mcol = 0
                    V(lambda v, mcol=mcol: v.tensor_reduce(
                        msc.ap()[0:1, mcol:mcol + 1], pb_[2].ap()[0:1, :],
                        AX.X, ALU.max), r=["pa4", "pa5"], w=["msc"])
                    if nt == 4 and a == 2:
                        V(lambda v: v.tensor_tensor(
                            msc.ap()[0:1, 5:6], msc.ap()[0:1, 5:6],
                            msc.ap()[0:1, 0:1], ALU.max), r=["msc"], w=["msc"])

                prbanks = [(0, 1)] if nt == 2 else [(6, 7), (0, 1)]
                for pr in range(nt // 2):
                    a = 2 * pr
                    pb = prbanks[pr]
                    for dc in range(NDC):
                        def pmm(t, a=a, dc=dc, pb=pb):
                            t.matmul(
                                pa[pb[0]].ap(),
                                wslab.ap()[:, wbase + dc * 128: wbase + dc * 128 + 128],
                                R1_bf.ap()[:, dc * S + a * 512: dc * S + a * 512 + 512],
                                start=(dc == 0), stop=(dc == NDC - 1),
                                skip_group_check=True)
                            return t.matmul(
                                pa[pb[1]].ap(),
                                wslab.ap()[:, wbase + dc * 128: wbase + dc * 128 + 128],
                                R1_bf.ap()[:, dc * S + a * 512 + 512:
                                            dc * S + a * 512 + 1024],
                                start=(dc == 0), stop=(dc == NDC - 1),
                                skip_group_check=True)
                        T(pmm, r=[f"R1_{dc}"] + wkeys, w=[f"pa{pb[0]}", f"pa{pb[1]}"])
                    if pr >= 1:
                        emit_tail_pair(a - 2)
                    emit_evac_pair(a, pb_[pb[0] // 2],
                                   [f"pa{pb[0]}", f"pa{pb[1]}"])
                if finale is not None:
                    finale()
                emit_tail_pair(nt - 2)

            qoff = hl * TL
            koff = hl * S
            hh = 2 * gi + hl
            proj_rope(2, hl * 2048, ["ws0" if hl == 0 else "ws1"], bqc, "bqc",
                      ck_sb, sk_sb, qbuf, qoff, [f"qb{hl}_{i}" for i in range(2)])
            def bound_finale():
                V(lambda v: v.tensor_tensor(
                    msc.ap()[0:1, 6:7], msc.ap()[0:1, 4:5], msc.ap()[0:1, 5:6],
                    ALU.mult), r=["msc"], w=["msc"])
                A(lambda s: s.activation(msc.ap()[0:1, 7:8], msc.ap()[0:1, 6:7],
                                         AF.Ln, scale=SCALE * SCALE),
                  r=["msc"], w=["msc"])
                A(lambda s: s.activation(msc.ap()[0:1, 7:8], msc.ap()[0:1, 7:8],
                                         AF.Exp, scale=0.5),
                  r=["msc"], w=["msc"])
                T(lambda t: t.matmul(
                    pa[3].ap()[:, 0:1], ones_r_f.ap(), msc.ap()[0:1, 7:8],
                    start=True, stop=True), r=["msc", "ones"], w=["pa3"])
                V(lambda v: v.tensor_scalar(
                    mq_sb.ap()[:, 6:7], pa[3].ap()[:, 0:1], -1.0, None,
                    ALU.mult), r=["pa3"], w=["mq"])

            proj_rope(4, 4096 + hl * 2048, ["ws2" if hl == 0 else "ws3"], bkc,
                      "bkc", ck_sb, sk_sb, kbuf, koff,
                      [f"kb{hl}_{i}" for i in range(4)], finale=bound_finale)

            def st_merged():
                def sums_pvs(kc):
                    j = kc % 2
                    for qb in range(2):
                        T(lambda t, kc=kc, j=j, qb=qb: t.matmul(
                            pa[6 + qb].ap(), ones_sq.ap(),
                            scr_bf.ap()[:, j * 1024 + qb * 512:
                                        j * 1024 + qb * 512 + 512],
                            start=(kc == 0), stop=(kc == NDC - 1),
                            skip_group_check=True),
                            r=[f"sb{2 * j + qb}", "ones"], w=[f"pa{6 + qb}"])
                        T(lambda t, kc=kc, j=j, qb=qb, hl=hl: t.matmul(
                            pa[4 + qb].ap(),
                            vbuf.ap()[:, kc * 256 + hl * 128:
                                      kc * 256 + hl * 128 + 128],
                            scr_bf.ap()[:, j * 1024 + qb * 512:
                                        j * 1024 + qb * 512 + 512],
                            start=(kc == 0), stop=(kc == NDC - 1),
                            skip_group_check=True),
                            r=[f"sb{2 * j + qb}", f"vb{kc}"], w=[f"pa{4 + qb}"])

                for kc in range(NDC):
                    j = kc % 2
                    for qb in range(2):
                        T(lambda t, kc=kc, j=j, qb=qb, qoff=qoff, koff=koff:
                          t.matmul(
                            pa[2 * j + qb].ap(),
                            kbuf.ap()[:, koff + kc * 128: koff + kc * 128 + 128],
                            qbuf.ap()[:, qoff + qb * 512: qoff + qb * 512 + 512],
                            start=True, stop=True),
                            r=[f"kb{hl}_{kc // 4}", f"qb{hl}_{qb}"],
                            w=[f"pa{2 * j + qb}"])
                    A(lambda s, j=j: s.activation(
                        scr_bf.ap()[:, j * 1024:(j + 1) * 1024],
                        pb[j].ap(), AF.Exp, bias=mq_sb.ap()[:, 6:7],
                        scale=SCALE),
                        r=[f"pa{2 * j}", f"pa{2 * j + 1}", "mq"],
                        w=[f"sb{2 * j}", f"sb{2 * j + 1}"])
                    if kc >= 1:
                        sums_pvs(kc - 1)
                sums_pvs(NDC - 1)

            def epilogue_m():
                for qb in range(2):
                    A(lambda s, qb=qb: s.activation(
                        scr_f.ap()[:, qb * 512:(qb + 1) * 512],
                        pa[6 + qb].ap(), AF.Ln),
                        r=[f"pa{6 + qb}"], w=[f"sf{qb}"])
                    A(lambda s, qb=qb: s.activation(
                        scr_f.ap()[:, qb * 512:(qb + 1) * 512],
                        scr_f.ap()[:, qb * 512:(qb + 1) * 512], AF.Exp,
                        scale=-1.0), r=[f"sf{qb}"], w=[f"sf{qb}"])
                    V(lambda v, qb=qb, hh=hh: v.tensor_tensor(
                        aog.ap()[:, hh * TL + qb * 512: hh * TL + qb * 512 + 512],
                        pa[4 + qb].ap(), scr_f.ap()[:, qb * 512:(qb + 1) * 512],
                        ALU.mult), r=[f"pa{4 + qb}", f"sf{qb}"], w=[f"ao{hh}"])
                    A(lambda s, hh=hh, qb=qb: s.activation(
                        aog.ap()[:, hh * TL + qb * 512: hh * TL + qb * 512 + 512],
                        aog.ap()[:, hh * TL + qb * 512: hh * TL + qb * 512 + 512],
                        AF.Identity, bias=bvc.ap()[:, hh:hh + 1]),
                        r=[f"ao{hh}", "bvc"], w=[f"ao{hh}"])

            st_merged()
            epilogue_m()

    for gi in range(NH // 2):
        emit_group(gi)
        if debug and gi == 0:
            Y(lambda s: s.dma_start(d_dbg_qkv[0][:, 0:2 * TL], qbuf.ap()),
              r=["qb0_0", "qb0_1", "qb1_0", "qb1_1"], dma="dbg5")
            Y(lambda s: s.dma_start(d_dbg_qkv[1], kbuf.ap()),
              r=[f"kb{h}_{i}" for h in range(2) for i in range(4)], dma="dbg5")
            Y(lambda s: s.dma_start(d_dbg_qkv[2], vbuf.ap()),
              r=[f"vb{t}" for t in range(NDC)], dma="dbg5")

    for i in range(NDC):
        V(lambda v, i=i: v.tensor_copy(
            xy.ap()[:, i * TL:(i + 1) * TL], R1_bf.ap()[:, i * S: i * S + TL]),
            r=[f"R1_{i}"], w=[f"xy{i}"])

    if debug:
        Y(lambda s: s.dma_start(d_dbg_ao, aog.ap()),
          r=[f"ao{h}" for h in range(NH)], dma="dbg6")
    pending_stats = [None]
    for ec in range(NHC):
        sl = ec % 2
        wok = f"ws{sl}"
        G(lambda g, ec=ec, sl=sl: g.dma_start(
            wslab.ap()[:, sl * 2048:(sl + 1) * 2048].rearrange(
                "p (dc e) -> p dc e", dc=NDC), d_wo[ec]),
            w=[wok], dma=f"swo{sl}")
        ob = 0 if ec % 2 == 0 else 2
        for dc in range(NDC):
            def omm(t, ec=ec, dc=dc, sl=sl, ob=ob):
                for nq in range(2):
                    last = t.matmul(
                        pa[ob + nq].ap(),
                        wslab.ap()[:, sl * 2048 + dc * 128:
                                   sl * 2048 + dc * 128 + 128],
                        aog.ap()[:, dc * TL + nq * 512: dc * TL + nq * 512 + 512],
                        start=(dc == 0), stop=(dc == NDC - 1),
                        skip_group_check=True)
                return last
            T(omm, r=[f"ao{dc}", wok], w=[f"pa{ob}", f"pa{ob + 1}"])
        if pending_stats[0] is not None:
            pending_stats[0]()
        for nq in range(2):
            V(lambda v, ec=ec, nq=nq, ob=ob: v.scalar_tensor_tensor(
                R1_f32.ap()[:, ec * TL + nq * 512: ec * TL + nq * 512 + 512],
                pa[ob + nq].ap(), boc.ap()[:, ec:ec + 1],
                xy.ap()[:, ec * TL + nq * 512: ec * TL + nq * 512 + 512],
                ALU.add, ALU.add),
                r=[f"pa{ob + nq}", f"xy{ec}", "boc"], w=[f"R1_{ec}"])

        def emit_stats(ec=ec):
            sqs = ec % 2
            A(lambda s, ec=ec, sqs=sqs: s.activation(
                vbuf.ap()[:, sqs * 1024:(sqs + 1) * 1024],
                R1_f32.ap()[:, ec * TL:(ec + 1) * TL], AF.Square),
                r=[f"R1_{ec}"], w=[f"vb{t}" for t in range(4 * sqs, 4 * sqs + 4)])
            A(lambda s, ec=ec, sqs=sqs: s.activation(
                scr_bf.ap()[:, sqs * 1024:(sqs + 1) * 1024],
                R1_f32.ap()[:, ec * TL:(ec + 1) * TL], AF.Identity),
                r=[f"R1_{ec}"], w=[f"sb{2 * sqs}", f"sb{2 * sqs + 1}"])
            for nq in range(2):
                T(lambda t, ec=ec, nq=nq, sqs=sqs: t.matmul(
                    pa[4 + nq].ap(), ones_sq.ap(),
                    scr_bf.ap()[:, sqs * 1024 + nq * 512:
                                sqs * 1024 + nq * 512 + 512],
                    start=(ec == 0), stop=(ec == NHC - 1),
                    skip_group_check=True),
                    r=[f"sb{2 * sqs + nq}", "ones"], w=[f"pa{4 + nq}"])
                T(lambda t, ec=ec, nq=nq, sqs=sqs: t.matmul(
                    pa[6 + nq].ap(), ones_sq.ap(),
                    vbuf.ap()[:, sqs * 1024 + nq * 512:
                              sqs * 1024 + nq * 512 + 512],
                    start=(ec == 0), stop=(ec == NHC - 1),
                    skip_group_check=True),
                    r=[f"vb{4 * sqs + nq * 2}", f"vb{4 * sqs + nq * 2 + 1}",
                       "ones"], w=[f"pa{6 + nq}"])
        pending_stats[0] = emit_stats
    pending_stats[0]()

    def emit_ln(src_f32, srckeys, dst, dstkeys, inplace, mpb, vpb, mkeys, vkeys,
                bm=4, bv=6):
        V(lambda v: v.tensor_scalar(
            kbuf.ap()[0:1, 0:1024], mpb.ap()[0:1, :], 1.0 / D, None,
            ALU.mult), r=mkeys, w=["kb0_0", "kb0_1"])
        V(lambda v: v.tensor_tensor(
            kbuf.ap()[0:1, 1024:2048], kbuf.ap()[0:1, 0:1024],
            kbuf.ap()[0:1, 0:1024], ALU.mult),
            r=["kb0_0", "kb0_1"], w=["kb0_2", "kb0_3"])
        V(lambda v: v.tensor_scalar(
            scr_f.ap()[0:1, 1024:2048], vpb.ap()[0:1, :], 1.0 / D, EPS,
            ALU.mult, ALU.add), r=vkeys, w=["sf2", "sf3"])
        V(lambda v: v.tensor_tensor(
            scr_f.ap()[0:1, 1024:2048], scr_f.ap()[0:1, 1024:2048],
            kbuf.ap()[0:1, 1024:2048], ALU.subtract),
            r=["kb0_2", "kb0_3", "sf2", "sf3"], w=["sf2", "sf3"])
        A(lambda s: s.activation(
            scr_f.ap()[0:1, 1024:2048], scr_f.ap()[0:1, 1024:2048], AF.Ln),
            r=["sf2", "sf3"], w=["sf2", "sf3"])
        A(lambda s: s.activation(
            scr_f.ap()[0:1, 1024:2048], scr_f.ap()[0:1, 1024:2048], AF.Exp,
            scale=-0.5), r=["sf2", "sf3"], w=["sf2", "sf3"])
        for nq in range(2):
            T(lambda t, nq=nq: t.matmul(
                pa[bm + nq].ap(), ones_r_bf.ap()[0:1, 0:128],
                kbuf.ap()[0:1, nq * 512: nq * 512 + 512],
                start=True, stop=True),
                r=[f"kb0_{nq}", "ones"], w=[f"pa{bm + nq}"])
            T(lambda t, nq=nq: t.matmul(
                pa[bv + nq].ap(), ones_r_f.ap()[0:1, 0:128],
                scr_f.ap()[0:1, 1024 + nq * 512: 1024 + nq * 512 + 512],
                start=True, stop=True),
                r=[f"sf{2 + nq}", "ones"], w=[f"pa{bv + nq}"])
        for ec in range(NHC):
            vsl = ec % 2
            vk = [f"vb{t}" for t in range(4 * vsl, 4 * vsl + 4)]
            V(lambda v, ec=ec, vsl=vsl: v.tensor_tensor(
                vbuf.ap()[:, vsl * 1024:(vsl + 1) * 1024],
                src_f32.ap()[:, ec * TL:(ec + 1) * TL],
                pb[bm // 2].ap(), ALU.subtract),
                r=[srckeys[ec], f"pa{bm}", f"pa{bm + 1}"], w=vk)
            V(lambda v, ec=ec, vsl=vsl: v.tensor_tensor(
                dst.ap()[:, ec * TL:(ec + 1) * TL],
                vbuf.ap()[:, vsl * 1024:(vsl + 1) * 1024],
                pb[bv // 2].ap(), ALU.mult),
                r=vk + [f"pa{bv}", f"pa{bv + 1}"], w=[dstkeys[ec]])
            if inplace:
                Y(lambda s, ec=ec: s.dma_start(
                    d_y[ec], R1_f32.ap()[:, ec * TL:(ec + 1) * TL]),
                    r=[f"R1_{ec}"], dma="st")

    emit_ln(R1_f32, R1k, xy, XYk, False, pb[2], pb[3],
            ["pa4", "pa5"], ["pa6", "pa7"])

    for dc in range(NDC):
        def lmm(t, dc=dc):
            for nq in range(2):
                last = t.matmul(
                    pa[nq].ap()[0:8, :], wg_sb.ap()[:, dc * NE:(dc + 1) * NE],
                    xy.ap()[:, dc * TL + nq * 512: dc * TL + nq * 512 + 512],
                    start=(dc == 0), stop=(dc == NDC - 1),
                    skip_group_check=True)
            return last
        T(lmm, r=[f"xy{dc}", "wg"], w=["pa0", "pa1"])
    for nq in range(2):
        V(lambda v, nq=nq: v.tensor_copy(
            scr_f.ap()[0:8, nq * 512:(nq + 1) * 512], pa[nq].ap()[0:8, :]),
            r=[f"pa{nq}"], w=["sf0", "sf1"])
    for tt in range(8):
        T(lambda t, tt=tt: t.transpose(
            pa[2].ap()[:, 0:8], scr_f.ap()[0:8, tt * 128:(tt + 1) * 128],
            id_sb.ap()[0:8, 0:8]), r=["sf0", "sf1", "id"], w=["pa2"])
        V(lambda v, tt=tt: v.tensor_copy(
            rt_sb.ap()[:, tt * 8:(tt + 1) * 8], pa[2].ap()[:, 0:8]),
            r=["pa2"], w=["rt"])
    A(lambda s: s.activation(wr_sb.ap(), rt_sb.ap(), AF.Sigmoid),
      r=["rt"], w=["wr"])
    for tt in range(8):
        V(lambda v, tt=tt: v.tensor_tensor(
            sc_sb.ap()[:, tt * 8:(tt + 1) * 8], wr_sb.ap()[:, tt * 8:(tt + 1) * 8],
            fbb.ap(), ALU.add), r=["wr", "fbb"], w=["sc"])
    for tt in range(8):
        sl8 = slice(tt * 8, (tt + 1) * 8)
        V(lambda v, tt=tt: v.tensor_reduce(
            m1_sb.ap()[:, tt:tt + 1], sc_sb.ap()[:, tt * 8:(tt + 1) * 8], AX.X,
            ALU.max), r=["sc"], w=["m1"])
        V(lambda v, tt=tt: v.tensor_scalar(
            s2_sb.ap()[:, tt * 8:(tt + 1) * 8], sc_sb.ap()[:, tt * 8:(tt + 1) * 8],
            m1_sb.ap()[:, tt:tt + 1], None, ALU.is_ge), r=["sc", "m1"], w=["s2"])
        V(lambda v, tt=tt: v.tensor_scalar(
            s2_sb.ap()[:, tt * 8:(tt + 1) * 8], s2_sb.ap()[:, tt * 8:(tt + 1) * 8],
            -BIG, None, ALU.mult), r=["s2"], w=["s2"])
        V(lambda v, tt=tt: v.tensor_tensor(
            s2_sb.ap()[:, tt * 8:(tt + 1) * 8], sc_sb.ap()[:, tt * 8:(tt + 1) * 8],
            s2_sb.ap()[:, tt * 8:(tt + 1) * 8], ALU.add), r=["sc", "s2"], w=["s2"])
        V(lambda v, tt=tt: v.tensor_reduce(
            m1_sb.ap()[:, tt:tt + 1], s2_sb.ap()[:, tt * 8:(tt + 1) * 8], AX.X,
            ALU.max), r=["s2"], w=["m1"])
        V(lambda v, tt=tt: v.tensor_scalar(
            s2_sb.ap()[:, tt * 8:(tt + 1) * 8], sc_sb.ap()[:, tt * 8:(tt + 1) * 8],
            m1_sb.ap()[:, tt:tt + 1], None, ALU.is_ge), r=["sc", "m1"], w=["s2"])
        V(lambda v, tt=tt: v.tensor_tensor(
            ew_sb.ap()[:, tt * 8:(tt + 1) * 8], wr_sb.ap()[:, tt * 8:(tt + 1) * 8],
            s2_sb.ap()[:, tt * 8:(tt + 1) * 8], ALU.mult), r=["wr", "s2"], w=["ew"])
        V(lambda v, tt=tt: v.tensor_reduce(
            m1_sb.ap()[:, tt:tt + 1], ew_sb.ap()[:, tt * 8:(tt + 1) * 8], AX.X,
            ALU.add), r=["ew"], w=["m1"])
    V(lambda v: v.reciprocal(m1_sb.ap()[:, 0:8], m1_sb.ap()[:, 0:8]),
      r=["m1"], w=["m1"])
    for tt in range(8):
        V(lambda v, tt=tt: v.tensor_scalar(
            ew_sb.ap()[:, tt * 8:(tt + 1) * 8], ew_sb.ap()[:, tt * 8:(tt + 1) * 8],
            m1_sb.ap()[:, tt:tt + 1], None, ALU.mult), r=["ew", "m1"], w=["ew"])

    for hc in range(NHC):
        V(lambda v, hc=hc: v.tensor_copy(
            R1_f32.ap()[:, hc * TL:(hc + 1) * TL],
            xy.ap()[:, hc * TL:(hc + 1) * TL]), r=[f"xy{hc}"], w=[f"R1_{hc}"])

    W1OFF = [0, 2048]
    W3OFF = [4096, 6144]
    W2OFF = [8192, 9216]
    W1KEY = ["ws0", "ws1"]
    W3KEY = ["ws2", "ws3"]
    W2KEY = ["ws4a", "ws4b"]

    w13_q = [(e, fc) for e in range(NE) for fc in range(NFC)]
    w2_q = [(e, hc) for e in range(NE) for hc in range(NHC)]
    w13_pos = [0]
    w2_pos = [0]

    def pump_w13(n=1):
        for _ in range(n):
            if w13_pos[0] >= len(w13_q):
                return
            e, fc = w13_q[w13_pos[0]]
            w13_pos[0] += 1
            s1 = fc % 2
            G(lambda g, e=e, fc=fc, s1=s1: g.dma_start(
                wslab.ap()[:, W1OFF[s1]:W1OFF[s1] + 2048].rearrange(
                    "p (dc f) -> p dc f", dc=NDC), d_w1[e, fc]),
                w=[W1KEY[s1]], dma=f"sw1_{s1}")
            G(lambda g, e=e, fc=fc, s1=s1: g.dma_start(
                wslab.ap()[:, W3OFF[s1]:W3OFF[s1] + 2048].rearrange(
                    "p (dc f) -> p dc f", dc=NDC), d_w3[e, fc]),
                w=[W3KEY[s1]], dma=f"sw3_{s1}")

    def pump_w2(n=1):
        for _ in range(n):
            if w2_pos[0] >= len(w2_q):
                return
            e, hc = w2_q[w2_pos[0]]
            w2_pos[0] += 1
            s2 = hc % 2
            G(lambda g, e=e, hc=hc, s2=s2: g.dma_start(
                wslab.ap()[:, W2OFF[s2]:W2OFF[s2] + 1024].rearrange(
                    "p (fc h) -> p fc h", fc=NFC), d_w2[e, hc]),
                w=[W2KEY[s2]], dma=f"sw2_{s2}")

    def emit_ew_rows(e):
        sl = e % 2
        for tt in range(8):
            bank = 2 + tt % 2
            T(lambda t, e=e, tt=tt, bank=bank: t.transpose(
                pa[bank].ap()[0:1, 0:128],
                ew_sb.ap()[:, tt * 8 + e: tt * 8 + e + 1], id_sb.ap()),
                r=["ew", "id"], w=[f"pa{bank}"])
            V(lambda v, tt=tt, bank=bank, sl=sl: v.tensor_copy(
                kbuf.ap()[0:1, sl * 1024 + tt * 128: sl * 1024 + tt * 128 + 128],
                pa[bank].ap()[0:1, 0:128]),
                r=[f"pa{bank}"], w=[f"kb0_{2 * sl + tt // 4}"])
        for nq in range(2):
            bank = 2 + nq
            T(lambda t, nq=nq, sl=sl, bank=bank: t.matmul(
                pa[bank].ap(), ones_r_bf.ap()[0:1, 0:128],
                kbuf.ap()[0:1, sl * 1024 + nq * 512: sl * 1024 + nq * 512 + 512],
                start=True, stop=True),
                r=[f"kb0_{2 * sl + nq}", "ones"], w=[f"pa{bank}"])
            V(lambda v, nq=nq, sl=sl, bank=bank: v.tensor_copy(
                scr_f.ap()[:, sl * 1024 + nq * 512: sl * 1024 + nq * 512 + 512],
                pa[bank].ap()), r=[f"pa{bank}"], w=[f"sf{2 * sl + nq}"])

    def emit_h13(e, hook=None):
        sl = e % 2
        for fc in range(NFC):
            s1 = fc % 2
            for dc in range(NDC):
                def h1mm(t, dc=dc, s1=s1):
                    for nc2 in range(2):
                        last = t.matmul(
                            pa[nc2].ap(),
                            wslab.ap()[:, W1OFF[s1] + dc * 128:
                                       W1OFF[s1] + dc * 128 + 128],
                            xy.ap()[:, dc * TL + nc2 * 512:
                                    dc * TL + nc2 * 512 + 512],
                            start=(dc == 0), stop=(dc == NDC - 1),
                            skip_group_check=True)
                    return last
                T(h1mm, r=[f"xy{dc}", W1KEY[s1]], w=["pa0", "pa1"])
            for nc2 in range(2):
                A(lambda s, nc2=nc2: s.activation(
                    scr_bf.ap()[:, nc2 * 512:(nc2 + 1) * 512], pa[nc2].ap(),
                    AF.Silu), r=[f"pa{nc2}"], w=[f"sb{nc2}"])
            for dc in range(NDC):
                def h3mm(t, dc=dc, s1=s1):
                    for nc2 in range(2):
                        last = t.matmul(
                            pa[4 + nc2].ap(),
                            wslab.ap()[:, W3OFF[s1] + dc * 128:
                                       W3OFF[s1] + dc * 128 + 128],
                            xy.ap()[:, dc * TL + nc2 * 512:
                                    dc * TL + nc2 * 512 + 512],
                            start=(dc == 0), stop=(dc == NDC - 1),
                            skip_group_check=True)
                    return last
                T(h3mm, r=[f"xy{dc}", W3KEY[s1]], w=["pa4", "pa5"])
                if hook is not None and fc == 0 and dc == NDC - 1:
                    hook()
                    hook = None
            for nc2 in range(2):
                bsl = 2 + nc2
                V(lambda v, bsl=bsl, sl=sl, nc2=nc2: v.tensor_tensor(
                    scr_bf.ap()[:, bsl * 512:(bsl + 1) * 512], pa[4 + nc2].ap(),
                    scr_f.ap()[:, sl * 1024 + nc2 * 512: sl * 1024 + nc2 * 512 + 512],
                    ALU.mult), r=[f"pa{4 + nc2}", f"sf{2 * sl + nc2}"], w=[f"sb{bsl}"])
                V(lambda v, fc=fc, nc2=nc2, bsl=bsl: v.tensor_tensor(
                    aog.ap()[:, fc * TL + nc2 * 512: fc * TL + nc2 * 512 + 512],
                    scr_bf.ap()[:, nc2 * 512:(nc2 + 1) * 512],
                    scr_bf.ap()[:, bsl * 512:(bsl + 1) * 512], ALU.mult),
                    r=[f"sb{nc2}", f"sb{bsl}"], w=[f"ao{fc}"])
            pump_w13(1)

    def emit_ln2_stats(ec):
        sqs = ec % 2
        A(lambda s, ec=ec, sqs=sqs: s.activation(
            vbuf.ap()[:, sqs * 1024:(sqs + 1) * 1024],
            R1_f32.ap()[:, ec * TL:(ec + 1) * TL], AF.Square),
            r=[f"R1_{ec}"], w=[f"vb{t}" for t in range(4 * sqs, 4 * sqs + 4)])
        A(lambda s, ec=ec, sqs=sqs: s.activation(
            scr_bf.ap()[:, sqs * 1024:(sqs + 1) * 1024],
            R1_f32.ap()[:, ec * TL:(ec + 1) * TL], AF.Identity),
            r=[f"R1_{ec}"], w=[f"sb{2 * sqs}", f"sb{2 * sqs + 1}"])
        for nq in range(2):
            T(lambda t, ec=ec, nq=nq, sqs=sqs: t.matmul(
                pa[0 + nq].ap(), ones_sq.ap(),
                scr_bf.ap()[:, sqs * 1024 + nq * 512: sqs * 1024 + nq * 512 + 512],
                start=(ec == 0), stop=(ec == NHC - 1), skip_group_check=True),
                r=[f"sb{2 * sqs + nq}", "ones"], w=[f"pa{0 + nq}"])
            T(lambda t, ec=ec, nq=nq, sqs=sqs: t.matmul(
                pa[4 + nq].ap(), ones_sq.ap(),
                vbuf.ap()[:, sqs * 1024 + nq * 512: sqs * 1024 + nq * 512 + 512],
                start=(ec == 0), stop=(ec == NHC - 1), skip_group_check=True),
                r=[f"vb{4 * sqs + nq * 2}", f"vb{4 * sqs + nq * 2 + 1}", "ones"],
                w=[f"pa{4 + nq}"])

    def emit_w2(e, stats=False):
        pend = [None]
        for hc in range(NHC):
            s2 = hc % 2
            wb = 6 if hc % 2 == 0 else 2
            for fc in range(NFC):
                def w2mm(t, hc=hc, fc=fc, s2=s2, wb=wb):
                    for nc2 in range(2):
                        last = t.matmul(
                            pa[wb + nc2].ap(),
                            wslab.ap()[:, W2OFF[s2] + fc * 128:
                                       W2OFF[s2] + fc * 128 + 128],
                            aog.ap()[:, fc * TL + nc2 * 512: fc * TL + nc2 * 512 + 512],
                            start=(fc == 0), stop=(fc == NFC - 1),
                            skip_group_check=True)
                    return last
                T(w2mm, r=[f"ao{fc}", W2KEY[s2]], w=[f"pa{wb}", f"pa{wb + 1}"])
            if pend[0] is not None:
                pend[0]()
                pend[0] = None
            for nc2 in range(2):
                V(lambda v, hc=hc, nc2=nc2, wb=wb: v.tensor_tensor(
                    R1_f32.ap()[:, hc * TL + nc2 * 512: hc * TL + nc2 * 512 + 512],
                    pa[wb + nc2].ap(),
                    R1_f32.ap()[:, hc * TL + nc2 * 512: hc * TL + nc2 * 512 + 512],
                    ALU.add), r=[f"pa{wb + nc2}", f"R1_{hc}"], w=[f"R1_{hc}"])
            if stats:
                pend[0] = lambda hc=hc: emit_ln2_stats(hc)
            pump_w2(1)
        if pend[0] is not None:
            pend[0]()

    pump_w13(2)
    pump_w2(2)
    for e in range(NE):
        emit_h13(e, hook=lambda e=e: emit_ew_rows(e))
        emit_w2(e, stats=(e == NE - 1))
        if debug:
            Y(lambda s, e=e: s.dma_start(d_dbg_acc[e], R1_f32.ap()),
              r=[f"R1_{i}" for i in range(NHC)], dma="dbg2")
            Y(lambda s, e=e: s.dma_start(d_dbg_g[e], aog.ap()),
              r=[f"ao{j}" for j in range(NH)], dma="dbg3")
            Y(lambda s, e=e, sl=e % 2: s.dma_start(
                d_dbg_ewb[e], scr_f.ap()[:, sl * TL:(sl + 1) * TL]),
                r=[f"sf{2 * (e % 2)}", f"sf{2 * (e % 2) + 1}"], dma=f"dbg4{e % 2}")

    emit_ln(R1_f32, R1k, R1_f32, R1k, True, pb[0], pb[2],
            ["pa0", "pa1"], ["pa4", "pa5"], bm=6, bv=2)

    build_program.last_sched = S_
    handles = S_.materialize(nc)
    st_total = S_.totals.get(("dma", "st"), 0)
    import contextlib
    with nc.allow_low_precision(
            reason="bf16 intermediates by design; fp32 accumulation in PSUM, "
                   "graded against the fp64 sim"), nc.Block() as block:
        @block.tensor
        def _(t):
            S_.replay("tensor", t)

        @block.vector
        def _(v):
            S_.replay("vector", v)

        @block.scalar
        def _(s):
            S_.replay("scalar", s)

        @block.gpsimd
        def _(g):
            S_.replay("gpsimd", g)

        @block.sync
        def _(s):
            S_.replay("sync", s)
            s.wait_ge(handles[("dma", "st")], st_total)
    return nc


_NC_CACHE = {}


def _get_nc():
    if "nc" not in _NC_CACHE:
        _NC_CACHE["nc"] = build_program()
    return _NC_CACHE["nc"]


def _host_inputs(inputs):
    f = np.float32
    x = np.asarray(inputs["L0_attn_x"], f)
    Wq = np.asarray(inputs["L0_attn_Wq"], f)
    Wk = np.asarray(inputs["L0_attn_Wk"], f)
    Wv = np.asarray(inputs["L0_attn_Wv"], f)
    Wo = np.asarray(inputs["L0_attn_Wo"], f)
    Wg = np.asarray(inputs["L0_ffn_Wg"], f)
    W1 = np.asarray(inputs["L0_ffn_W1"], f)
    W2 = np.asarray(inputs["L0_ffn_W2"], f)
    W3 = np.asarray(inputs["L0_ffn_W3"], f)

    def att_t(W):
        return np.ascontiguousarray(
            W.reshape(NH, 128, NDC, 128).transpose(0, 3, 2, 1))

    shared = {
        "wq_t": att_t(Wq), "wk_t": att_t(Wk), "wo_t": att_t(Wo),
        "wv_t": np.ascontiguousarray(
            Wv.reshape(NH // 2, 256, NDC, 128).transpose(0, 3, 2, 1)),
        "w1_t": np.ascontiguousarray(
            W1.reshape(NE, NFC, 128, NDC, 128).transpose(0, 1, 4, 3, 2)),
        "w3_t": np.ascontiguousarray(
            W3.reshape(NE, NFC, 128, NDC, 128).transpose(0, 1, 4, 3, 2)),
        "w2_t": np.ascontiguousarray(
            W2.reshape(NE, NHC, 128, NFC, 128).transpose(0, 1, 4, 3, 2)),
        "wg_t": np.ascontiguousarray(Wg.T.reshape(NDC, 128, NE).transpose(1, 0, 2)),
        "bq": np.ascontiguousarray(np.asarray(inputs["L0_attn_bq"], f).reshape(NH, 128).T),
        "bk": np.ascontiguousarray(np.asarray(inputs["L0_attn_bk"], f).reshape(NH, 128).T),
        "bv": np.ascontiguousarray(np.asarray(inputs["L0_attn_bv"], f).reshape(NH, 128).T),
        "bo": np.ascontiguousarray(np.asarray(inputs["L0_attn_bo"], f).reshape(NHC, 128).T),
        "fbias": np.asarray(inputs["L0_ffn_bias"], f).reshape(1, NE),
    }
    pswap = np.zeros((128, 128), f)
    pswap[np.arange(128), np.arange(128) ^ 1] = 1.0
    shared["pswap"] = pswap
    shared["ident"] = np.eye(128, dtype=f)

    f32 = np.float32
    freqs = np.array(_ROPE_FREQ_BITS, dtype=np.uint32).view(f32)
    pos = np.arange(S, dtype=f32)
    ang = pos[:, None] * freqs[None, :]
    cosT = np.cos(ang)
    sinT = np.sin(ang)
    cos = np.repeat(cosT.T, 2, axis=0)
    sin = sinT.T
    ssgn = np.empty((128, S), f32)
    ssgn[0::2] = -sin
    ssgn[1::2] = sin

    in_maps = []
    for c in range(8):
        b = c // 2
        half = c % 2
        own = np.arange(half * TL, (half + 1) * TL)
        other = np.arange((1 - half) * TL, (2 - half) * TL)
        perm = np.concatenate([own, other])
        m = dict(shared)
        m["x_t"] = np.ascontiguousarray(x[b].T[:, perm]).reshape(NDC, 128, S)
        m["rope_ck"] = np.ascontiguousarray(cos[:, perm]).astype(f)
        m["rope_sk"] = np.ascontiguousarray(ssgn[:, perm]).astype(f)
        in_maps.append(m)
    return in_maps


def kernel(**inputs):
    in_maps = _host_inputs(inputs)
    nc = _get_nc()
    res = run_bass_kernel_spmd(nc, in_maps, core_ids=list(range(8)))
    y = np.empty((4, S, D), np.float32)
    for c in range(8):
        o = np.asarray(res.results[c]["y"])
        y[c // 2, (c % 2) * TL:(c % 2 + 1) * TL, :] = \
            o.transpose(2, 0, 1).reshape(TL, D)
    return y


# version 22
# speedup vs baseline: 1.0569x; 1.0232x over previous
import numpy as np
import concourse.bass as bass
import concourse.mybir as mybir
from concourse.bass_utils import run_bass_kernel_spmd

F32 = mybir.dt.float32
BF16 = mybir.dt.bfloat16
AF = mybir.ActivationFunctionType
ALU = mybir.AluOpType
AX = mybir.AxisListType

D = 2048
NDC = 16
NH = 16
HD = 128
S = 2048
TL = 1024
NE = 8
FFN = 1024
NFC = 8
NHC = 16
SCALE = float(HD) ** -0.5
EPS = 1e-5
BIG = 1e30

ENGS = ("tensor", "vector", "scalar", "gpsimd", "sync")

_ROPE_FREQ_BITS = [
    0x3f800000, 0x3f5dafd7, 0x3f3ff911, 0x3f263de0, 0x3f0ff59a, 0x3ef953cf, 0x3ed7e89b, 0x3ebaf81b,
    0x3ea1e89b, 0x3e8c3504, 0x3e72d423, 0x3e5247ed, 0x3e361887, 0x3e1db040, 0x3e088d77, 0x3dec7fd6,
    0x3dcccccd, 0x3db15978, 0x3d99940d, 0x3d84fe4d, 0x3d6655c2, 0x3d47763f, 0x3d2cba15, 0x3d159348,
    0x3d0186e3, 0x3ce054d2, 0x3cc2434f, 0x3ca8398b, 0x3c91ad39, 0x3c7c4d33, 0x3c5a7bf2, 0x3c3d3311,
    0x3c23d70a, 0x3c0de12d, 0x3bf5b9b0, 0x3bd4ca15, 0x3bb8449c, 0x3b9f91cc, 0x3b8a2e77, 0x3b6f520e,
    0x3b4f3e38, 0x3b33770f, 0x3b1b690d, 0x3b06946f, 0x3ae91528, 0x3ac9d75c, 0x3aaec98e, 0x3a975c0e,
    0x3a83126f, 0x3a6301e2, 0x3a44948c, 0x3a2a3b44, 0x3a136a16, 0x39ff4fac, 0x39dd1725, 0x39bf74d7,
    0x39a5cb60, 0x398f9272, 0x3978a815, 0x395753e4, 0x393a7753, 0x39217916, 0x390bd472, 0x38f22ce2,
]


class Sched:

    def __init__(self):
        self.items = []
        self.stream = {e: [] for e in ENGS}
        self.lastw = {}
        self.readers = {}
        self.batch_sems = set()

    def op(self, eng, fn, r=(), w=(), dma=None):
        o = {"eng": eng, "fn": fn, "dma": dma, "idx": len(self.items),
             "marked": False, "cnt": None,
             "sem": ("dma", dma) if dma else ("eng", eng)}
        deps = {}

        def add_dep(d):
            if d is None or d is o:
                return
            if (eng == "tensor" and d["eng"] == "tensor"
                    and d["dma"] is None and dma is None):
                return
            if dma is not None and d["dma"] == dma and dma in self.batch_sems:
                return
            k = d["sem"]
            prev = deps.get(k)
            if prev is None or d["idx"] > prev["idx"]:
                deps[k] = d

        for res in r:
            add_dep(self.lastw.get(res))
        for res in w:
            add_dep(self.lastw.get(res))
            for dd in self.readers.get(res, {}).values():
                add_dep(dd)
        o["deps"] = list(deps.values())
        for res in r:
            self.readers.setdefault(res, {})[o["sem"]] = o
        for res in w:
            self.lastw[res] = o
            self.readers[res] = {}
        self.items.append(o)
        self.stream[eng].append(o)
        return o

    def materialize(self, nc):
        for o in self.items:
            if o["dma"] in ("st", "dbg", "dbg2", "dbg3", "dbg40", "dbg41", "dbg5", "dbg6"):
                o["marked"] = True
            for d in o["deps"]:
                d["marked"] = True
        totals = {}
        for o in self.items:
            if o["marked"]:
                amt = 16 if o["dma"] else 1
                totals[o["sem"]] = totals.get(o["sem"], 0) + amt
                o["cnt"] = totals[o["sem"]]
        self.totals = totals
        self.handles = {}
        for skey in totals:
            name = skey[1] if skey[0] == "dma" else "e_" + skey[1]
            self.handles[skey] = nc.alloc_semaphore(name)
        return self.handles

    def replay(self, engname, engobj):
        wm = {}
        for o in self.stream[engname]:
            waits = {}
            for d in o["deps"]:
                if d["cnt"] is None:
                    continue
                val = (self.totals[d["sem"]]
                       if (d["dma"] and d["dma"] in self.batch_sems)
                       else d["cnt"])
                if waits.get(d["sem"], 0) < val:
                    waits[d["sem"]] = val
            wl = [(k, v) for k, v in waits.items() if wm.get(k, 0) < v]
            for k, v in wl:
                engobj.wait_ge(self.handles[k], v)
                wm[k] = v
            inst = o["fn"](engobj)
            if o["marked"]:
                inst.then_inc(self.handles[o["sem"]], 16 if o["dma"] else 1)


def build_program(debug=False):
    nc = bass.Bass("TRN2", target_bir_lowering=False)

    def din(name, shape):
        return nc.dram_tensor(name, list(shape), F32, kind="ExternalInput").ap()

    d_x = din("x_t", (NDC, 128, S))
    d_wq = din("wq_t", (NH, 128, NDC, 128))
    d_wk = din("wk_t", (NH, 128, NDC, 128))
    d_wv = din("wv_t", (NH // 2, 128, NDC, 256))
    d_wo = din("wo_t", (NHC, 128, NDC, 128))
    d_w1 = din("w1_t", (NE, NFC, 128, NDC, 128))
    d_w3 = din("w3_t", (NE, NFC, 128, NDC, 128))
    d_w2 = din("w2_t", (NE, NHC, 128, NFC, 128))
    d_wg = din("wg_t", (128, NDC, NE))
    d_bq = din("bq", (128, NH))
    d_bk = din("bk", (128, NH))
    d_bv = din("bv", (128, NH))
    d_bo = din("bo", (128, NHC))
    d_fb = din("fbias", (1, NE))
    d_ck = din("rope_ck", (128, S))
    d_sk = din("rope_sk", (128, S))
    d_ps = din("pswap", (128, 128))
    d_id = din("ident", (128, 128))
    d_y = nc.dram_tensor("y", [NDC, 128, TL], F32, kind="ExternalOutput").ap()
    if debug:
        d_dbg_g = nc.dram_tensor("dbg_g", [NE, 128, NH * TL], BF16,
                                 kind="ExternalOutput").ap()
        d_dbg_acc = nc.dram_tensor("dbg_acc", [NE, 128, NHC * TL], F32,
                                   kind="ExternalOutput").ap()
        d_dbg_ewb = nc.dram_tensor("dbg_ewb", [NE, 128, TL], F32,
                                   kind="ExternalOutput").ap()
        d_dbg_ao = nc.dram_tensor("dbg_ao", [128, NH * TL], BF16,
                                  kind="ExternalOutput").ap()
        d_dbg_qkv = nc.dram_tensor("dbg_qkv", [3, 128, 2 * S], BF16,
                                   kind="ExternalOutput").ap()

    R1_bf = nc.alloc_sbuf_tensor("R1_bf", [128, NDC * S], BF16)
    r1_addr = nc.lookup_mloc(R1_bf).addr
    R1_f32 = nc.alloc_sbuf_tensor_at("R1_f32", [128, NDC * TL], F32,
                                     offset=r1_addr)
    aog = nc.alloc_sbuf_tensor("aog", [128, NH * TL], BF16)
    xy = nc.alloc_sbuf_tensor("xy", [128, NDC * TL], BF16)
    qbuf = nc.alloc_sbuf_tensor("qbuf", [128, 2 * TL], BF16)
    kbuf = nc.alloc_sbuf_tensor("kbuf", [128, 2 * S], BF16)
    vbuf = nc.alloc_sbuf_tensor("vbuf", [128, NDC * 256], BF16)
    wslab = nc.alloc_sbuf_tensor("wslab", [128, 12288], BF16)
    scr_bf = nc.alloc_sbuf_tensor("scr_bf", [128, 2048], BF16)
    scr_f = nc.alloc_sbuf_tensor("scr_f", [128, 2048], F32)
    ck_sb = nc.alloc_sbuf_tensor("ck_sb", [128, S], F32)
    sk_sb = nc.alloc_sbuf_tensor("sk_sb", [128, S], F32)
    id_sb = nc.alloc_sbuf_tensor("id_sb", [128, 128], F32)
    ps_sb = nc.alloc_sbuf_tensor("ps_sb", [128, 128], BF16)
    ones_c_bf = nc.alloc_sbuf_tensor("ones_c_bf", [128, 1], BF16)
    ones_r_bf = nc.alloc_sbuf_tensor("ones_r_bf", [1, 512], BF16)
    ones_c_f = nc.alloc_sbuf_tensor("ones_c_f", [128, 1], F32)
    ones_r_f = nc.alloc_sbuf_tensor("ones_r_f", [1, 128], F32)
    ones_sq = nc.alloc_sbuf_tensor("ones_sq", [128, 128], BF16)
    wg_sb = nc.alloc_sbuf_tensor("wg_sb", [128, NDC * NE], BF16)
    bqc = nc.alloc_sbuf_tensor("bqc", [128, NH], F32)
    bkc = nc.alloc_sbuf_tensor("bkc", [128, NH], F32)
    bvc = nc.alloc_sbuf_tensor("bvc", [128, NH], F32)
    boc = nc.alloc_sbuf_tensor("boc", [128, NHC], F32)
    msc = nc.alloc_sbuf_tensor("msc", [1, 8], F32)
    fbrow = nc.alloc_sbuf_tensor("fbrow", [1, NE], F32)
    fbb = nc.alloc_sbuf_tensor("fbb", [128, NE], F32)
    mq_sb = nc.alloc_sbuf_tensor("mq_sb", [128, 8], F32)
    rinv = nc.alloc_sbuf_tensor("rinv", [1, 512], BF16)
    rows_f = nc.alloc_sbuf_tensor("rows_f", [1, 512], F32)
    rt_sb = nc.alloc_sbuf_tensor("rt_sb", [128, 64], F32)
    wr_sb = nc.alloc_sbuf_tensor("wr_sb", [128, 64], F32)
    sc_sb = nc.alloc_sbuf_tensor("sc_sb", [128, 64], F32)
    s2_sb = nc.alloc_sbuf_tensor("s2_sb", [128, 64], F32)
    ew_sb = nc.alloc_sbuf_tensor("ew_sb", [128, 64], F32)
    m1_sb = nc.alloc_sbuf_tensor("m1_sb", [128, 8], F32)

    pb = [nc.alloc_psum_tensor(f"pb{i}", [128, 1024], F32) for i in range(4)]

    class _PA:
        def __init__(self, t, off):
            self.t, self.off = t, off

        def ap(self):
            return self.t.ap()[:, self.off:self.off + 512]

    pa = [_PA(pb[i // 2], (i % 2) * 512) for i in range(8)]
    pb_ = pb

    S_ = Sched()
    S_.batch_sems.update({"cst", "gld", "st", "dbg", "dbg5"})

    def T(fn, r=(), w=()):
        S_.op("tensor", fn, r, w)

    def V(fn, r=(), w=()):
        S_.op("vector", fn, r, w)

    def A(fn, r=(), w=()):
        S_.op("scalar", fn, r, w)

    def G(fn, r=(), w=(), dma=None):
        S_.op("gpsimd", fn, r, w, dma=dma)

    def Y(fn, r=(), w=(), dma=None):
        S_.op("sync", fn, r, w, dma=dma)

    R1k = [f"R1_{i}" for i in range(NDC)]
    XYk = [f"xy{i}" for i in range(NDC)]

    G(lambda g: g.memset(ones_c_bf.ap(), 1.0), w=["ones"])
    G(lambda g: g.memset(ones_r_bf.ap(), 1.0), w=["ones"])
    G(lambda g: g.memset(ones_c_f.ap(), 1.0), w=["ones"])
    G(lambda g: g.memset(ones_r_f.ap(), 1.0), w=["ones"])
    G(lambda g: g.memset(ones_sq.ap(), 1.0), w=["ones"])
    Y(lambda s: s.dma_start(ck_sb.ap(), d_ck), w=["ck"], dma="cst")
    Y(lambda s: s.dma_start(sk_sb.ap(), d_sk), w=["sk"], dma="cst")
    Y(lambda s: s.dma_start(id_sb.ap(), d_id), w=["id"], dma="cst")
    Y(lambda s: s.dma_start(fbrow.ap(), d_fb), w=["fbrow"], dma="cst")
    Y(lambda s: s.dma_start(bqc.ap(), d_bq), w=["bqc"], dma="cst")
    Y(lambda s: s.dma_start(bkc.ap(), d_bk), w=["bkc"], dma="cst")
    Y(lambda s: s.dma_start(bvc.ap(), d_bv), w=["bvc"], dma="cst")
    Y(lambda s: s.dma_start(boc.ap(), d_bo), w=["boc"], dma="cst")
    G(lambda g: g.dma_start(
        wslab.ap()[:, 8192:12288].rearrange("p (dc e) -> p dc e", dc=NDC),
        d_wv[0]), w=["ws4", "ws5"], dma="swv")
    for dc in range(NDC):
        G(lambda g, dc=dc: g.dma_start(
            R1_bf.ap()[:, dc * S:(dc + 1) * S], d_x[dc]),
            r=([f"R1_{dc - 2}"] if dc >= 2 else []),
            w=[f"R1_{dc}"], dma=f"gx{dc % 2}")
    G(lambda g: g.dma_start(
        wslab.ap()[:, 0:4096].rearrange("p (h dc e) -> p h dc e", h=2, dc=NDC),
        d_wq[0:2].rearrange("h p dc e -> p h dc e")),
        w=["ws0", "ws1"], dma="swq")
    G(lambda g: g.dma_start(ps_sb.ap(), d_ps), w=["pswap"], dma="gld")
    G(lambda g: g.dma_start(
        wslab.ap()[:, 4096:8192].rearrange("p (h dc e) -> p h dc e", h=2, dc=NDC),
        d_wk[0:2].rearrange("h p dc e -> p h dc e")),
        w=["ws2", "ws3"], dma="swk")
    G(lambda g: g.dma_start(
        wg_sb.ap().rearrange("p (dc e) -> p dc e", dc=NDC), d_wg),
        w=["wg"], dma="gld")
    T(lambda t: t.matmul(pa[0].ap()[:, 0:NE], ones_r_f.ap(), fbrow.ap(),
                         start=True, stop=True),
      r=["ones", "fbrow"], w=["pa0"])
    V(lambda v: v.tensor_copy(fbb.ap(), pa[0].ap()[:, 0:NE]),
      r=["pa0"], w=["fbb"])

    def emit_group(gi):
        if gi > 0:
            G(lambda g, gi=gi: g.dma_start(
                wslab.ap()[:, 8192:12288].rearrange("p (dc e) -> p dc e", dc=NDC),
                d_wv[gi]), w=["ws4", "ws5"], dma="swv")
            G(lambda g, gi=gi: g.dma_start(
                wslab.ap()[:, 0:4096].rearrange(
                    "p (h dc e) -> p h dc e", h=2, dc=NDC),
                d_wq[2 * gi:2 * gi + 2].rearrange("h p dc e -> p h dc e")),
                w=["ws0", "ws1"], dma="swq")
            G(lambda g, gi=gi: g.dma_start(
                wslab.ap()[:, 4096:8192].rearrange(
                    "p (h dc e) -> p h dc e", h=2, dc=NDC),
                d_wk[2 * gi:2 * gi + 2].rearrange("h p dc e -> p h dc e")),
                w=["ws2", "ws3"], dma="swk")

        if gi == 0:
            for dc in range(NDC):
                def vmm0(t, dc=dc):
                    last = None
                    for tc in range(NDC):
                        last = t.matmul(
                            pa[tc // 2].ap()[:, (tc % 2) * 256:
                                             (tc % 2) * 256 + 256],
                            R1_bf.ap()[:, dc * S + tc * 128:
                                       dc * S + tc * 128 + 128],
                            wslab.ap()[:, 8192 + dc * 256: 8192 + dc * 256 + 256],
                            start=(dc == 0 and tc % 2 == 0),
                            stop=(dc == NDC - 1),
                            skip_group_check=True)
                    return last
                T(vmm0, r=[f"R1_{dc}", "ws4" if dc < 8 else "ws5"],
                  w=[f"pa{b}" for b in range(8)])
            for tc in range(NDC):
                A(lambda s, tc=tc: s.copy(
                    vbuf.ap()[:, tc * 256: tc * 256 + 256],
                    pa[tc // 2].ap()[:, (tc % 2) * 256:(tc % 2) * 256 + 256]),
                    r=[f"pa{tc // 2}"], w=[f"vb{tc}"])
        else:
            for tc in range(NDC):
                bank = 2 + (tc % 2)
                pk = f"pa{bank}"

                def vmm(t, tc=tc, bank=bank):
                    last = None
                    for dc in range(NDC):
                        last = t.matmul(
                            pa[bank].ap()[:, 0:256],
                            R1_bf.ap()[:, dc * S + tc * 128: dc * S + tc * 128 + 128],
                            wslab.ap()[:, 8192 + dc * 256: 8192 + dc * 256 + 256],
                            start=(dc == 0), stop=(dc == NDC - 1))
                    return last
                T(vmm, r=R1k + ["ws4", "ws5"], w=[pk])
                A(lambda s, tc=tc, bank=bank: s.copy(
                    vbuf.ap()[:, tc * 256: tc * 256 + 256], pa[bank].ap()[:, 0:256]),
                    r=[pk], w=[f"vb{tc}"])

        for hl in range(2):
            def proj_rope(nt, wbase, wkeys, bcol, bckey, cos, sin, obuf, ooff,
                          okeys, hl=hl, hh=2 * gi + hl, finale=None):
                def emit_tail_pair(a):
                    qso = 1024 if (a // 2) % 2 == 0 else 0
                    qk = ["sb2", "sb3"] if qso else ["sb0", "sb1"]
                    for q2 in range(2):
                        T(lambda t, a=a, q2=q2, qso=qso: t.matmul(
                            pa[2 + q2].ap(), ps_sb.ap(),
                            scr_bf.ap()[:, qso + q2 * 512: qso + q2 * 512 + 512],
                            start=True, stop=True),
                            r=[qk[q2], "pswap"], w=[f"pa{2 + q2}"])
                    V(lambda v, a=a, sin=sin: v.tensor_tensor(
                        scr_f.ap()[:, 0:1024], pb_[1].ap(),
                        sin.ap()[:, a * 512: a * 512 + 1024], ALU.mult),
                        r=["pa2", "pa3", "sk"], w=["sf0", "sf1"])
                    V(lambda v, a=a, qso=qso, cos=cos: v.tensor_tensor(
                        scr_f.ap()[:, 1024:2048], scr_bf.ap()[:, qso:qso + 1024],
                        cos.ap()[:, a * 512: a * 512 + 1024], ALU.mult),
                        r=qk + ["ck"], w=["sf2", "sf3"])
                    V(lambda v, a=a: v.tensor_tensor(
                        obuf.ap()[:, ooff + a * 512: ooff + a * 512 + 1024],
                        scr_f.ap()[:, 1024:2048], scr_f.ap()[:, 0:1024],
                        ALU.add),
                        r=["sf0", "sf1", "sf2", "sf3"],
                        w=[okeys[a], okeys[a + 1]])

                def emit_evac_pair(a, pbt, pbkeys):
                    qso = 1024 if (a // 2) % 2 == 0 else 0
                    sqo = 1024 - qso
                    qk = ["sb2", "sb3"] if qso else ["sb0", "sb1"]
                    sk_ = ["sb2", "sb3"] if sqo else ["sb0", "sb1"]
                    A(lambda s, pbt=pbt, qso=qso: s.activation(
                        scr_bf.ap()[:, qso:qso + 1024], pbt.ap(), AF.Identity,
                        bias=bcol.ap()[:, hh:hh + 1]),
                        r=pbkeys + [bckey], w=qk)
                    V(lambda v, qso=qso, sqo=sqo: v.tensor_tensor(
                        scr_bf.ap()[:, sqo:sqo + 1024], scr_bf.ap()[:, qso:qso + 1024],
                        scr_bf.ap()[:, qso:qso + 1024], ALU.mult),
                        r=qk, w=sk_)
                    for q2 in range(2):
                        T(lambda t, q2=q2, sqo=sqo: t.matmul(
                            pa[4 + q2].ap(), ones_sq.ap(),
                            scr_bf.ap()[:, sqo + q2 * 512: sqo + q2 * 512 + 512],
                            start=True, stop=True),
                            r=[sk_[q2], "ones"], w=[f"pa{4 + q2}"])
                    if nt == 2:
                        mcol = 4
                    elif a == 0:
                        mcol = 5
                    else:
                        mcol = 0
                    V(lambda v, mcol=mcol: v.tensor_reduce(
                        msc.ap()[0:1, mcol:mcol + 1], pb_[2].ap()[0:1, :],
                        AX.X, ALU.max), r=["pa4", "pa5"], w=["msc"])
                    if nt == 4 and a == 2:
                        V(lambda v: v.tensor_tensor(
                            msc.ap()[0:1, 5:6], msc.ap()[0:1, 5:6],
                            msc.ap()[0:1, 0:1], ALU.max), r=["msc"], w=["msc"])

                prbanks = [(0, 1)] if nt == 2 else [(6, 7), (0, 1)]
                for pr in range(nt // 2):
                    a = 2 * pr
                    pb = prbanks[pr]
                    for dc in range(NDC):
                        def pmm(t, a=a, dc=dc, pb=pb):
                            t.matmul(
                                pa[pb[0]].ap(),
                                wslab.ap()[:, wbase + dc * 128: wbase + dc * 128 + 128],
                                R1_bf.ap()[:, dc * S + a * 512: dc * S + a * 512 + 512],
                                start=(dc == 0), stop=(dc == NDC - 1),
                                skip_group_check=True)
                            return t.matmul(
                                pa[pb[1]].ap(),
                                wslab.ap()[:, wbase + dc * 128: wbase + dc * 128 + 128],
                                R1_bf.ap()[:, dc * S + a * 512 + 512:
                                            dc * S + a * 512 + 1024],
                                start=(dc == 0), stop=(dc == NDC - 1),
                                skip_group_check=True)
                        T(pmm, r=[f"R1_{dc}"] + wkeys, w=[f"pa{pb[0]}", f"pa{pb[1]}"])
                        if dc == 2 and pr >= 1:
                            emit_tail_pair(a - 2)
                    emit_evac_pair(a, pb_[pb[0] // 2],
                                   [f"pa{pb[0]}", f"pa{pb[1]}"])
                if finale is not None:
                    finale()
                emit_tail_pair(nt - 2)

            qoff = hl * TL
            koff = hl * S
            hh = 2 * gi + hl
            proj_rope(2, hl * 2048, ["ws0" if hl == 0 else "ws1"], bqc, "bqc",
                      ck_sb, sk_sb, qbuf, qoff, [f"qb{hl}_{i}" for i in range(2)])
            def bound_finale():
                V(lambda v: v.tensor_tensor(
                    msc.ap()[0:1, 6:7], msc.ap()[0:1, 4:5], msc.ap()[0:1, 5:6],
                    ALU.mult), r=["msc"], w=["msc"])
                A(lambda s: s.activation(msc.ap()[0:1, 7:8], msc.ap()[0:1, 6:7],
                                         AF.Ln, scale=SCALE * SCALE),
                  r=["msc"], w=["msc"])
                A(lambda s: s.activation(msc.ap()[0:1, 7:8], msc.ap()[0:1, 7:8],
                                         AF.Exp, scale=0.5),
                  r=["msc"], w=["msc"])

            def bound_bcast():
                T(lambda t: t.matmul(
                    pa[6].ap()[:, 0:1], ones_r_f.ap(), msc.ap()[0:1, 7:8],
                    start=True, stop=True), r=["msc", "ones"], w=["pa6"])
                V(lambda v: v.tensor_scalar(
                    mq_sb.ap()[:, 6:7], pa[6].ap()[:, 0:1], -1.0, None,
                    ALU.mult), r=["pa6"], w=["mq"])

            proj_rope(4, 4096 + hl * 2048, ["ws2" if hl == 0 else "ws3"], bkc,
                      "bkc", ck_sb, sk_sb, kbuf, koff,
                      [f"kb{hl}_{i}" for i in range(4)], finale=bound_finale)

            def st_merged(pre=None):
                def emit_mms(kc):
                    j = kc % 2
                    for qb in range(2):
                        T(lambda t, kc=kc, j=j, qb=qb, qoff=qoff, koff=koff:
                          t.matmul(
                            pa[2 * j + qb].ap(),
                            kbuf.ap()[:, koff + kc * 128: koff + kc * 128 + 128],
                            qbuf.ap()[:, qoff + qb * 512: qoff + qb * 512 + 512],
                            start=True, stop=True),
                            r=[f"kb{hl}_{kc // 4}", f"qb{hl}_{qb}"],
                            w=[f"pa{2 * j + qb}"])

                def emit_exp(kc):
                    j = kc % 2
                    so = (1 - j) * 1024
                    A(lambda s, j=j, so=so: s.activation(
                        scr_bf.ap()[:, so:so + 1024],
                        pb[j].ap(), AF.Exp, bias=mq_sb.ap()[:, 6:7],
                        scale=SCALE),
                        r=[f"pa{2 * j}", f"pa{2 * j + 1}", "mq"],
                        w=[f"sb{2 * (1 - j)}", f"sb{2 * (1 - j) + 1}"])

                def sums_pvs(kc):
                    j = kc % 2
                    so = (1 - j) * 1024
                    for qb in range(2):
                        T(lambda t, kc=kc, so=so, j=j, qb=qb: t.matmul(
                            pa[6 + qb].ap(), ones_sq.ap(),
                            scr_bf.ap()[:, so + qb * 512: so + qb * 512 + 512],
                            start=(kc == 0), stop=(kc == NDC - 1),
                            skip_group_check=True),
                            r=[f"sb{2 * (1 - j) + qb}", "ones"], w=[f"pa{6 + qb}"])
                        T(lambda t, kc=kc, so=so, j=j, qb=qb, hl=hl: t.matmul(
                            pa[4 + qb].ap(),
                            vbuf.ap()[:, kc * 256 + hl * 128:
                                      kc * 256 + hl * 128 + 128],
                            scr_bf.ap()[:, so + qb * 512: so + qb * 512 + 512],
                            start=(kc == 0), stop=(kc == NDC - 1),
                            skip_group_check=True),
                            r=[f"sb{2 * (1 - j) + qb}", f"vb{kc}"],
                            w=[f"pa{4 + qb}"])

                emit_mms(0)
                emit_mms(1)
                if pre is not None:
                    pre()
                emit_exp(0)
                for kc in range(2, NDC):
                    emit_mms(kc)
                    emit_exp(kc - 1)
                    if kc >= 2:
                        sums_pvs(kc - 2)
                emit_exp(NDC - 1)
                sums_pvs(NDC - 2)
                sums_pvs(NDC - 1)

            def epilogue_m():
                for qb in range(2):
                    A(lambda s, qb=qb: s.activation(
                        scr_f.ap()[:, qb * 512:(qb + 1) * 512],
                        pa[6 + qb].ap(), AF.Ln),
                        r=[f"pa{6 + qb}"], w=[f"sf{qb}"])
                    A(lambda s, qb=qb: s.activation(
                        scr_f.ap()[:, qb * 512:(qb + 1) * 512],
                        scr_f.ap()[:, qb * 512:(qb + 1) * 512], AF.Exp,
                        scale=-1.0), r=[f"sf{qb}"], w=[f"sf{qb}"])
                    V(lambda v, qb=qb, hh=hh: v.tensor_tensor(
                        aog.ap()[:, hh * TL + qb * 512: hh * TL + qb * 512 + 512],
                        pa[4 + qb].ap(), scr_f.ap()[:, qb * 512:(qb + 1) * 512],
                        ALU.mult), r=[f"pa{4 + qb}", f"sf{qb}"], w=[f"ao{hh}"])
                    A(lambda s, hh=hh, qb=qb: s.activation(
                        aog.ap()[:, hh * TL + qb * 512: hh * TL + qb * 512 + 512],
                        aog.ap()[:, hh * TL + qb * 512: hh * TL + qb * 512 + 512],
                        AF.Identity, bias=bvc.ap()[:, hh:hh + 1]),
                        r=[f"ao{hh}", "bvc"], w=[f"ao{hh}"])

            st_merged(pre=bound_bcast)
            epilogue_m()

    for gi in range(NH // 2):
        emit_group(gi)
        if debug and gi == 0:
            Y(lambda s: s.dma_start(d_dbg_qkv[0][:, 0:2 * TL], qbuf.ap()),
              r=["qb0_0", "qb0_1", "qb1_0", "qb1_1"], dma="dbg5")
            Y(lambda s: s.dma_start(d_dbg_qkv[1], kbuf.ap()),
              r=[f"kb{h}_{i}" for h in range(2) for i in range(4)], dma="dbg5")
            Y(lambda s: s.dma_start(d_dbg_qkv[2], vbuf.ap()),
              r=[f"vb{t}" for t in range(NDC)], dma="dbg5")

    for i in range(NDC):
        V(lambda v, i=i: v.tensor_copy(
            xy.ap()[:, i * TL:(i + 1) * TL], R1_bf.ap()[:, i * S: i * S + TL]),
            r=[f"R1_{i}"], w=[f"xy{i}"])

    if debug:
        Y(lambda s: s.dma_start(d_dbg_ao, aog.ap()),
          r=[f"ao{h}" for h in range(NH)], dma="dbg6")
    pending_stats = [None]
    for ec in range(NHC):
        sl = ec % 2
        wok = f"ws{sl}"
        G(lambda g, ec=ec, sl=sl: g.dma_start(
            wslab.ap()[:, sl * 2048:(sl + 1) * 2048].rearrange(
                "p (dc e) -> p dc e", dc=NDC), d_wo[ec]),
            w=[wok], dma=f"swo{sl}")
        ob = 0 if ec % 2 == 0 else 2
        for dc in range(NDC):
            def omm(t, ec=ec, dc=dc, sl=sl, ob=ob):
                for nq in range(2):
                    last = t.matmul(
                        pa[ob + nq].ap(),
                        wslab.ap()[:, sl * 2048 + dc * 128:
                                   sl * 2048 + dc * 128 + 128],
                        aog.ap()[:, dc * TL + nq * 512: dc * TL + nq * 512 + 512],
                        start=(dc == 0), stop=(dc == NDC - 1),
                        skip_group_check=True)
                return last
            T(omm, r=[f"ao{dc}", wok], w=[f"pa{ob}", f"pa{ob + 1}"])
        if pending_stats[0] is not None:
            pending_stats[0]()
        for nq in range(2):
            V(lambda v, ec=ec, nq=nq, ob=ob: v.scalar_tensor_tensor(
                R1_f32.ap()[:, ec * TL + nq * 512: ec * TL + nq * 512 + 512],
                pa[ob + nq].ap(), boc.ap()[:, ec:ec + 1],
                xy.ap()[:, ec * TL + nq * 512: ec * TL + nq * 512 + 512],
                ALU.add, ALU.add),
                r=[f"pa{ob + nq}", f"xy{ec}", "boc"], w=[f"R1_{ec}"])

        def emit_stats(ec=ec):
            sqs = ec % 2
            A(lambda s, ec=ec, sqs=sqs: s.activation(
                vbuf.ap()[:, sqs * 1024:(sqs + 1) * 1024],
                R1_f32.ap()[:, ec * TL:(ec + 1) * TL], AF.Square),
                r=[f"R1_{ec}"], w=[f"vb{t}" for t in range(4 * sqs, 4 * sqs + 4)])
            A(lambda s, ec=ec, sqs=sqs: s.activation(
                scr_bf.ap()[:, sqs * 1024:(sqs + 1) * 1024],
                R1_f32.ap()[:, ec * TL:(ec + 1) * TL], AF.Identity),
                r=[f"R1_{ec}"], w=[f"sb{2 * sqs}", f"sb{2 * sqs + 1}"])
            for nq in range(2):
                T(lambda t, ec=ec, nq=nq, sqs=sqs: t.matmul(
                    pa[4 + nq].ap(), ones_sq.ap(),
                    scr_bf.ap()[:, sqs * 1024 + nq * 512:
                                sqs * 1024 + nq * 512 + 512],
                    start=(ec == 0), stop=(ec == NHC - 1),
                    skip_group_check=True),
                    r=[f"sb{2 * sqs + nq}", "ones"], w=[f"pa{4 + nq}"])
                T(lambda t, ec=ec, nq=nq, sqs=sqs: t.matmul(
                    pa[6 + nq].ap(), ones_sq.ap(),
                    vbuf.ap()[:, sqs * 1024 + nq * 512:
                              sqs * 1024 + nq * 512 + 512],
                    start=(ec == 0), stop=(ec == NHC - 1),
                    skip_group_check=True),
                    r=[f"vb{4 * sqs + nq * 2}", f"vb{4 * sqs + nq * 2 + 1}",
                       "ones"], w=[f"pa{6 + nq}"])
        pending_stats[0] = emit_stats
    pending_stats[0]()

    def emit_ln(src_f32, srckeys, dst, dstkeys, inplace, mpb, vpb, mkeys, vkeys,
                bm=4, bv=6):
        V(lambda v: v.tensor_scalar(
            kbuf.ap()[0:1, 0:1024], mpb.ap()[0:1, :], 1.0 / D, None,
            ALU.mult), r=mkeys, w=["kb0_0", "kb0_1"])
        V(lambda v: v.tensor_tensor(
            kbuf.ap()[0:1, 1024:2048], kbuf.ap()[0:1, 0:1024],
            kbuf.ap()[0:1, 0:1024], ALU.mult),
            r=["kb0_0", "kb0_1"], w=["kb0_2", "kb0_3"])
        V(lambda v: v.tensor_scalar(
            scr_f.ap()[0:1, 1024:2048], vpb.ap()[0:1, :], 1.0 / D, EPS,
            ALU.mult, ALU.add), r=vkeys, w=["sf2", "sf3"])
        V(lambda v: v.tensor_tensor(
            scr_f.ap()[0:1, 1024:2048], scr_f.ap()[0:1, 1024:2048],
            kbuf.ap()[0:1, 1024:2048], ALU.subtract),
            r=["kb0_2", "kb0_3", "sf2", "sf3"], w=["sf2", "sf3"])
        A(lambda s: s.activation(
            scr_f.ap()[0:1, 1024:2048], scr_f.ap()[0:1, 1024:2048], AF.Ln),
            r=["sf2", "sf3"], w=["sf2", "sf3"])
        A(lambda s: s.activation(
            scr_f.ap()[0:1, 1024:2048], scr_f.ap()[0:1, 1024:2048], AF.Exp,
            scale=-0.5), r=["sf2", "sf3"], w=["sf2", "sf3"])
        for nq in range(2):
            T(lambda t, nq=nq: t.matmul(
                pa[bm + nq].ap(), ones_r_bf.ap()[0:1, 0:128],
                kbuf.ap()[0:1, nq * 512: nq * 512 + 512],
                start=True, stop=True),
                r=[f"kb0_{nq}", "ones"], w=[f"pa{bm + nq}"])
            T(lambda t, nq=nq: t.matmul(
                pa[bv + nq].ap(), ones_r_f.ap()[0:1, 0:128],
                scr_f.ap()[0:1, 1024 + nq * 512: 1024 + nq * 512 + 512],
                start=True, stop=True),
                r=[f"sf{2 + nq}", "ones"], w=[f"pa{bv + nq}"])
        for ec in range(NHC):
            vsl = ec % 2
            vk = [f"vb{t}" for t in range(4 * vsl, 4 * vsl + 4)]
            V(lambda v, ec=ec, vsl=vsl: v.tensor_tensor(
                vbuf.ap()[:, vsl * 1024:(vsl + 1) * 1024],
                src_f32.ap()[:, ec * TL:(ec + 1) * TL],
                pb[bm // 2].ap(), ALU.subtract),
                r=[srckeys[ec], f"pa{bm}", f"pa{bm + 1}"], w=vk)
            V(lambda v, ec=ec, vsl=vsl: v.tensor_tensor(
                dst.ap()[:, ec * TL:(ec + 1) * TL],
                vbuf.ap()[:, vsl * 1024:(vsl + 1) * 1024],
                pb[bv // 2].ap(), ALU.mult),
                r=vk + [f"pa{bv}", f"pa{bv + 1}"], w=[dstkeys[ec]])
            if inplace:
                Y(lambda s, ec=ec: s.dma_start(
                    d_y[ec], R1_f32.ap()[:, ec * TL:(ec + 1) * TL]),
                    r=[f"R1_{ec}"], dma="st")

    emit_ln(R1_f32, R1k, xy, XYk, False, pb[2], pb[3],
            ["pa4", "pa5"], ["pa6", "pa7"])

    for dc in range(NDC):
        def lmm(t, dc=dc):
            for nq in range(2):
                last = t.matmul(
                    pa[nq].ap()[0:8, :], wg_sb.ap()[:, dc * NE:(dc + 1) * NE],
                    xy.ap()[:, dc * TL + nq * 512: dc * TL + nq * 512 + 512],
                    start=(dc == 0), stop=(dc == NDC - 1),
                    skip_group_check=True)
            return last
        T(lmm, r=[f"xy{dc}", "wg"], w=["pa0", "pa1"])
    for nq in range(2):
        V(lambda v, nq=nq: v.tensor_copy(
            scr_f.ap()[0:8, nq * 512:(nq + 1) * 512], pa[nq].ap()[0:8, :]),
            r=[f"pa{nq}"], w=["sf0", "sf1"])
    for tt in range(8):
        T(lambda t, tt=tt: t.transpose(
            pa[2].ap()[:, 0:8], scr_f.ap()[0:8, tt * 128:(tt + 1) * 128],
            id_sb.ap()[0:8, 0:8]), r=["sf0", "sf1", "id"], w=["pa2"])
        V(lambda v, tt=tt: v.tensor_copy(
            rt_sb.ap()[:, tt * 8:(tt + 1) * 8], pa[2].ap()[:, 0:8]),
            r=["pa2"], w=["rt"])
    A(lambda s: s.activation(wr_sb.ap(), rt_sb.ap(), AF.Sigmoid),
      r=["rt"], w=["wr"])
    for tt in range(8):
        V(lambda v, tt=tt: v.tensor_tensor(
            sc_sb.ap()[:, tt * 8:(tt + 1) * 8], wr_sb.ap()[:, tt * 8:(tt + 1) * 8],
            fbb.ap(), ALU.add), r=["wr", "fbb"], w=["sc"])
    for tt in range(8):
        sl8 = slice(tt * 8, (tt + 1) * 8)
        V(lambda v, tt=tt: v.tensor_reduce(
            m1_sb.ap()[:, tt:tt + 1], sc_sb.ap()[:, tt * 8:(tt + 1) * 8], AX.X,
            ALU.max), r=["sc"], w=["m1"])
        V(lambda v, tt=tt: v.tensor_scalar(
            s2_sb.ap()[:, tt * 8:(tt + 1) * 8], sc_sb.ap()[:, tt * 8:(tt + 1) * 8],
            m1_sb.ap()[:, tt:tt + 1], None, ALU.is_ge), r=["sc", "m1"], w=["s2"])
        V(lambda v, tt=tt: v.tensor_scalar(
            s2_sb.ap()[:, tt * 8:(tt + 1) * 8], s2_sb.ap()[:, tt * 8:(tt + 1) * 8],
            -BIG, None, ALU.mult), r=["s2"], w=["s2"])
        V(lambda v, tt=tt: v.tensor_tensor(
            s2_sb.ap()[:, tt * 8:(tt + 1) * 8], sc_sb.ap()[:, tt * 8:(tt + 1) * 8],
            s2_sb.ap()[:, tt * 8:(tt + 1) * 8], ALU.add), r=["sc", "s2"], w=["s2"])
        V(lambda v, tt=tt: v.tensor_reduce(
            m1_sb.ap()[:, tt:tt + 1], s2_sb.ap()[:, tt * 8:(tt + 1) * 8], AX.X,
            ALU.max), r=["s2"], w=["m1"])
        V(lambda v, tt=tt: v.tensor_scalar(
            s2_sb.ap()[:, tt * 8:(tt + 1) * 8], sc_sb.ap()[:, tt * 8:(tt + 1) * 8],
            m1_sb.ap()[:, tt:tt + 1], None, ALU.is_ge), r=["sc", "m1"], w=["s2"])
        V(lambda v, tt=tt: v.tensor_tensor(
            ew_sb.ap()[:, tt * 8:(tt + 1) * 8], wr_sb.ap()[:, tt * 8:(tt + 1) * 8],
            s2_sb.ap()[:, tt * 8:(tt + 1) * 8], ALU.mult), r=["wr", "s2"], w=["ew"])
        V(lambda v, tt=tt: v.tensor_reduce(
            m1_sb.ap()[:, tt:tt + 1], ew_sb.ap()[:, tt * 8:(tt + 1) * 8], AX.X,
            ALU.add), r=["ew"], w=["m1"])
    V(lambda v: v.reciprocal(m1_sb.ap()[:, 0:8], m1_sb.ap()[:, 0:8]),
      r=["m1"], w=["m1"])
    for tt in range(8):
        V(lambda v, tt=tt: v.tensor_scalar(
            ew_sb.ap()[:, tt * 8:(tt + 1) * 8], ew_sb.ap()[:, tt * 8:(tt + 1) * 8],
            m1_sb.ap()[:, tt:tt + 1], None, ALU.mult), r=["ew", "m1"], w=["ew"])

    for hc in range(NHC):
        V(lambda v, hc=hc: v.tensor_copy(
            R1_f32.ap()[:, hc * TL:(hc + 1) * TL],
            xy.ap()[:, hc * TL:(hc + 1) * TL]), r=[f"xy{hc}"], w=[f"R1_{hc}"])

    W1OFF = [0, 2048]
    W3OFF = [4096, 6144]
    W2OFF = [8192, 9216]
    W1KEY = ["ws0", "ws1"]
    W3KEY = ["ws2", "ws3"]
    W2KEY = ["ws4a", "ws4b"]

    w13_q = [(e, fc) for e in range(NE) for fc in range(NFC)]
    w2_q = [(e, hc) for e in range(NE) for hc in range(NHC)]
    w13_pos = [0]
    w2_pos = [0]

    def pump_w13(n=1):
        for _ in range(n):
            if w13_pos[0] >= len(w13_q):
                return
            e, fc = w13_q[w13_pos[0]]
            w13_pos[0] += 1
            s1 = fc % 2
            G(lambda g, e=e, fc=fc, s1=s1: g.dma_start(
                wslab.ap()[:, W1OFF[s1]:W1OFF[s1] + 2048].rearrange(
                    "p (dc f) -> p dc f", dc=NDC), d_w1[e, fc]),
                w=[W1KEY[s1]], dma=f"sw1_{s1}")
            G(lambda g, e=e, fc=fc, s1=s1: g.dma_start(
                wslab.ap()[:, W3OFF[s1]:W3OFF[s1] + 2048].rearrange(
                    "p (dc f) -> p dc f", dc=NDC), d_w3[e, fc]),
                w=[W3KEY[s1]], dma=f"sw3_{s1}")

    def pump_w2(n=1):
        for _ in range(n):
            if w2_pos[0] >= len(w2_q):
                return
            e, hc = w2_q[w2_pos[0]]
            w2_pos[0] += 1
            s2 = hc % 2
            G(lambda g, e=e, hc=hc, s2=s2: g.dma_start(
                wslab.ap()[:, W2OFF[s2]:W2OFF[s2] + 1024].rearrange(
                    "p (fc h) -> p fc h", fc=NFC), d_w2[e, hc]),
                w=[W2KEY[s2]], dma=f"sw2_{s2}")

    def emit_ew_rows(e):
        sl = e % 2
        for tt in range(8):
            bank = 2 + tt % 2
            T(lambda t, e=e, tt=tt, bank=bank: t.transpose(
                pa[bank].ap()[0:1, 0:128],
                ew_sb.ap()[:, tt * 8 + e: tt * 8 + e + 1], id_sb.ap()),
                r=["ew", "id"], w=[f"pa{bank}"])
            V(lambda v, tt=tt, bank=bank, sl=sl: v.tensor_copy(
                kbuf.ap()[0:1, sl * 1024 + tt * 128: sl * 1024 + tt * 128 + 128],
                pa[bank].ap()[0:1, 0:128]),
                r=[f"pa{bank}"], w=[f"kb0_{2 * sl + tt // 4}"])
        for nq in range(2):
            bank = 2 + nq
            T(lambda t, nq=nq, sl=sl, bank=bank: t.matmul(
                pa[bank].ap(), ones_r_bf.ap()[0:1, 0:128],
                kbuf.ap()[0:1, sl * 1024 + nq * 512: sl * 1024 + nq * 512 + 512],
                start=True, stop=True),
                r=[f"kb0_{2 * sl + nq}", "ones"], w=[f"pa{bank}"])
            V(lambda v, nq=nq, sl=sl, bank=bank: v.tensor_copy(
                scr_f.ap()[:, sl * 1024 + nq * 512: sl * 1024 + nq * 512 + 512],
                pa[bank].ap()), r=[f"pa{bank}"], w=[f"sf{2 * sl + nq}"])

    def emit_h13(e, hook=None):
        sl = e % 2
        for fc in range(NFC):
            s1 = fc % 2
            for dc in range(NDC):
                def h1mm(t, dc=dc, s1=s1):
                    for nc2 in range(2):
                        last = t.matmul(
                            pa[nc2].ap(),
                            wslab.ap()[:, W1OFF[s1] + dc * 128:
                                       W1OFF[s1] + dc * 128 + 128],
                            xy.ap()[:, dc * TL + nc2 * 512:
                                    dc * TL + nc2 * 512 + 512],
                            start=(dc == 0), stop=(dc == NDC - 1),
                            skip_group_check=True)
                    return last
                T(h1mm, r=[f"xy{dc}", W1KEY[s1]], w=["pa0", "pa1"])
            for nc2 in range(2):
                A(lambda s, nc2=nc2: s.activation(
                    scr_bf.ap()[:, nc2 * 512:(nc2 + 1) * 512], pa[nc2].ap(),
                    AF.Silu), r=[f"pa{nc2}"], w=[f"sb{nc2}"])
            for dc in range(NDC):
                def h3mm(t, dc=dc, s1=s1):
                    for nc2 in range(2):
                        last = t.matmul(
                            pa[4 + nc2].ap(),
                            wslab.ap()[:, W3OFF[s1] + dc * 128:
                                       W3OFF[s1] + dc * 128 + 128],
                            xy.ap()[:, dc * TL + nc2 * 512:
                                    dc * TL + nc2 * 512 + 512],
                            start=(dc == 0), stop=(dc == NDC - 1),
                            skip_group_check=True)
                    return last
                T(h3mm, r=[f"xy{dc}", W3KEY[s1]], w=["pa4", "pa5"])
                if hook is not None and fc == 0 and dc == NDC - 1:
                    hook()
                    hook = None
            for nc2 in range(2):
                bsl = 2 + nc2
                V(lambda v, bsl=bsl, sl=sl, nc2=nc2: v.tensor_tensor(
                    scr_bf.ap()[:, bsl * 512:(bsl + 1) * 512], pa[4 + nc2].ap(),
                    scr_f.ap()[:, sl * 1024 + nc2 * 512: sl * 1024 + nc2 * 512 + 512],
                    ALU.mult), r=[f"pa{4 + nc2}", f"sf{2 * sl + nc2}"], w=[f"sb{bsl}"])
                V(lambda v, fc=fc, nc2=nc2, bsl=bsl: v.tensor_tensor(
                    aog.ap()[:, fc * TL + nc2 * 512: fc * TL + nc2 * 512 + 512],
                    scr_bf.ap()[:, nc2 * 512:(nc2 + 1) * 512],
                    scr_bf.ap()[:, bsl * 512:(bsl + 1) * 512], ALU.mult),
                    r=[f"sb{nc2}", f"sb{bsl}"], w=[f"ao{fc}"])
            pump_w13(1)

    def emit_ln2_stats(ec):
        sqs = ec % 2
        A(lambda s, ec=ec, sqs=sqs: s.activation(
            vbuf.ap()[:, sqs * 1024:(sqs + 1) * 1024],
            R1_f32.ap()[:, ec * TL:(ec + 1) * TL], AF.Square),
            r=[f"R1_{ec}"], w=[f"vb{t}" for t in range(4 * sqs, 4 * sqs + 4)])
        A(lambda s, ec=ec, sqs=sqs: s.activation(
            scr_bf.ap()[:, sqs * 1024:(sqs + 1) * 1024],
            R1_f32.ap()[:, ec * TL:(ec + 1) * TL], AF.Identity),
            r=[f"R1_{ec}"], w=[f"sb{2 * sqs}", f"sb{2 * sqs + 1}"])
        for nq in range(2):
            T(lambda t, ec=ec, nq=nq, sqs=sqs: t.matmul(
                pa[0 + nq].ap(), ones_sq.ap(),
                scr_bf.ap()[:, sqs * 1024 + nq * 512: sqs * 1024 + nq * 512 + 512],
                start=(ec == 0), stop=(ec == NHC - 1), skip_group_check=True),
                r=[f"sb{2 * sqs + nq}", "ones"], w=[f"pa{0 + nq}"])
            T(lambda t, ec=ec, nq=nq, sqs=sqs: t.matmul(
                pa[4 + nq].ap(), ones_sq.ap(),
                vbuf.ap()[:, sqs * 1024 + nq * 512: sqs * 1024 + nq * 512 + 512],
                start=(ec == 0), stop=(ec == NHC - 1), skip_group_check=True),
                r=[f"vb{4 * sqs + nq * 2}", f"vb{4 * sqs + nq * 2 + 1}", "ones"],
                w=[f"pa{4 + nq}"])

    def emit_w2(e, stats=False):
        pend = [None]
        for hc in range(NHC):
            s2 = hc % 2
            wb = 6 if hc % 2 == 0 else 2
            for fc in range(NFC):
                def w2mm(t, hc=hc, fc=fc, s2=s2, wb=wb):
                    for nc2 in range(2):
                        last = t.matmul(
                            pa[wb + nc2].ap(),
                            wslab.ap()[:, W2OFF[s2] + fc * 128:
                                       W2OFF[s2] + fc * 128 + 128],
                            aog.ap()[:, fc * TL + nc2 * 512: fc * TL + nc2 * 512 + 512],
                            start=(fc == 0), stop=(fc == NFC - 1),
                            skip_group_check=True)
                    return last
                T(w2mm, r=[f"ao{fc}", W2KEY[s2]], w=[f"pa{wb}", f"pa{wb + 1}"])
            if pend[0] is not None:
                pend[0]()
                pend[0] = None
            for nc2 in range(2):
                V(lambda v, hc=hc, nc2=nc2, wb=wb: v.tensor_tensor(
                    R1_f32.ap()[:, hc * TL + nc2 * 512: hc * TL + nc2 * 512 + 512],
                    pa[wb + nc2].ap(),
                    R1_f32.ap()[:, hc * TL + nc2 * 512: hc * TL + nc2 * 512 + 512],
                    ALU.add), r=[f"pa{wb + nc2}", f"R1_{hc}"], w=[f"R1_{hc}"])
            if stats:
                pend[0] = lambda hc=hc: emit_ln2_stats(hc)
            pump_w2(1)
        if pend[0] is not None:
            pend[0]()

    pump_w13(2)
    pump_w2(2)
    for e in range(NE):
        emit_h13(e, hook=lambda e=e: emit_ew_rows(e))
        emit_w2(e, stats=(e == NE - 1))
        if debug:
            Y(lambda s, e=e: s.dma_start(d_dbg_acc[e], R1_f32.ap()),
              r=[f"R1_{i}" for i in range(NHC)], dma="dbg2")
            Y(lambda s, e=e: s.dma_start(d_dbg_g[e], aog.ap()),
              r=[f"ao{j}" for j in range(NH)], dma="dbg3")
            Y(lambda s, e=e, sl=e % 2: s.dma_start(
                d_dbg_ewb[e], scr_f.ap()[:, sl * TL:(sl + 1) * TL]),
                r=[f"sf{2 * (e % 2)}", f"sf{2 * (e % 2) + 1}"], dma=f"dbg4{e % 2}")

    emit_ln(R1_f32, R1k, R1_f32, R1k, True, pb[0], pb[2],
            ["pa0", "pa1"], ["pa4", "pa5"], bm=6, bv=2)

    build_program.last_sched = S_
    handles = S_.materialize(nc)
    st_total = S_.totals.get(("dma", "st"), 0)
    import contextlib
    with nc.allow_low_precision(
            reason="bf16 intermediates by design; fp32 accumulation in PSUM, "
                   "graded against the fp64 sim"), nc.Block() as block:
        @block.tensor
        def _(t):
            S_.replay("tensor", t)

        @block.vector
        def _(v):
            S_.replay("vector", v)

        @block.scalar
        def _(s):
            S_.replay("scalar", s)

        @block.gpsimd
        def _(g):
            S_.replay("gpsimd", g)

        @block.sync
        def _(s):
            S_.replay("sync", s)
            s.wait_ge(handles[("dma", "st")], st_total)
    return nc


_NC_CACHE = {}


def _get_nc():
    if "nc" not in _NC_CACHE:
        _NC_CACHE["nc"] = build_program()
    return _NC_CACHE["nc"]


def _host_inputs(inputs):
    f = np.float32
    x = np.asarray(inputs["L0_attn_x"], f)
    Wq = np.asarray(inputs["L0_attn_Wq"], f)
    Wk = np.asarray(inputs["L0_attn_Wk"], f)
    Wv = np.asarray(inputs["L0_attn_Wv"], f)
    Wo = np.asarray(inputs["L0_attn_Wo"], f)
    Wg = np.asarray(inputs["L0_ffn_Wg"], f)
    W1 = np.asarray(inputs["L0_ffn_W1"], f)
    W2 = np.asarray(inputs["L0_ffn_W2"], f)
    W3 = np.asarray(inputs["L0_ffn_W3"], f)

    def att_t(W):
        return np.ascontiguousarray(
            W.reshape(NH, 128, NDC, 128).transpose(0, 3, 2, 1))

    shared = {
        "wq_t": att_t(Wq), "wk_t": att_t(Wk), "wo_t": att_t(Wo),
        "wv_t": np.ascontiguousarray(
            Wv.reshape(NH // 2, 256, NDC, 128).transpose(0, 3, 2, 1)),
        "w1_t": np.ascontiguousarray(
            W1.reshape(NE, NFC, 128, NDC, 128).transpose(0, 1, 4, 3, 2)),
        "w3_t": np.ascontiguousarray(
            W3.reshape(NE, NFC, 128, NDC, 128).transpose(0, 1, 4, 3, 2)),
        "w2_t": np.ascontiguousarray(
            W2.reshape(NE, NHC, 128, NFC, 128).transpose(0, 1, 4, 3, 2)),
        "wg_t": np.ascontiguousarray(Wg.T.reshape(NDC, 128, NE).transpose(1, 0, 2)),
        "bq": np.ascontiguousarray(np.asarray(inputs["L0_attn_bq"], f).reshape(NH, 128).T),
        "bk": np.ascontiguousarray(np.asarray(inputs["L0_attn_bk"], f).reshape(NH, 128).T),
        "bv": np.ascontiguousarray(np.asarray(inputs["L0_attn_bv"], f).reshape(NH, 128).T),
        "bo": np.ascontiguousarray(np.asarray(inputs["L0_attn_bo"], f).reshape(NHC, 128).T),
        "fbias": np.asarray(inputs["L0_ffn_bias"], f).reshape(1, NE),
    }
    pswap = np.zeros((128, 128), f)
    pswap[np.arange(128), np.arange(128) ^ 1] = 1.0
    shared["pswap"] = pswap
    shared["ident"] = np.eye(128, dtype=f)

    f32 = np.float32
    freqs = np.array(_ROPE_FREQ_BITS, dtype=np.uint32).view(f32)
    pos = np.arange(S, dtype=f32)
    ang = pos[:, None] * freqs[None, :]
    cosT = np.cos(ang)
    sinT = np.sin(ang)
    cos = np.repeat(cosT.T, 2, axis=0)
    sin = sinT.T
    ssgn = np.empty((128, S), f32)
    ssgn[0::2] = -sin
    ssgn[1::2] = sin

    in_maps = []
    for c in range(8):
        b = c // 2
        half = c % 2
        own = np.arange(half * TL, (half + 1) * TL)
        other = np.arange((1 - half) * TL, (2 - half) * TL)
        perm = np.concatenate([own, other])
        m = dict(shared)
        m["x_t"] = np.ascontiguousarray(x[b].T[:, perm]).reshape(NDC, 128, S)
        m["rope_ck"] = np.ascontiguousarray(cos[:, perm]).astype(f)
        m["rope_sk"] = np.ascontiguousarray(ssgn[:, perm]).astype(f)
        in_maps.append(m)
    return in_maps


def kernel(**inputs):
    in_maps = _host_inputs(inputs)
    nc = _get_nc()
    res = run_bass_kernel_spmd(nc, in_maps, core_ids=list(range(8)))
    y = np.empty((4, S, D), np.float32)
    for c in range(8):
        o = np.asarray(res.results[c]["y"])
        y[c // 2, (c % 2) * TL:(c % 2 + 1) * TL, :] = \
            o.transpose(2, 0, 1).reshape(TL, D)
    return y
